# version 24
# baseline (speedup 1.0000x reference)
import numpy as np
from contextlib import ExitStack

import concourse.bass as bass
import concourse.mybir as mybir

F32 = mybir.dt.float32
BF16 = mybir.dt.bfloat16
AF = mybir.ActivationFunctionType
ALU = mybir.AluOpType
AXX = mybir.AxisListType.X

MAGIC = 12582912.0
HID = 2048
PROJ = 5136
NH = 16
Q = 256
T = 1024
NPRE = 3072
EPS = 1e-5
FFH = 8192


class Builder:
    def __init__(self, debug=False, upto="all"):
        self.debug = debug
        self.upto = upto
        self.nc = bass.Bass("TRN2", target_bir_lowering=False)
        self.es = ExitStack()
        self.dbg = {}
        self._semn = 0

    def din(self, name, shape, dt=F32):
        return self.nc.dram_tensor(name, shape, dt, kind="ExternalInput").ap()

    def dint(self, name, shape, dt=F32):
        return self.nc.dram_tensor(name, shape, dt, kind="Internal").ap()

    def dout(self, name, shape, dt=F32):
        ap = self.nc.dram_tensor(name, shape, dt, kind="ExternalOutput").ap()
        return ap

    def ddbg(self, name, shape, dt=F32):
        if self.debug:
            self.dbg[name] = self.nc.dram_tensor(
                name, shape, dt, kind="ExternalOutput").ap()
            return self.dbg[name]
        return None

    def sb(self, name, shape, dt=F32, scope=None, side=None):
        if side is None:
            side = "left" if scope is None else "right"
        cm = self.nc.sbuf_tensor(name, shape, dt, side=side)
        return (scope or self.es).enter_context(cm)

    def sem(self, name=None):
        self._semn += 1
        return Sem(self, name or f"s{self._semn}")

    def defer(self, queue, which, emit, after=(), sem=None, k=1):
        lz = Lazy(sem) if sem is not None else None
        queue.append((which, emit, list(after), sem, k, lz))
        return lz

    def flush_zip(self, queues):
        queues = [list(q) for q in queues]
        idx = [0] * len(queues)

        def ready(a):
            if a is None:
                return True
            if isinstance(a, Lazy):
                return a.val is not None
            return True

        def conv(a):
            if isinstance(a, Lazy):
                return (a.sem, a.val)
            return a

        while any(i < len(q) for i, q in zip(idx, queues)):
            emitted = False
            for qi, q in enumerate(queues):
                if idx[qi] >= len(q):
                    continue
                which, emit, after, sem, k, lz = q[idx[qi]]
                if not all(ready(a) for a in after):
                    continue
                m = self.step(which, emit,
                              after=[conv(a) for a in after if a is not None],
                              sem=sem, k=k)
                if lz is not None:
                    lz.val = m[1]
                idx[qi] += 1
                emitted = True
            if not emitted:
                raise RuntimeError("zipper deadlock")

    def step(self, which, emit, after=(), sem=None, k=1, dbgname=None):
        blk = self.blk
        dec = getattr(blk, which)
        out = {}

        def _(eng):
            for ms in after:
                if ms is not None:
                    eng.wait_ge(ms[0].h, ms[1])
            out["last"] = emit(eng)
        dec(_)
        if sem is not None:
            assert out["last"] is not None, dbgname
            return sem.inc(out["last"], k)
        return None


class Lazy:
    __slots__ = ("sem", "val")

    def __init__(self, sem):
        self.sem = sem
        self.val = None


class Sem:
    def __init__(self, b, name):
        self.h = b.es.enter_context(b.nc.semaphore(name))
        self.n = 0

    def inc(self, instr, k=1):
        instr.then_inc(self.h, k)
        self.n += k
        return (self, self.n)

    def mark(self):
        return (self, self.n)


def build(debug=False, upto="all"):
    b = Builder(debug=debug, upto=upto)
    nc, es = b.nc, b.es
    PHASES = ["ph0", "pre", "ph1", "ph2", "ph3", "ph4"]
    upto_i = len(PHASES) - 1 if upto == "all" else PHASES.index(upto)

    def enabled(ph):
        return PHASES.index(ph) <= upto_i

    xe = b.din("xe", [3 + T, HID])
    xp = b.din("xp", [NPRE, HID])
    w_in = b.din("w_in", [HID, PROJ])
    wmod = b.din("wmod", [HID, 12288])
    c_pk = b.din("c_pk", [128, 16])
    convw = b.din("convw", [128, 96])
    convb = b.din("convb", [128, 24])
    acol = b.din("acol", [16, 1])
    dtbcol = b.din("dtbcol", [16, 1])
    drow = b.din("drow", [1, 16])
    nwrow = b.din("nwrow", [1, HID])
    w_out = b.din("w_out", [HID, HID])
    gwB = b.din("gwB", [64, 128, 2048])
    vwB = b.din("vwB", [64, 128, 2048])
    owB = b.din("owB", [16, 4, 128, 2048])
    gb2 = b.din("gb2", [128, 64])
    vb2 = b.din("vb2", [128, 64])
    obrow = b.din("obrow", [1, HID])
    identb = b.din("identb", [128, 128], BF16)
    identf = b.din("identf", [128, 128])
    masklog = b.din("masklog", [128, 512])
    onesel = b.din("onesel", [16, 2048])
    oco = b.din("oco", [128, 256])
    onesrow_b = b.din("onesrow_b", [1, 128], BF16)
    onesrow_f = b.din("onesrow_f", [1, 128])
    onescol_f = b.din("onescol_f", [128, 1])
    halom = b.din("halom", [128, 1])
    premrow = b.din("premrow", [1, 12])
    premB = b.din("premB", [128, 12])
    epscol = b.din("epscol", [128, 1])

    xo = b.dout("xo", [T, HID], F32)
    modsc = b.dint("modsc", [1, 12288 + 2 * HID])
    xsc = b.dint("xsc", [T, HID])

    sb = b.sb
    t_identb = sb("t_identb", [128, 128], BF16)
    t_identf = sb("t_identf", [128, 128])
    t_ones_b = sb("t_ones_b", [1, 128], BF16)
    t_ones_f = sb("t_ones_f", [1, 128])
    t_onesc = sb("t_onesc", [128, 1])
    t_halom = sb("t_halom", [128, 1])
    t_prem = sb("t_prem", [1, 12])
    t_premB = sb("t_premB", [128, 12])
    t_eps = sb("t_eps", [128, 1])
    t_convw = sb("t_convw", [128, 96])
    t_convb = sb("t_convb", [128, 24])
    t_acol = sb("t_acol", [16, 1])
    t_dtbcol = sb("t_dtbcol", [16, 1])
    t_drow = sb("t_drow", [1, 16])
    t_gb2 = sb("t_gb2", [128, 64])
    t_vb2 = sb("t_vb2", [128, 64])
    t_db = sb("t_db", [128, 16])
    t_cpk = sb("t_cpk", [128, 16])
    t_silc = sb("t_silc", [128, 16], BF16)
    t_modc = sb("t_modc", [128, 64])
    t_gbgc = sb("t_gbgc", [128, 16])
    t_gamma = sb("t_gamma", [128, 12])
    t_gacc = sb("t_gacc", [128, 4])
    t_gq = sb("t_gq", [128, 48])
    t_grow = sb("t_grow", [1, 12])
    t_col = sb("t_col", [128, 16])
    esL = ExitStack()
    t_masklog = b.sb("t_masklog", [128, 512], scope=esL, side="left")
    t_onesel = b.sb("t_onesel", [16, 2048], scope=esL, side="left")
    t_oco = b.sb("t_oco", [128, 256], scope=esL, side="left")
    t_nwsel = b.sb("t_nwsel", [16, 2048], scope=esL, side="left")
    t_junk = b.sb("t_junk", [128, 2048], scope=esL, side="left")
    t_dthm = b.sb("t_dthm", [16, T], scope=esL, side="left")
    t_ldthm = b.sb("t_ldthm", [16, T], scope=esL, side="left")
    t_cumhm = b.sb("t_cumhm", [16, T], scope=esL, side="left")
    t_whm = b.sb("t_whm", [16, T], scope=esL, side="left")
    t_z16 = b.sb("t_z16", [16, 256], scope=esL, side="left")
    t_S = [b.sb(f"t_S{h}", [128, 128], scope=esL, side="left")
           for h in range(NH)]
    t_msrow = b.sb("t_msrow", [1, 2048], scope=esL, side="left")

    pb = [es.enter_context(nc.psum_tensor(f"pb{i}", [128, 512], F32))
          for i in range(6)]
    pt32 = es.enter_context(nc.psum_tensor("pt32", [128, 512], F32))
    ptb = es.enter_context(nc.psum_tensor("ptb", [128, 1024], BF16))

    s_const = b.sem("s_const")
    s_pe = b.sem("s_pe")
    s_v = b.sem("s_v")
    s_a = b.sem("s_a")
    s_g = b.sem("s_g")
    s_ev = b.sem("s_ev")
    s_mod = b.sem("s_mod")
    s_mx1 = b.sem("s_mx1")
    s_mx2 = b.sem("s_mx2")
    s_hd = b.sem("s_hd")
    s_gd = b.sem("s_gd")
    s_st = b.sem("s_st")

    b.blk = es.enter_context(nc.Block())
    st = b.step

    def _c(sy):
        m = None
        for dst, src in [
            (t_identb, identb), (t_identf, identf), (t_masklog, masklog),
            (t_onesel, onesel), (t_oco, oco), (t_ones_b, onesrow_b),
            (t_ones_f, onesrow_f), (t_onesc, onescol_f), (t_halom, halom),
            (t_prem, premrow), (t_premB, premB), (t_eps, epscol), (t_convw, convw), (t_convb, convb),
            (t_acol, acol), (t_dtbcol, dtbcol), (t_drow, drow),
            (t_gb2, gb2), (t_vb2, vb2),
            (t_cpk, c_pk),
        ]:
            m = sy.dma_start(out=dst[:], in_=src)
            s_const.inc(m, 16)
        return None
    st("sync", _c)
    m_const = s_const.mark()

    m_nwz = st("gpsimd", lambda g: (g.memset(t_nwsel[:], 0.0),
                                    g.memset(t_z16[:], 0.0))[-1],
               sem=s_g)

    def _nw(sy):
        m = None
        for k in range(16):
            m = sy.dma_start(out=t_nwsel[k:k + 1, 128 * k:128 * (k + 1)],
                             in_=nwrow[0:1, 128 * k:128 * (k + 1)])
            s_const.inc(m, 16)
        return None
    st("sync", _nw, after=[m_nwz, m_const])
    m_nwsel = s_const.mark()

    m_silc = st("scalar", lambda a: a.activation(out=t_silc[:], in_=t_cpk[:],
                                                 func=AF.Silu),
                after=[m_const], sem=s_a)

    m_acol0 = st("scalar", lambda a: a.activation(out=t_acol[:], in_=t_acol[:],
                                                  func=AF.Exp),
                 after=[m_const], sem=s_a)
    m_acol = st("vector", lambda v: v.tensor_scalar(
        out=t_acol[:], in0=t_acol[:], scalar1=-1.0, scalar2=None,
        op0=ALU.mult), after=[m_acol0], sem=s_v)

    m_pedb = st("tensor", lambda t: t.matmul(pt32[:, 0:16], t_ones_f[:],
                                             t_drow[:], start=True, stop=True),
                after=[m_const], sem=s_pe)
    m_db = st("vector", lambda v: v.tensor_copy(out=t_db[:], in_=pt32[:, 0:16]),
              after=[m_pedb], sem=s_v)

    with nc.sbuf_tensor("t_wm0", [128, 2048], BF16) as _wm0, \
         nc.sbuf_tensor("t_wm1", [128, 2048], BF16) as _wm1:
        t_wm = [_wm0, _wm1]
        s_wm = [b.sem("s_wm0"), b.sem("s_wm1")]
        s_mw = [b.sem(f"s_mw{i}") for i in range(3)]
        mod_evac = []
        mm_done = []
        for cs in range(6):
            for k in range(16):
                i = cs * 16 + k
                sl = i % 2
                aft_l = [mm_done[i - 2]] if i >= 2 else []
                st("gpsimd",
                   lambda g, cs=cs, k=k, sl=sl: g.dma_start(
                       out=t_wm[sl][:],
                       in_=wmod[128 * k:128 * (k + 1),
                                2048 * cs:2048 * (cs + 1)]),
                   after=aft_l, sem=s_wm[sl], k=16)
                m_l = (s_wm[sl], 16 * (i // 2 + 1))

                def _mm(t, cs=cs, k=k, sl=sl):
                    m = None
                    for j in range(4):
                        m = t.matmul(pb[j][0:1, :], t_silc[:, k:k + 1],
                                     t_wm[sl][:, 512 * j:512 * (j + 1)],
                                     start=(k == 0), stop=(k == 15))
                    return m
                aft = [m_l, m_silc, m_db]
                if k == 0 and cs >= 1:
                    aft.append(mod_evac[cs - 1][0])
                st("tensor", _mm, after=aft, sem=s_pe)
                mm_done.append(s_pe.mark())

            def _ev(a, cs=cs):
                m = None
                for j in range(4):
                    m = a.activation(
                        out=t_msrow[0:1, 512 * j:512 * (j + 1)],
                        in_=pb[j][0:1, :], func=AF.Copy)
                return m
            m_ev = st("scalar", _ev, after=[mm_done[-1]] +
                      ([mod_evac[cs - 1][1]] if cs >= 1 else []), sem=s_a)
            grp = 0 if cs < 2 else (1 if cs < 5 else 2)
            m_wr = st("sync", lambda sy, cs=cs: sy.dma_start(
                out=modsc[0:1, 2048 * cs:2048 * (cs + 1)], in_=t_msrow[:]),
                after=[m_ev], sem=s_mw[grp], k=16)
            mod_evac.append((m_ev, m_wr))
        m_mrow = st("gpsimd", lambda g: g.engine_nop(),
                    after=[(s_mw[0], 32), (s_mw[1], 48), (s_mw[2], 16)],
                    sem=s_g)

    def _rt1(sy):
        for i, off in enumerate([0, 2048]):
            m = sy.dma_start(
                out=t_modc[:, 16 * i:16 * (i + 1)],
                in_=modsc[0:1, off:off + 2048].rearrange(
                    "a (k p) -> p a k", p=128).rearrange("p a k -> (p a) k"),
                allow_slow_non_contiguous=True)
            s_mx1.inc(m, 16)
        return None
    st("sync", _rt1, after=[mod_evac[1][1]])
    m_modc0 = s_mx1.mark()

    m_modc1 = st("vector", lambda v: v.tensor_scalar_add(
        out=t_modc[:, 16:32], in0=t_modc[:, 16:32], scalar1=1.0),
        after=[m_modc0], sem=s_v)

    def _rt2(sy):
        for i, off in enumerate([3 * 2048, 4 * 2048]):
            m = sy.dma_start(
                out=t_modc[:, 32 + 16 * i:48 + 16 * i],
                in_=modsc[0:1, off:off + 2048].rearrange(
                    "a (k p) -> p a k", p=128).rearrange("p a k -> (p a) k"),
                allow_slow_non_contiguous=True)
            s_mx2.inc(m, 16)
        return None
    st("sync", _rt2, after=[mod_evac[4][1]])
    m_modc2x = s_mx2.mark()

    m_modc = st("vector", lambda v: v.tensor_scalar_add(
        out=t_modc[:, 48:64], in0=t_modc[:, 48:64], scalar1=1.0),
        after=[m_modc2x, m_modc1], sem=s_v)

    def bcast_row(row_ap, out_tile, width, bf16, after):
        prev = None
        ones = t_ones_b if bf16 else t_ones_f
        for j in range((width + 511) // 512):
            w0, w1 = 512 * j, min(512 * (j + 1), width)

            def _t(t, w0=w0, w1=w1):
                return t.matmul(pt32[:, 0:w1 - w0], ones[:],
                                row_ap[0:1, w0:w1], start=True, stop=True)
            aft = list(after) + ([prev] if prev else [])
            m_b = st("tensor", _t, after=aft, sem=s_pe)

            def _v(v, w0=w0, w1=w1):
                return v.tensor_copy(out=out_tile[:, w0:w1],
                                     in_=pt32[:, 0:w1 - w0])
            prev = st("vector", _v, after=[m_b], sem=s_v)
        return prev

    gam_srcs = []
    for j in range(16):
        gam_srcs.append(gwB[4 * j:4 * (j + 1)].rearrange("a p w -> p a w"))
    for j in range(16):
        gam_srcs.append(vwB[4 * j:4 * (j + 1)].rearrange("a p w -> p a w"))
    for j in range(16):
        gam_srcs.append(owB[j].rearrange("s p w -> p s w"))

    def emit_gamma_tail(aft0):
        m_gr0 = st("tensor", lambda t: t.matmul(pt32[0:1, 0:48], t_onesc[:],
                                                t_gq[:], start=True,
                                                stop=True),
                   after=aft0 + [m_const], sem=s_pe)

        def _grsum(v):
            for mi in range(3):
                v.tensor_reduce(out=pt32[0:1, 48 + mi:49 + mi],
                                in_=pt32[0:1, 16 * mi:16 * (mi + 1)],
                                axis=AXX, op=ALU.add)
            v.drain()
            return v.tensor_copy(out=pt32[0:1, 51:52], in_=pt32[0:1, 48:49])
        m_gr = st("vector", _grsum, after=[m_gr0], sem=s_v)

        def _gfin(v):
            v.tensor_scalar(out=t_grow[0:1, 0:4], in0=pt32[0:1, 48:52],
                            scalar1=1.0 / (FFH * HID), scalar2=1e-5,
                            op0=ALU.mult, op1=ALU.max)
            v.drain()
            v.tensor_scalar(out=t_grow[0:1, 0:4], in0=t_grow[0:1, 0:4],
                            scalar1=0.5, scalar2=None, op0=ALU.mult)
            v.drain()
            v.tensor_scalar(out=t_grow[0:1, 4:8], in0=t_grow[0:1, 0:4],
                            scalar1=-1.0, scalar2=None, op0=ALU.mult)
            v.drain()
            return v.tensor_scalar(out=t_grow[0:1, 8:12],
                                   in0=t_grow[0:1, 0:4],
                                   scalar1=2.0, scalar2=None, op0=ALU.mult)
        m_gfin = st("vector", _gfin, after=[m_gr], sem=s_v)

        m_gcb = st("tensor", lambda t: t.matmul(pt32[:, 0:12], t_ones_f[:],
                                                t_grow[0:1, 0:12], start=True,
                                                stop=True),
                   after=[m_gfin], sem=s_pe)
        return st("vector", lambda v: v.tensor_copy(out=t_gamma[:],
                                                    in_=pt32[:, 0:12]),
                  after=[m_gcb], sem=s_v)

    if debug:
        d_mrow = b.ddbg("d_mrow", [1, 12288])
        st("sync", lambda sy: sy.dma_start(out=d_mrow, in_=modsc[0:1, 0:12288]),
           after=[m_mrow], sem=s_st, k=16)

    m_join0 = st("gpsimd", lambda g: g.engine_nop(),
                 after=[m_nwsel, m_db,
                        (s_st.mark() if debug else None)], sem=s_g)
    m_tail = m_join0
    m_gamt = None

    def emit_ln(xtile, utok, rows, after_x, c0=0, stepf=None):
        cA = t_col[:, c0:c0 + 1]
        cB = t_col[:, c0 + 1:c0 + 2]
        cC = t_col[:, c0 + 2:c0 + 3]
        cD = t_col[:, c0 + 3:c0 + 4]
        cE = t_col[:, c0 + 4:c0 + 5]
        cF = t_col[:, c0 + 5:c0 + 6]
        xs = xtile[0:rows, :]

        def _v1(v):
            v.scalar_tensor_tensor(out=t_junk[0:rows, :], in0=xs, scalar=1.0,
                                   in1=xs, op0=ALU.mult, op1=ALU.mult,
                                   accum_out=cB[0:rows])
            v.tensor_reduce(out=cA[0:rows], in_=xs, axis=AXX, op=ALU.add)
            v.drain()
            v.tensor_scalar(out=cC[0:rows], in0=cA[0:rows],
                            scalar1=1.0 / HID, scalar2=None, op0=ALU.mult)
            v.drain()
            v.tensor_scalar(out=cD[0:rows], in0=cC[0:rows], scalar1=cC[0:rows],
                            scalar2=-1.0, op0=ALU.mult, op1=ALU.mult)
            v.drain()
            return v.tensor_scalar(out=cD[0:rows], in0=cD[0:rows],
                                   scalar1=EPS, scalar2=None, op0=ALU.add)
        stf = stepf or st
        mv1 = stf("vector", _v1, after=after_x, sem=s_v)

        def _a2(a):
            a.activation(out=cE[0:rows], in_=cB[0:rows], func=AF.Ln,
                         bias=cD[0:rows], scale=1.0 / HID)
            a.drain()
            return a.activation(out=cD[0:rows], in_=cE[0:rows], func=AF.Exp,
                                scale=-0.5)
        ma2 = stf("scalar", _a2, after=[mv1], sem=s_a)

        def _v2(v):
            v.tensor_scalar(out=cF[0:rows], in0=cC[0:rows],
                            scalar1=cD[0:rows], scalar2=-1.0,
                            op0=ALU.mult, op1=ALU.mult)
            v.drain()
            return v.tensor_scalar(out=utok[0:rows, :], in0=xs,
                                   scalar1=cD[0:rows], scalar2=cF[0:rows],
                                   op0=ALU.mult, op1=ALU.add)
        mu = stf("vector", _v2, after=[ma2], sem=s_v)
        return mu

    if enabled("pre"):
        esP = ExitStack()
        t_wpre = b.sb("t_wpre", [128, 16 * 1040], BF16, scope=esP)
        t_px = [b.sb(f"t_px{i}", [128, HID], scope=esP) for i in range(2)]
        t_put = b.sb("t_put", [128, HID], BF16, scope=esP)
        t_upf = [b.sb(f"t_upf{k}", [128, 512], BF16, scope=esP)
                 for k in range(16)]
        t_psc = [b.sb(f"t_psc{cc}", [128, 2 * 259], scope=esP)
                 for cc in range(8)]
        t_tails = b.sb("t_tails", [128, 24], scope=esP)
        t_pxb = [b.sb(f"t_pxb{cc}", [128, 256], BF16, scope=esP)
                 for cc in range(8)]
        t_cvp = b.sb("t_cvp", [128, 256], scope=esP)
        t_cgp = b.sb("t_cgp", [128, 256], scope=esP)
        t_csp = b.sb("t_csp", [128, 8 * 256], scope=esP)
        t_dtp = b.sb("t_dtp", [16, 512], scope=esP)
        t_ldp = b.sb("t_ldp", [16, 256], scope=esP)
        t_cup = b.sb("t_cup", [16, 256], scope=esP)
        t_whp = b.sb("t_whp", [16, 256], scope=esP)
        t_xctp = [b.sb(f"t_xctp{i}", [128, 128], BF16, scope=esP)
                  for i in range(8)]
        t_btp = [b.sb(f"t_btp{i}", [128, 128], BF16, scope=esP)
                 for i in range(8)]
        t_bwp = b.sb("t_bwp", [128, 8 * 128], BF16, scope=esP)
        t_wtmp = [b.sb(f"t_wtmp{j}", [128, 16], scope=esP) for j in range(2)]
        t_cdbp = b.sb("t_cdbp", [128, 16], scope=esP)
        t_clrp = b.sb("t_clrp", [1, 16], scope=esP)
        s_px = [b.sem("s_px0"), b.sem("s_px1")]
        s_wpre = b.sem("s_wpre")

        def _pz(g):
            for h in range(NH):
                g.memset(t_S[h][:], 0.0)
            return g.memset(t_tails[:], 0.0)
        m_pz = st("gpsimd", _pz, after=[], sem=s_g)

        st("gpsimd", lambda g: g.dma_start(
            out=t_wpre[:, 0:16 * 1024],
            in_=w_in[:, 2048:3072].rearrange("(c p) w -> p c w", p=128)),
           after=[], sem=s_wpre, k=16)
        st("gpsimd", lambda g: g.dma_start(
            out=t_wpre[:, 16 * 1024:16 * 1040],
            in_=w_in[:, 5120:5136].rearrange("(c p) w -> p c w", p=128)),
           sem=s_wpre, k=16)
        m_wpre = s_wpre.mark()

        stA = {}
        last_mu = [None, None]
        last_wave = None
        last_b6 = None
        last_su = [m_pz, m_pz]
        last_evs = None
        prev_b = {"tails": m_pz, "conv": None, "sil": None}

        def stage_a(q, c):
            par = c % 2
            r0 = 256 * c
            A = {}
            dstp = lambda which, emit, after=(), sem=None, k=1: \
                b.defer(q, which, emit, after, sem, k)
            mus = []
            waves = []
            for i in range(2):
                aft_l = [m_pz, m_join0] if last_mu[i] is None \
                    else [last_mu[i]]
                lx = dstp("sync", lambda sy, r0=r0, i=i: sy.dma_start(
                    out=t_px[i][:],
                    in_=xp[r0 + 128 * i:r0 + 128 * (i + 1), :]),
                    after=aft_l, sem=s_px[i], k=16)
                aft_x = [lx] + ([mus[0]] if i == 1 else [])
                mu = emit_ln(t_px[i], t_put, 128, aft_x, c0=8, stepf=dstp)
                mus.append(mu)
                last_mu[i] = mu
                for w in range(2):
                    def _twp(t, i=i, w=w):
                        m = None
                        for kk in range(8):
                            k = 8 * w + kk
                            m = t.transpose(
                                ptb[:, 128 * kk:128 * (kk + 1)],
                                t_put[:, 128 * k:128 * (k + 1)],
                                t_identb[:])
                        return m
                    aft = [mu] + ([waves[-1]] if waves else
                                  ([last_b6] if last_b6 else []))
                    m_t = dstp("tensor", _twp, after=aft, sem=s_pe)

                    def _evp(a, i=i, w=w, par=par):
                        m = None
                        for kk in range(8):
                            k = 8 * w + kk
                            m = a.activation(
                                out=t_upf[k][:, 256 * par + 128 * i:
                                             256 * par + 128 * (i + 1)],
                                in_=ptb[:, 128 * kk:128 * (kk + 1)],
                                func=AF.Identity,
                                scale=t_modc[:, 16 + k:17 + k],
                                bias=t_modc[:, k:k + 1])
                        return m
                    mw = dstp("scalar", _evp, after=[m_t, m_modc1], sem=s_a)
                    waves.append(mw)
            A["upf"] = waves[-1]
            A["lastwave"] = waves[-1]

            evs = []
            prev_pair = None
            for jp in range(4):
                def _pmm(t, jp=jp, par=par):
                    m = None
                    for j in (2 * jp, 2 * jp + 1):
                        pX = pb[j % 2]
                        for k in range(16):
                            m = t.matmul(
                                pX[:, 0:256],
                                t_wpre[:, 1024 * k + 128 * j:
                                       1024 * k + 128 * (j + 1)],
                                t_upf[k][:, 256 * par:256 * (par + 1)],
                                start=(k == 0), stop=(k == 15))
                    return m
                aft = [A["upf"], m_wpre]
                if prev_pair is not None:
                    aft.append(prev_pair)
                elif last_evs is not None:
                    aft.append(last_evs[-1])
                m_pm = dstp("tensor", _pmm, after=aft, sem=s_pe)

                def _pev(v, jp=jp, par=par, c=c):
                    m = None
                    for j in (2 * jp, 2 * jp + 1):
                        m = v.tensor_scalar(
                            out=t_psc[j][:, 259 * par + 3:259 * par + 259],
                            in0=pb[j % 2][:, 0:256],
                            scalar1=t_premB[:, c:c + 1],
                            scalar2=None, op0=ALU.mult)
                    return m
                aft_e = [m_pm]
                if prev_b["conv"] is not None:
                    aft_e.append(prev_b["conv"])
                pe_ = dstp("vector", _pev, after=aft_e, sem=s_v)
                evs.append(pe_)
                prev_pair = pe_
            A["evs"] = evs

            def _pdt(t):
                m = None
                for k in range(16):
                    m = t.matmul(pb[2][0:16, 0:256],
                                 t_wpre[:, 16 * 1024 + 16 * k:
                                        16 * 1024 + 16 * (k + 1)],
                                 t_upf[k][:, 256 * par:256 * (par + 1)],
                                 start=(k == 0), stop=(k == 15))
                return m
            aft_dt = [A["upf"], m_wpre]
            if "dtev" in stA.get(c - 1, {}):
                aft_dt.append(stA[c - 1]["dtev"])
            m_pdt = dstp("tensor", _pdt, after=aft_dt, sem=s_pe)
            A["dtev"] = dstp("vector", lambda v, par=par: v.tensor_copy(
                out=t_dtp[0:16, 256 * par:256 * (par + 1)],
                in_=pb[2][0:16, 0:256]), after=[m_pdt], sem=s_v)
            stA[c] = A
            return A

        def stage_b(q, c):
            par = c % 2
            A = stA[c]
            dstp = lambda which, emit, after=(), sem=None, k=1: \
                b.defer(q, which, emit, after, sem, k)
            def _tl(v, par=par):
                m = None
                for cc in range(8):
                    m = v.tensor_copy(
                        out=t_psc[cc][:, 259 * par:259 * par + 3],
                        in_=t_tails[:, 3 * cc:3 * cc + 3])
                return m
            m_tl = dstp("vector", _tl, after=[prev_b["tails"], A["evs"][0]],
                        sem=s_v)

            convs = []
            for half in range(2):
                def _cvp(v, half=half, par=par):
                    m = None
                    for cc in range(4 * half, 4 * half + 4):
                        sc = t_psc[cc][:, 259 * par:259 * (par + 1)]
                        v.tensor_scalar(out=t_cvp[:], in0=sc[:, 0:256],
                                        scalar1=t_convw[:, 4 * cc:4 * cc + 1],
                                        scalar2=None, op0=ALU.mult)
                        v.tensor_scalar(out=t_cgp[:], in0=sc[:, 2:258],
                                        scalar1=t_convw[:, 4 * cc + 2:
                                                        4 * cc + 3],
                                        scalar2=None, op0=ALU.mult)
                        v.drain()
                        v.scalar_tensor_tensor(
                            out=t_cvp[:], in0=sc[:, 1:257],
                            scalar=t_convw[:, 4 * cc + 1:4 * cc + 2],
                            in1=t_cvp[:], op0=ALU.mult, op1=ALU.add)
                        v.scalar_tensor_tensor(
                            out=t_cgp[:], in0=sc[:, 3:259],
                            scalar=t_convw[:, 4 * cc + 3:4 * cc + 4],
                            in1=t_cgp[:], op0=ALU.mult, op1=ALU.add)
                        v.drain()
                        m = v.tensor_tensor(
                            out=t_csp[:, 256 * cc:256 * (cc + 1)],
                            in0=t_cvp[:], in1=t_cgp[:], op=ALU.add)
                    return m
                cv = dstp("vector", _cvp,
                          after=[A["evs"][2 * half + 1], m_tl], sem=s_v)
                convs.append(cv)
            prev_b["conv"] = convs[-1]

            def _sil(a):
                m = None
                for cc in range(8):
                    m = a.activation(out=t_pxb[cc][:],
                                     in_=t_csp[:, 256 * cc:256 * (cc + 1)],
                                     func=AF.Silu, bias=t_convb[:, cc:cc + 1])
                return m
            m_si = dstp("scalar", _sil, after=convs, sem=s_a)
            prev_b["sil"] = m_si

            def _tw(v, par=par):
                m = None
                for cc in range(8):
                    m = v.tensor_copy(
                        out=t_tails[:, 3 * cc:3 * cc + 3],
                        in_=t_psc[cc][:, 259 * par + 256:259 * par + 259])
                return m
            prev_b["tails"] = dstp("vector", _tw, after=[m_si], sem=s_v)

            ps1 = dstp("vector", lambda v, par=par: v.tensor_scalar(
                out=t_dtp[0:16, 256 * par:256 * (par + 1)],
                in0=t_dtp[0:16, 256 * par:256 * (par + 1)],
                scalar1=t_dtbcol[0:16, 0:1], scalar2=None, op0=ALU.add),
                after=[A["dtev"]], sem=s_v)

            def _ps2(a, par=par):
                dtv = t_dtp[0:16, 256 * par:256 * (par + 1)]
                a.activation(out=t_ldp[:], in_=dtv, func=AF.Abs)
                a.drain()
                a.activation(out=t_ldp[:], in_=t_ldp[:], func=AF.Exp,
                             scale=-1.0)
                a.drain()
                return a.activation(out=t_ldp[:], in_=t_ldp[:], func=AF.Ln,
                                    bias=1.0)
            ps2 = dstp("scalar", _ps2, after=[ps1], sem=s_a)

            def _ps3(v, par=par):
                dtv = t_dtp[0:16, 256 * par:256 * (par + 1)]
                v.tensor_scalar(out=t_cup[:], in0=dtv, scalar1=0.0,
                                scalar2=None, op0=ALU.max)
                v.drain()
                return v.tensor_tensor(out=dtv, in0=t_cup[:], in1=t_ldp[:],
                                       op=ALU.add)
            ps3 = dstp("vector", _ps3, after=[ps2], sem=s_v)
            pld = dstp("scalar", lambda a, par=par: a.activation(
                out=t_ldp[:], in_=t_dtp[0:16, 256 * par:256 * (par + 1)],
                func=AF.Ln), after=[ps3], sem=s_a)

            def _pcum(v, par=par):
                dtv = t_dtp[0:16, 256 * par:256 * (par + 1)]
                v.tensor_scalar(out=t_whp[:], in0=dtv,
                                scalar1=t_acol[0:16, 0:1], scalar2=None,
                                op0=ALU.mult)
                v.drain()
                v.tensor_tensor_scan(out=t_cup[:], data0=t_whp[:],
                                     data1=t_z16[:, 0:256], initial=0.0,
                                     op0=ALU.add, op1=ALU.bypass)
                v.drain()
                v.tensor_tensor(out=t_whp[:], in0=t_ldp[:], in1=t_cup[:],
                                op=ALU.subtract)
                v.drain()
                return v.tensor_scalar(out=t_whp[:], in0=t_whp[:],
                                       scalar1=t_cup[:, 255:256],
                                       scalar2=None, op0=ALU.add)
            pcum = dstp("vector", _pcum, after=[pld, m_acol], sem=s_v)

            pc1 = dstp("tensor", lambda t: t.transpose(
                pt32[0:1, 0:16], t_cup[:, 255:256], t_identf[0:16, 0:16]),
                after=[pcum, last_su[0], last_su[1]], sem=s_pe)

            def _pclr2(v, c=c):
                v.tensor_copy(out=t_clrp[:], in_=pt32[0:1, 0:16])
                v.drain()
                return v.tensor_scalar(out=t_clrp[:], in0=t_clrp[:],
                                       scalar1=t_prem[0:1, c:c + 1],
                                       scalar2=None, op0=ALU.mult)
            pc2 = dstp("vector", _pclr2, after=[pc1], sem=s_v)
            pc3 = dstp("tensor", lambda t: t.matmul(
                pt32[:, 64:80], t_ones_f[:], t_clrp[:], start=True,
                stop=True), after=[pc2], sem=s_pe)

            def _pexp(a):
                a.activation(out=t_whp[:], in_=t_whp[:], func=AF.Exp)
                return a.activation(out=t_cdbp[:], in_=pt32[:, 64:80],
                                    func=AF.Exp)
            pcdb = dstp("scalar", _pexp, after=[pc3, pcum], sem=s_a)

            def _pwt(t):
                t.transpose(pt32[:, 0:16], t_whp[0:16, 0:128],
                            t_identf[0:16, 0:16])
                return t.transpose(pt32[:, 16:32], t_whp[0:16, 128:256],
                                   t_identf[0:16, 0:16])
            pwt = dstp("tensor", _pwt, after=[pcdb], sem=s_pe)

            def _pwt2(v):
                v.tensor_copy(out=t_wtmp[0][:], in_=pt32[:, 0:16])
                return v.tensor_copy(out=t_wtmp[1][:], in_=pt32[:, 16:32])
            pwt2 = dstp("vector", _pwt2, after=[pwt], sem=s_v)

            nonloc = {}

            def _pt1(t):
                m = None
                for g_ in range(4):
                    for jc in range(2):
                        m = t.transpose(
                            ptb[:, 128 * (2 * g_ + jc):
                                128 * (2 * g_ + jc + 1)],
                            t_pxb[g_][:, 128 * jc:128 * (jc + 1)],
                            t_identb[:])
                return m
            aft1 = [m_si, last_su[0], last_su[1]]
            if c + 1 in stA:
                aft1.append(stA[c + 1]["lastwave"])
            pt1 = dstp("tensor", _pt1, after=aft1, sem=s_pe)

            def _pe1(a):
                m = None
                for i in range(8):
                    m = a.activation(out=t_xctp[i][:],
                                     in_=ptb[:, 128 * i:128 * (i + 1)],
                                     func=AF.Copy)
                return m
            pe1 = dstp("scalar", _pe1, after=[pt1], sem=s_a)

            def _pt2(t):
                m = None
                for g_ in range(4):
                    for jc in range(2):
                        m = t.transpose(
                            ptb[:, 128 * (2 * g_ + jc):
                                128 * (2 * g_ + jc + 1)],
                            t_pxb[4 + g_][:, 128 * jc:128 * (jc + 1)],
                            t_identb[:])
                return m
            pt2 = dstp("tensor", _pt2, after=[pe1], sem=s_pe)

            def _pe2(v):
                m = None
                for i in range(8):
                    m = v.tensor_copy(out=t_btp[i][:],
                                      in_=ptb[:, 128 * i:128 * (i + 1)])
                return m
            pe2 = dstp("vector", _pe2, after=[pt2], sem=s_v)
            b6 = pe2

            for grp in range(4):
                gpar = grp % 2
                pSc = pb[3 + gpar]

                def _gbw(v, grp=grp):
                    m = None
                    for hh in range(4):
                        h = 4 * grp + hh
                        g_ = h // 4
                        po = 256 * hh
                        v.tensor_scalar(
                            out=t_bwp[:, po:po + 128],
                            in0=t_btp[2 * g_][:],
                            scalar1=t_wtmp[0][:, h:h + 1],
                            scalar2=None, op0=ALU.mult)
                        m = v.tensor_scalar(
                            out=t_bwp[:, po + 128:po + 256],
                            in0=t_btp[2 * g_ + 1][:],
                            scalar1=t_wtmp[1][:, h:h + 1],
                            scalar2=None, op0=ALU.mult)
                    return m
                gbw = dstp("vector", _gbw,
                           after=[pe2, pwt2, last_su[gpar]], sem=s_v)

                def _gsc(t, grp=grp, pSc=pSc):
                    m = None
                    for hh in range(4):
                        h = 4 * grp + hh
                        g_ = h // 4
                        po = 256 * hh
                        t.matmul(pSc[:, 128 * hh:128 * (hh + 1)],
                                 t_bwp[:, po:po + 128],
                                 t_xctp[2 * g_][:], start=True, stop=False)
                        m = t.matmul(pSc[:, 128 * hh:128 * (hh + 1)],
                                     t_bwp[:, po + 128:po + 256],
                                     t_xctp[2 * g_ + 1][:], start=False,
                                     stop=True)
                    return m
                gsc = dstp("tensor", _gsc, after=[gbw, last_su[gpar]],
                           sem=s_pe)

                def _gsu(v, grp=grp, pSc=pSc):
                    m = None
                    for hh in range(4):
                        h = 4 * grp + hh
                        m = v.scalar_tensor_tensor(
                            out=t_S[h][:], in0=t_S[h][:],
                            scalar=t_cdbp[:, h:h + 1],
                            in1=pSc[:, 128 * hh:128 * (hh + 1)],
                            op0=ALU.mult, op1=ALU.add)
                    return m
                gsu = dstp("vector", _gsu, after=[gsc, pcdb], sem=s_v)
                last_su[gpar] = gsu
            return b6

        t_gmb = [b.sb(f"t_gmb{i}", [128, 8192], BF16, scope=esP)
                 for i in range(2)]
        s_gam = [b.sem("s_gam0"), b.sem("s_gam1")]
        gam_red = []
        qa = []
        stage_a(qa, 0)
        b.flush_zip([qa])
        for c in range(12):
            qa, qb = [], []
            if c + 1 < 12:
                stage_a(qa, c + 1)
            last_b6 = stage_b(qb, c)
            b.flush_zip([qa, qb])
            for i in range(4 * c, 4 * c + 4):
                sl = i % 2
                aft = [gam_red[i - 2]] if i >= 2 else []
                st("gpsimd", lambda g, i=i, sl=sl: g.dma_start(
                    out=t_gmb[sl][:], in_=gam_srcs[i]),
                   after=aft, sem=s_gam[sl], k=16)
                m_l = s_gam[sl].mark()

                def _gred(v, sl=sl, i=i):
                    return v.scalar_tensor_tensor(
                        out=t_gmb[sl][:], in0=t_gmb[sl][:], scalar=-1.0,
                        in1=t_gmb[sl][:], op0=ALU.mult, op1=ALU.max,
                        accum_out=t_gq[:, i:i + 1])
                gam_red.append(st("vector", _gred, after=[m_l], sem=s_v))

        m_pre = st("gpsimd", lambda g: g.engine_nop(),
                   after=[(last_su[0].sem, last_su[0].val),
                          (last_su[1].sem, last_su[1].val),
                          (prev_b["tails"].sem, prev_b["tails"].val)],
                   sem=s_g)
        m_tail = m_pre
        m_gamt = emit_gamma_tail([gam_red[-1], m_pre])
        esP.close()

        if debug:
            d_sinit = b.ddbg("d_sinit", [NH * 128, 128])

            def _dbgp(sy):
                for h in range(16):
                    s_st.inc(sy.dma_start(
                        out=d_sinit[128 * h:128 * (h + 1), :],
                        in_=t_S[h][:]), 16)
                sy.wait_ge(s_st.h, s_st.n)
                return None
            st("sync", _dbgp, after=[m_pre])
            m_pre = st("gpsimd", lambda g: g.engine_nop(),
                       after=[m_pre, (s_st.mark())], sem=s_g)
            m_tail = m_pre

    if enabled("ph1"):
        esA = ExitStack()
        t_sg = [b.sb(f"t_sg{k}", [128, T], BF16, scope=esA) for k in range(16)]
        esA2 = ExitStack()
        t_xbc = [b.sb(f"t_xbc{c}", [128, T], BF16, scope=esA2)
                 for c in range(24)]
        t_bias_hm = b.sb("t_bias_hm", [16, T], scope=esA2)
        esP1 = ExitStack()
        t_x = [b.sb("t_x0", [128, HID], scope=esP1)]
        t_utok = b.sb("t_utok", [128, HID], BF16, scope=esP1)
        t_u1 = [b.sb(f"t_u1_{k}", [128, 3 + T], BF16, scope=esP1)
                for k in range(16)]
        t_wsl = [b.sb(f"t_wsl{i}", [128, 16 * 128], BF16, scope=esP1)
                 for i in range(2)]
        t_wdt = b.sb("t_wdt", [128, 256], BF16, scope=esP1)
        t_scr = [b.sb("t_scr0", [128, 3 + T], scope=esP1)]
        t_cv = t_junk[:, 0:1024]
        t_cg2 = t_x[0][:, 1024:2048]
        t_cs = [t_junk[:, 1024:2048]]
        s_x = [b.sem("s_x0")]
        s_wsl = [b.sem("s_wsl0"), b.sem("s_wsl1")]

        m_u = [None] * 9
        m_tw = [None] * 9
        prev_evac = None
        for i in range(9):
            rows = 128 if i < 8 else 3
            sl = 0
            aft_l = [m_tail] if i < 1 else [m_u[i - 1]]
            st("sync", lambda sy, i=i, sl=sl, rows=rows: sy.dma_start(
                out=t_x[sl][0:rows, :], in_=xe[128 * i:128 * i + rows, :]),
               after=aft_l, sem=s_x[sl], k=16)
            m_xl = (s_x[sl], s_x[sl].n)
            aft_x = [m_xl] + ([m_tw[i - 1]] if i >= 1 else [])
            m_u[i] = emit_ln(t_x[sl], t_utok, rows, aft_x)

            prev_w = None
            for w in range(2):
                def _tw(t, i=i, w=w, rows=rows):
                    m = None
                    for kk in range(8):
                        k = 8 * w + kk
                        m = t.transpose(
                            ptb[:, 128 * kk:128 * kk + rows],
                            t_utok[0:rows, 128 * k:128 * (k + 1)],
                            t_identb[0:rows, 0:rows])
                    return m
                aft = [m_u[i]] + ([prev_w] if prev_w else
                                  ([m_tw[i - 1]] if i >= 1 else []))
                m_t = st("tensor", _tw, after=aft, sem=s_pe)

                def _ev(v, i=i, w=w, rows=rows):
                    m = None
                    off = 128 * i
                    for kk in range(8):
                        k = 8 * w + kk
                        m = v.tensor_scalar(
                            out=t_u1[k][:, off:off + rows],
                            in0=ptb[:, 128 * kk:128 * kk + rows],
                            scalar1=t_modc[:, 16 + k:17 + k],
                            scalar2=t_modc[:, k:k + 1],
                            op0=ALU.mult, op1=ALU.add)
                    return m
                prev_w = st("vector", _ev, after=[m_t, m_modc1], sem=s_v)
            m_tw[i] = prev_w
        m_u1all = m_tw[8]

        wfree = [None, None]
        mm_ct = [None] * 41
        ev_ct = [None] * 41
        prev_silu = None
        for cs in range(40):
            sl = cs % 2
            aft_l = [wfree[sl]] if wfree[sl] else [m_tail, m_u1all]
            st("gpsimd", lambda g, cs=cs, sl=sl: g.dma_start(
                out=t_wsl[sl][:],
                in_=w_in[:, 128 * cs:128 * (cs + 1)].rearrange(
                    "(c p) w -> p c w", p=128)),
               after=aft_l, sem=s_wsl[sl], k=16)
            m_wl = (s_wsl[sl], 16 * (cs // 2 + 1))
            for j in range(1):
                ct = cs
                is_hbc = 16 <= ct < 40
                pA = pb[2 * (ct % 2)]
                pB = pb[2 * (ct % 2) + 1]
                pH = pb[4 + ct % 2]
                ph_off = 0

                def _mm(t, ct=ct, sl=sl, j=j, is_hbc=is_hbc, pA=pA, pB=pB,
                        pH=pH, ph_off=ph_off):
                    m = None
                    for k in range(16):
                        lh = t_wsl[sl][:, 128 * k:128 * (k + 1)]
                        m = t.matmul(pA[:, :], lh, t_u1[k][:, 3:515],
                                     start=(k == 0), stop=(k == 15))
                        m = t.matmul(pB[:, :], lh, t_u1[k][:, 515:1027],
                                     start=(k == 0), stop=(k == 15))
                        if is_hbc:
                            m = t.matmul(pH[:, ph_off:ph_off + 3], lh,
                                         t_u1[k][:, 0:3],
                                         start=(k == 0), stop=(k == 15))
                    return m
                aft = [m_u1all, m_wl]
                if ct >= 2:
                    aft.append(ev_ct[ct - 2])
                mm_ct[ct] = st("tensor", _mm, after=aft, sem=s_pe)

                if ct < 16:
                    def _evg(a, ct=ct, pA=pA, pB=pB):
                        a.activation(out=t_sg[ct][:, 0:512], in_=pA[:, :],
                                     func=AF.Silu)
                        return a.activation(out=t_sg[ct][:, 512:1024],
                                            in_=pB[:, :], func=AF.Silu)
                    ev_ct[ct] = st("scalar", _evg, after=[mm_ct[ct]], sem=s_a)
                else:
                    cc = ct - 16
                    ssl = 0

                    def _eva(a, ct=ct, pA=pA, pB=pB, ssl=ssl):
                        a.activation(out=t_scr[ssl][:, 3:515], in_=pA[:, :],
                                     func=AF.Copy)
                        return a.activation(out=t_scr[ssl][:, 515:1027],
                                            in_=pB[:, :], func=AF.Copy)
                    m_ea = st("scalar", _eva, after=[mm_ct[ct]], sem=s_a)

                    def _evh(v, ct=ct, ssl=ssl, pH=pH, ph_off=ph_off):
                        return v.tensor_scalar(
                            out=t_scr[ssl][:, 0:3],
                            in0=pH[:, ph_off:ph_off + 3],
                            scalar1=t_halom[:, 0:1], scalar2=None,
                            op0=ALU.mult)
                    m_eh = st("vector", _evh, after=[mm_ct[ct], m_const],
                              sem=s_v)
                    ev_ct[ct] = m_ea

                    def _cv(v, cc=cc, ssl=ssl):
                        sc = t_scr[ssl]
                        v.tensor_scalar(out=t_cv, in0=sc[:, 0:1024],
                                        scalar1=t_convw[:, 4 * cc:4 * cc + 1],
                                        scalar2=None, op0=ALU.mult)
                        v.tensor_scalar(out=t_cg2, in0=sc[:, 2:1026],
                                        scalar1=t_convw[:, 4 * cc + 2:4 * cc + 3],
                                        scalar2=None, op0=ALU.mult)
                        v.drain()
                        v.scalar_tensor_tensor(
                            out=t_cv, in0=sc[:, 1:1025],
                            scalar=t_convw[:, 4 * cc + 1:4 * cc + 2],
                            in1=t_cv, op0=ALU.mult, op1=ALU.add)
                        return v.scalar_tensor_tensor(
                            out=t_cg2, in0=sc[:, 3:1027],
                            scalar=t_convw[:, 4 * cc + 3:4 * cc + 4],
                            in1=t_cg2, op0=ALU.mult, op1=ALU.add)
                    m_cv = st("vector", _cv, after=[m_ea, m_eh], sem=s_v)

                    m_cm = st("gpsimd",
                              lambda g, ssl=ssl: g.tensor_tensor(
                                  out=t_cs[0], in0=t_cv, in1=t_cg2,
                                  op=ALU.add),
                              after=[m_cv, prev_silu], sem=s_g)
                    st("scalar",
                       lambda a, cc=cc, ssl=ssl: a.activation(
                           out=t_xbc[cc][:], in_=t_cs[0], func=AF.Silu,
                           bias=t_convb[:, cc:cc + 1]),
                       after=[m_cm], sem=s_a)
                    ev_ct[ct] = s_a.mark()
                    prev_silu = ev_ct[ct]
            wfree[sl] = ev_ct[cs]

        st("gpsimd", lambda g: g.dma_start(
            out=t_wdt[:],
            in_=w_in[:, 5120:5136].rearrange("(c p) w -> p c w", p=128)),
           after=[m_tail], sem=s_gd, k=16)
        m_wdt = s_gd.mark()

        def _mmdt(t):
            m = None
            for k in range(16):
                lh = t_wdt[:, 16 * k:16 * (k + 1)]
                m = t.matmul(pb[4][0:16, :], lh, t_u1[k][:, 3:515],
                             start=(k == 0), stop=(k == 15))
                m = t.matmul(pb[5][0:16, :], lh, t_u1[k][:, 515:1027],
                             start=(k == 0), stop=(k == 15))
            return m
        m_mmdt = st("tensor", _mmdt,
                    after=[m_u1all, m_wdt, ev_ct[39], ev_ct[40 - 2]],
                    sem=s_pe)

        def _evdt(a):
            a.activation(out=t_dthm[:, 0:512], in_=pb[4][0:16, :],
                         func=AF.Copy)
            return a.activation(out=t_dthm[:, 512:1024], in_=pb[5][0:16, :],
                                func=AF.Copy)
        m_dt0 = st("scalar", _evdt, after=[m_mmdt], sem=s_a)

        def _sp1(v):
            return v.tensor_scalar(out=t_dthm[:], in0=t_dthm[:],
                                   scalar1=t_dtbcol[0:16, 0:1], scalar2=None,
                                   op0=ALU.add)
        m_sp1 = st("vector", _sp1, after=[m_dt0, m_const], sem=s_v)

        def _sp2(a):
            a.activation(out=t_ldthm[:], in_=t_dthm[:], func=AF.Abs)
            a.drain()
            a.activation(out=t_ldthm[:], in_=t_ldthm[:],
                         func=AF.Exp, scale=-1.0)
            a.drain()
            return a.activation(out=t_ldthm[:], in_=t_ldthm[:],
                                func=AF.Ln, bias=1.0)
        m_sp2 = st("scalar", _sp2, after=[m_sp1], sem=s_a)

        def _sp3(v):
            v.tensor_scalar(out=t_junk[0:16, 0:1024], in0=t_dthm[:],
                            scalar1=0.0, scalar2=None, op0=ALU.max)
            v.drain()
            return v.tensor_tensor(out=t_dthm[:], in0=t_junk[0:16, 0:1024],
                                   in1=t_ldthm[:], op=ALU.add)
        m_sp3 = st("vector", _sp3, after=[m_sp2], sem=s_v)

        m_ldt = st("scalar", lambda a: a.activation(out=t_ldthm[:],
                                                    in_=t_dthm[:], func=AF.Ln),
                   after=[m_sp3], sem=s_a)

        def _cum(v):
            v.tensor_scalar(out=t_junk[0:16, 0:1024], in0=t_dthm[:],
                            scalar1=t_acol[0:16, 0:1], scalar2=None,
                            op0=ALU.mult)
            v.drain()
            m = None
            for c in range(4):
                m = v.tensor_tensor_scan(
                    out=t_cumhm[:, 256 * c:256 * (c + 1)],
                    data0=t_junk[0:16, 256 * c:256 * (c + 1)],
                    data1=t_z16[:, 0:256],
                    initial=0.0, op0=ALU.add, op1=ALU.bypass)
            return m
        m_cum = st("vector", _cum, after=[m_ldt, m_acol, m_nwz], sem=s_v)

        def _whm(v):
            v.tensor_tensor(out=t_bias_hm[:], in0=t_ldthm[:], in1=t_cumhm[:],
                            op=ALU.subtract)
            v.drain()
            m = None
            for c in range(4):
                m = v.tensor_scalar(
                    out=t_whm[:, 256 * c:256 * (c + 1)],
                    in0=t_bias_hm[:, 256 * c:256 * (c + 1)],
                    scalar1=t_cumhm[:, 256 * c + 255:256 * c + 256],
                    scalar2=None, op0=ALU.add)
            return m
        m_whm0 = st("vector", _whm, after=[m_cum], sem=s_v)
        m_whm = st("scalar", lambda a: a.activation(out=t_whm[:], in_=t_whm[:],
                                                    func=AF.Exp),
                   after=[m_whm0], sem=s_a)

        m_ph1 = m_whm
        m_tail = m_ph1

        if debug:
            d_u1 = b.ddbg("d_u1", [16 * 128, 3 + T], BF16)
            d_sg = b.ddbg("d_sg", [16 * 128, T], BF16)
            d_xbc = b.ddbg("d_xbc", [24 * 128, T], BF16)
            d_dtq = b.ddbg("d_dtq", [64, T])

            def _dbg1(sy):
                m = None
                for k in range(16):
                    m = sy.dma_start(out=d_u1[128 * k:128 * (k + 1), :],
                                     in_=t_u1[k][:])
                    s_st.inc(m, 16)
                for k in range(16):
                    m = sy.dma_start(out=d_sg[128 * k:128 * (k + 1), :],
                                     in_=t_sg[k][:])
                    s_st.inc(m, 16)
                for cck in range(24):
                    m = sy.dma_start(out=d_xbc[128 * cck:128 * (cck + 1), :],
                                     in_=t_xbc[cck][:])
                    s_st.inc(m, 16)
                for nm, tt in [(0, t_dthm), (1, t_ldthm), (2, t_cumhm),
                               (3, t_whm)]:
                    m = sy.dma_start(out=d_dtq[16 * nm:16 * (nm + 1), :],
                                     in_=tt[:])
                    s_st.inc(m, 16)
                sy.wait_ge(s_st.h, s_st.n)
                return None
            st("sync", _dbg1, after=[m_ph1, ev_ct[15], m_u1all])
            m_ph1 = st("gpsimd", lambda g: g.engine_nop(),
                       after=[m_ph1, (s_st, s_st.n)], sem=s_g)
            m_tail = m_ph1

        esP1.close()

    if enabled("ph2"):
        esS = ExitStack()
        t_xct = [b.sb(f"t_xct{i}", [128, 128], BF16, scope=esS)
                 for i in range(8)]
        t_bt = [b.sb(f"t_bt{i}", [128, 128], BF16, scope=esS)
                for i in range(8)]
        t_bw = b.sb("t_bw", [128, 512], BF16, scope=esS)
        t_Sb = b.sb("t_Sb", [128, 256], BF16, scope=esS)
        t_e2 = b.sb("t_e2", [128, 512], BF16, scope=esS)
        t_ep = [b.sb(f"t_ep{j}", [128, 512], scope=esS) for j in range(2)]
        t_et = b.sb("t_et", [128, 1024], scope=esS)
        t_mt = [b.sb(f"t_mt{j}", [128, 512], BF16, scope=esS)
                for j in range(2)]
        t_ce = b.sb("t_ce", [128, 512], BF16, scope=esS)
        t_xd = b.sb("t_xd", [128, 256], scope=esS)
        t_y = b.sb("t_y", [128, 256], scope=esS)
        t_g = [b.sb(f"t_g{h}", [128, 256], BF16, scope=esS)
               for h in range(NH)]
        t_sq = b.sb("t_sq", [128, 256], scope=esS)
        t_rhm = b.sb("t_rhm", [16, 256], scope=esS)
        t_biastm = [b.sb(f"t_biastm{j}", [128, 16], scope=esS)
                    for j in range(2)]
        t_wtm = [b.sb(f"t_wtm{j}", [128, 16], scope=esS) for j in range(2)]
        t_cdb = b.sb("t_cdb", [128, 16], scope=esS)
        t_cumlr = b.sb("t_cumlr", [1, 16], scope=esS)
        t_gn = t_sg
        t_dI = [b.sb(f"t_dI{h}", [128, 128], BF16, scope=esS)
                for h in range(NH)]

        def _dib(v):
            m = None
            for h in range(NH):
                m = v.tensor_scalar(out=t_dI[h][:], in0=t_identb[:],
                                    scalar1=t_db[:, h:h + 1], scalar2=None,
                                    op0=ALU.mult)
            return m
        m_dI = st("vector", _dib, after=[m_const, m_db, m_tail], sem=s_v)

        if enabled("pre"):
            m_sz = m_tail
        else:
            def _sz(g):
                for h in range(NH):
                    g.memset(t_S[h][:], 0.0)
                    g.drain()
                return g.engine_nop()
            m_sz = st("gpsimd", _sz, after=[m_tail], sem=s_g)

        m_prev_y = m_tail
        m_prev_mt = m_tail
        m_prev_bc = [m_tail, m_tail]
        m_prev_sc = m_tail
        m_prev_gn = m_tail
        m_prev_gne = m_tail
        m_tr_free = m_tail
        for c in range(4):
            c0 = 256 * c
            def _tr1(t, c0=c0):
                m = None
                for g_ in range(4):
                    for jc in range(2):
                        m = t.transpose(
                            ptb[:, 128 * (2 * g_ + jc):128 * (2 * g_ + jc + 1)],
                            t_xbc[g_][:, c0 + 128 * jc:c0 + 128 * (jc + 1)],
                            t_identb[:])
                return m
            m_t1 = st("tensor", _tr1, after=[m_tail, m_prev_y, m_tr_free,
                                             m_prev_gn], sem=s_pe)

            def _te1(v):
                m = None
                for i in range(8):
                    m = v.tensor_copy(out=t_xct[i][:],
                                      in_=ptb[:, 128 * i:128 * (i + 1)])
                return m
            m_e1 = st("vector", _te1, after=[m_t1], sem=s_v)

            def _tr2(t, c0=c0):
                m = None
                for g_ in range(4):
                    for jc in range(2):
                        m = t.transpose(
                            ptb[:, 128 * (2 * g_ + jc):128 * (2 * g_ + jc + 1)],
                            t_xbc[4 + g_][:, c0 + 128 * jc:c0 + 128 * (jc + 1)],
                            t_identb[:])
                return m
            m_t2 = st("tensor", _tr2, after=[m_e1], sem=s_pe)

            def _te2(v):
                m = None
                for i in range(8):
                    m = v.tensor_copy(out=t_bt[i][:],
                                      in_=ptb[:, 128 * i:128 * (i + 1)])
                return m
            m_e2m = st("vector", _te2, after=[m_t2], sem=s_v)

            def _tr3(t, c0=c0):
                t.transpose(pt32[:, 0:16],
                            t_bias_hm[0:16, c0:c0 + 128], t_identf[0:16, 0:16])
                t.transpose(pt32[:, 16:32],
                            t_bias_hm[0:16, c0 + 128:c0 + 256],
                            t_identf[0:16, 0:16])
                t.transpose(pt32[:, 32:48],
                            t_whm[0:16, c0:c0 + 128], t_identf[0:16, 0:16])
                t.transpose(pt32[:, 48:64],
                            t_whm[0:16, c0 + 128:c0 + 256],
                            t_identf[0:16, 0:16])
                return t.transpose(pt32[0:1, 64:80],
                                   t_cumhm[0:16, c0 + 255:c0 + 256],
                                   t_identf[0:16, 0:16])
            m_t3 = st("tensor", _tr3, after=[m_tail, m_e2m], sem=s_pe)

            def _te3(v):
                v.tensor_copy(out=t_biastm[0][:], in_=pt32[:, 0:16])
                v.tensor_copy(out=t_biastm[1][:], in_=pt32[:, 16:32])
                v.tensor_copy(out=t_wtm[0][:], in_=pt32[:, 32:48])
                v.tensor_copy(out=t_wtm[1][:], in_=pt32[:, 48:64])
                return v.tensor_copy(out=t_cumlr[:], in_=pt32[0:1, 64:80])
            m_e3 = st("vector", _te3, after=[m_t3], sem=s_v)

            m_cdb0 = st("tensor", lambda t: t.matmul(
                pt32[:, 64:80], t_ones_f[:], t_cumlr[:],
                start=True, stop=True), after=[m_e3], sem=s_pe)
            m_cdb = st("scalar", lambda a: a.activation(
                out=t_cdb[:], in_=pt32[:, 64:80], func=AF.Exp),
                after=[m_cdb0], sem=s_a)

            m_ss = m_prev_gn
            m_mtm_p = [m_prev_mt, m_prev_mt]
            m_bc_p = [m_prev_bc[0], m_prev_bc[1]]
            m_y_p = [m_prev_y, m_prev_y]
            m_sc_p = [m_prev_sc, m_prev_sc]
            m_et_prev = m_prev_bc[0]
            m_e2_prev = [m_prev_bc[0], m_prev_bc[1]]
            for h in range(NH):
                g_ = h // 4
                par = h % 2
                pMt = pb[par]
                pBc = pb[2 + par]
                pYS = pb[4 + par]
                yo = 0
                so = 256

                def _mt(t, h=h, g_=g_, c0=c0, pMt=pMt):
                    m = None
                    for jc in range(2):
                        m = t.matmul(
                            pMt[:, 256 * jc:256 * (jc + 1)],
                            t_xbc[4 + g_][:, c0 + 128 * jc:c0 + 128 * (jc + 1)],
                            t_xbc[8 + h][:, c0:c0 + 256],
                            start=True, stop=True)
                    return m
                m_mtp = st("tensor", _mt, after=[m_mtm_p[par], m_e2m],
                           sem=s_pe)

                m_bcp = st("tensor", lambda t, h=h, c0=c0, pBc=pBc: t.matmul(
                    pBc[:, 0:256], t_onesel[0:16, 128 * h:128 * (h + 1)],
                    t_cumhm[0:16, c0:c0 + 256], start=True, stop=True),
                    after=[m_bc_p[par], m_e3], sem=s_pe)

                m_e2b = st("scalar", lambda a, pBc=pBc, par=par: a.activation(
                    out=t_e2[:, 256 * par:256 * (par + 1)], in_=pBc[:, 0:256],
                    func=AF.Exp),
                    after=[m_bcp, m_e2_prev[par], m_mtm_p[par]], sem=s_a)
                m_e2_prev[par] = m_e2b

                def _ep(v, pBc=pBc, par=par):
                    v.tensor_tensor(out=t_ep[par][:, 0:256],
                                    in0=pBc[:, 0:256],
                                    in1=t_masklog[:, 0:256], op=ALU.add)
                    return v.tensor_tensor(out=t_ep[par][:, 256:512],
                                           in0=pBc[:, 0:256],
                                           in1=t_masklog[:, 256:512],
                                           op=ALU.add)
                m_ep = st("vector", _ep, after=[m_bcp], sem=s_v)
                m_bc_p[par] = st("gpsimd", lambda g: g.engine_nop(),
                                 after=[m_ep, m_e2b], sem=s_g)

                def _et(a, h=h, par=par):
                    a.activation(out=t_et[:, 512 * par:512 * par + 256],
                                 in_=t_ep[par][:, 0:256],
                                 func=AF.Exp, bias=t_biastm[0][:, h:h + 1])
                    return a.activation(
                        out=t_et[:, 512 * par + 256:512 * (par + 1)],
                        in_=t_ep[par][:, 256:512],
                        func=AF.Exp, bias=t_biastm[1][:, h:h + 1])
                m_et = st("scalar", _et,
                          after=[m_ep, m_et_prev, m_mtm_p[par]], sem=s_a)
                m_et_prev = m_et

                def _v2(v, h=h, g_=g_, c0=c0, par=par, pMt=pMt):
                    v.tensor_tensor(out=t_mt[0][:, 256 * par:256 * (par + 1)],
                                    in0=pMt[:, 0:256],
                                    in1=t_et[:, 512 * par:512 * par + 256],
                                    op=ALU.mult)
                    v.tensor_tensor(out=t_mt[1][:, 256 * par:256 * (par + 1)],
                                    in0=pMt[:, 256:512],
                                    in1=t_et[:, 512 * par + 256:512 * (par + 1)],
                                    op=ALU.mult)
                    v.tensor_tensor(out=t_ce[:, 256 * par:256 * (par + 1)],
                                    in0=t_xbc[8 + h][:, c0:c0 + 256],
                                    in1=t_e2[:, 256 * par:256 * (par + 1)],
                                    op=ALU.mult)
                    v.tensor_copy(out=t_Sb[:, 128 * par:128 * (par + 1)],
                                  in_=t_S[h][:])
                    v.tensor_scalar(out=t_bw[:, 256 * par:256 * par + 128],
                                    in0=t_bt[2 * g_][:],
                                    scalar1=t_wtm[0][:, h:h + 1], scalar2=None,
                                    op0=ALU.mult)
                    return v.tensor_scalar(
                        out=t_bw[:, 256 * par + 128:256 * (par + 1)],
                        in0=t_bt[2 * g_ + 1][:],
                        scalar1=t_wtm[1][:, h:h + 1], scalar2=None,
                        op0=ALU.mult)
                aft_v2 = [m_et, m_mtp, m_e2b, m_mtm_p[par], m_sc_p[par]]
                if c == 0 and h < 2:
                    aft_v2.append(m_sz)
                m_v2 = st("vector", _v2, after=aft_v2, sem=s_v)
                m_mtm_p[par] = m_v2

                def _y(t, h=h, g_=g_, c0=c0, par=par, pYS=pYS, yo=yo, so=so):
                    t.matmul(pYS[:, yo:yo + 256], t_xct[2 * g_][:],
                             t_mt[0][:, 256 * par:256 * (par + 1)],
                             start=True, stop=False)
                    t.matmul(pYS[:, yo:yo + 256], t_xct[2 * g_ + 1][:],
                             t_mt[1][:, 256 * par:256 * (par + 1)],
                             start=False, stop=False)
                    t.matmul(pYS[:, yo:yo + 256],
                             t_Sb[:, 128 * par:128 * (par + 1)],
                             t_ce[:, 256 * par:256 * (par + 1)],
                             start=False, stop=False)
                    t.matmul(pYS[:, yo:yo + 256], t_dI[h][:],
                             t_xbc[g_][:, c0:c0 + 256],
                             start=False, stop=True)
                    t.matmul(pYS[:, so:so + 128],
                             t_bw[:, 256 * par:256 * par + 128],
                             t_xct[2 * g_][:], start=True, stop=False)
                    return t.matmul(pYS[:, so:so + 128],
                                    t_bw[:, 256 * par + 128:256 * (par + 1)],
                                    t_xct[2 * g_ + 1][:], start=False,
                                    stop=True)
                m_yp = st("tensor", _y,
                          after=[m_v2, m_y_p[par], m_sc_p[par], m_e1] +
                          ([m_dI] if (c == 0 and h == 0) else []), sem=s_pe)

                def _v3(v, h=h, g_=g_, c0=c0, par=par, pYS=pYS, yo=yo,
                        so=so):
                    v.tensor_tensor(out=t_g[h][:], in0=pYS[:, yo:yo + 256],
                                    in1=t_sg[h][:, c0:c0 + 256],
                                    op=ALU.mult)
                    v.scalar_tensor_tensor(
                        out=t_S[h][:], in0=t_S[h][:],
                        scalar=t_cdb[:, h:h + 1], in1=pYS[:, so:so + 128],
                        op0=ALU.mult, op1=ALU.add)
                    v.drain()
                    return v.tensor_tensor(out=t_sq[:], in0=t_g[h][:],
                                           in1=t_g[h][:], op=ALU.mult)
                m_v3 = st("vector", _v3, after=[m_yp, m_cdb, m_ss], sem=s_v)
                m_y_p[par] = m_v3
                m_sc_p[par] = m_v3

                m_ss = st("tensor", lambda t, h=h: t.matmul(
                    pt32[0:16, 256:512], t_oco[:, 16 * h:16 * (h + 1)],
                    t_sq[:], start=(h == 0), stop=(h == 15)),
                    after=[m_v3] + ([m_prev_gn] if h == 0 else []),
                    sem=s_pe)
            m_prev_mt = m_mtm_p[1]
            m_prev_bc = m_bc_p
            m_prev_y = m_y_p[1]
            m_prev_sc = m_sc_p[1]
            m_bw = m_v2
            m_tr_free = m_bw

            def _rhm(a):
                a.activation(out=t_rhm[:], in_=pt32[0:16, 256:512],
                             func=AF.Ln, scale=1.0 / 128.0,
                             bias=t_eps[0:16, 0:1])
                a.drain()
                return a.activation(out=t_rhm[:], in_=t_rhm[:],
                                    func=AF.Exp, scale=-0.5)
            m_rhm = st("scalar", _rhm, after=[m_ss], sem=s_a)

            m_gne_p = [m_prev_gne, m_prev_gne]
            for h in range(NH):
                par = h % 2
                m_gnp = st("tensor", lambda t, h=h, par=par: t.matmul(
                    pb[par][:, 0:256], t_nwsel[0:16, 128 * h:128 * (h + 1)],
                    t_rhm[0:16, 0:256], start=True, stop=True),
                    after=[m_rhm, m_gne_p[par], m_prev_mt], sem=s_pe)
                m_gne = st("vector", lambda v, h=h, c0=c0, par=par:
                           v.tensor_tensor(
                               out=t_gn[h][:, c0:c0 + 256], in0=t_g[h][:],
                               in1=pb[par][:, 0:256], op=ALU.mult),
                           after=[m_gnp], sem=s_v)
                m_gne_p[par] = m_gne
            m_prev_gne = st("gpsimd", lambda g: g.engine_nop(),
                            after=[m_gne_p[0], m_gne_p[1]], sem=s_g)
            m_prev_gn = m_prev_gne
            m_prev_mt = m_prev_gne
            m_prev_bc = [m_prev_gne, m_prev_gne]

        esS.close()
        m_ph2 = m_prev_gne
        m_tail = m_ph2

        if debug:
            d_gn = b.ddbg("d_gn", [16 * 128, T], BF16)
            d_S = b.ddbg("d_S", [16 * 128, 128])

            def _dbg2(sy):
                for k in range(16):
                    s_st.inc(sy.dma_start(
                        out=d_gn[128 * k:128 * (k + 1), :], in_=t_gn[k][:]),
                        16)
                for h in range(16):
                    s_st.inc(sy.dma_start(
                        out=d_S[128 * h:128 * (h + 1), :], in_=t_S[h][:]), 16)
                sy.wait_ge(s_st.h, s_st.n)
                return None
            st("sync", _dbg2, after=[m_ph2, m_prev_sc])

    if enabled("ph3"):
        esA2.close()
        es3 = ExitStack()
        t_wo = [b.sb(f"t_wo{k}", [128, HID], BF16, scope=es3)
                for k in range(16)]
        t_x3 = [b.sb(f"t_x3_{i}", [128, HID], scope=es3) for i in range(2)]
        t_ev3 = b.sb("t_ev3", [128, 1024], scope=es3)
        t_gar = b.sb("t_gar", [1, HID], BF16, scope=es3)
        t_gaj = b.sb("t_gaj", [1, HID], scope=es3)
        t_gab = b.sb("t_gab", [128, HID], BF16, scope=es3)
        s_wo = b.sem("s_wo")
        s_x3 = [b.sem("s_x3_0"), b.sem("s_x3_1")]
        s_xss = [b.sem("s_xss0"), b.sem("s_xss1")]

        st("sync", lambda sy: sy.dma_start(
            out=t_gaj[:], in_=modsc[0:1, 2 * 2048:3 * 2048]),
           after=[m_tail], sem=s_hd, k=16)
        m_gl = s_hd.mark()
        def _gs(a):
            a.activation(out=t_gar[:], in_=t_gaj[:], func=AF.Sign)
            return a.activation(out=t_gaj[:], in_=t_gaj[:], func=AF.Abs)
        m_gs = st("scalar", _gs, after=[m_gl], sem=s_a)

        def _gm(v):
            v.tensor_scalar(out=t_gaj[:], in0=t_gaj[:], scalar1=0.25,
                            scalar2=None, op0=ALU.max)
            v.drain()
            return v.tensor_tensor(out=t_gar[:], in0=t_gaj[:], in1=t_gar[:],
                                   op=ALU.mult)
        m_gm = st("vector", _gm, after=[m_gs], sem=s_v)
        m_gab3 = bcast_row(t_gar, t_gab, HID, True, [m_gm, m_tail])

        for k in range(16):
            st("gpsimd", lambda g, k=k: g.dma_start(
                out=t_wo[k][:], in_=w_out[128 * k:128 * (k + 1), :]),
               after=([m_tail] if k == 0 else []), sem=s_wo, k=16)
        m_wol = s_wo.mark()

        m_ev_prev = [m_gab3, m_gab3]
        m_xs_done = []
        for tc in range(8):
            sl = tc % 2
            aft = [m_tail] if tc < 2 else [m_xs_done[tc - 2]]
            st("sync", lambda sy, tc=tc, sl=sl: sy.dma_start(
                out=t_x3[sl][:], in_=xe[3 + 128 * tc:3 + 128 * (tc + 1), :]),
               after=aft, sem=s_x3[sl], k=16)
            m_xl = (s_x3[sl], s_x3[sl].n)
            for half in range(2):
                pA, pB = (pb[0], pb[1]) if half == 0 else (pb[2], pb[3])

                def _wo(t, tc=tc, half=half, pA=pA, pB=pB):
                    m = None
                    for k in range(16):
                        lh = t_gn[k][:, 128 * tc:128 * (tc + 1)]
                        m = t.matmul(pA[:, :], lh,
                                     t_wo[k][:, 1024 * half:1024 * half + 512],
                                     start=(k == 0), stop=(k == 15))
                        m = t.matmul(pB[:, :], lh,
                                     t_wo[k][:, 1024 * half + 512:
                                             1024 * (half + 1)],
                                     start=(k == 0), stop=(k == 15))
                    return m
                m_mm = st("tensor", _wo,
                          after=[m_wol, m_tail, m_ev_prev[half]], sem=s_pe)

                def _ev(v, tc=tc, half=half, pA=pA, pB=pB, sl=sl):
                    m = None
                    for j, pX in enumerate([pA, pB]):
                        off = 1024 * half + 512 * j
                        v.tensor_tensor(out=t_ev3[:, 512 * j:512 * (j + 1)],
                                        in0=pX[:, :],
                                        in1=t_gab[:, off:off + 512],
                                        op=ALU.mult)
                    v.drain()
                    for j in range(2):
                        off = 1024 * half + 512 * j
                        m = v.tensor_tensor(out=t_x3[sl][:, off:off + 512],
                                            in0=t_x3[sl][:, off:off + 512],
                                            in1=t_ev3[:, 512 * j:512 * (j + 1)],
                                            op=ALU.add)
                    return m
                m_ev_prev[half] = st("vector", _ev,
                                     after=[m_mm, m_xl, m_gab3], sem=s_v)
            st("sync", lambda sy, tc=tc, sl=sl: sy.dma_start(
                out=xsc[128 * tc:128 * (tc + 1), :], in_=t_x3[sl][:]),
               after=[m_ev_prev[0], m_ev_prev[1]], sem=s_xss[sl], k=16)
            m_xs_done.append((s_xss[sl], s_xss[sl].n))

        m_ph3 = st("gpsimd", lambda g: g.engine_nop(),
                   after=[(s_xss[0], s_xss[0].n), (s_xss[1], s_xss[1].n),
                          m_ev_prev[0], m_ev_prev[1]], sem=s_g)
        m_tail = m_ph3

        if debug:
            d_x3 = b.ddbg("d_x3", [T, HID])

            def _dbg3(sy):
                m = sy.dma_start(out=d_x3, in_=xsc)
                s_st.inc(m, 16)
                sy.wait_ge(s_st.h, s_st.n)
                return None
            st("sync", _dbg3, after=[m_ph3])
            m_ph3 = st("gpsimd", lambda g: g.engine_nop(),
                       after=[m_ph3, (s_st, s_st.n)], sem=s_g)
            m_tail = m_ph3

        es3.close()
        esA.close()

    if enabled("ph4"):
        esL.close()
        tern_g = b.dint("tern_g", [64, 128, 2048], BF16)
        tern_v = b.dint("tern_v", [64, 128, 2048], BF16)
        tern_o = b.dint("tern_o", [16, 4, 128, 2048], BF16)
        es4 = ExitStack()
        t_gbb4 = b.sb("t_gbb4", [128, HID], BF16, scope=es4)
        t_r4 = b.sb("t_r4", [1, HID], scope=es4)
        t_amx = b.sb("t_amx", [128, 512], scope=es4)
        t_amn = b.sb("t_amn", [128, 512], scope=es4)
        t_amc = b.sb("t_amc", [128, 8], scope=es4)
        t_srow = b.sb("t_srow", [1, 1024], scope=es4)
        t_s1b = b.sb("t_s1b", [128, 512], scope=es4)
        t_s1ib = b.sb("t_s1ib", [128, 512], scope=es4)
        t_s2b = t_s1b
        t_s2ib = t_s1ib
        t_qq = b.sb("t_qq", [128, 512], BF16, scope=es4)
        t_vv = b.sb("t_vv", [128, 512], BF16, scope=es4)
        t_e1 = t_s1b
        t_stg = [b.sb(f"t_stg{i}", [128, 2048], scope=es4) for i in range(4)]
        t_q3 = [b.sb(f"t_q3_{i}", [128, 2048], BF16, scope=es4)
                for i in range(4)]
        t_xdq = [b.sb(f"t_xdq{k}", [128, 512], BF16, scope=es4)
                 for k in range(16)]
        s_stg = [b.sem(f"s_stg{i}") for i in range(4)]
        s_tn = [b.sem(f"s_tn{i}") for i in range(4)]
        s_x4 = [b.sem(f"s_x4_{i}") for i in range(4)]
        s_sto = b.sem("s_sto")

        t_ob4 = b.sb("t_ob4", [1, HID], scope=es4)
        st("sync", lambda sy: sy.dma_start(
            out=t_r4[:], in_=modsc[0:1, 5 * 2048:6 * 2048]),
           after=[m_tail], sem=s_hd, k=16)
        st("sync", lambda sy: sy.dma_start(out=t_ob4[:], in_=obrow),
           sem=s_hd, k=16)
        m_r4 = None

        st("sync", lambda sy: sy.dma_start(
            out=t_gbgc[:],
            in_=modsc[0:1, 5 * 2048:6 * 2048].rearrange(
                "a (k p) -> p a k", p=128).rearrange("p a k -> (p a) k"),
            allow_slow_non_contiguous=True),
           after=[m_tail], sem=s_hd, k=16)
        m_gbc0 = s_hd.mark()
        m_r4 = m_gbc0

        def _gb4(a):
            a.activation(out=t_gbb4[0:1, 0:HID], in_=t_r4[:], func=AF.Sign)
            return a.activation(out=t_r4[:], in_=t_r4[:], func=AF.Abs)
        m_gb4a = st("scalar", _gb4, after=[m_r4], sem=s_a)

        def _gb4b(v):
            v.tensor_scalar(out=t_r4[:], in0=t_r4[:], scalar1=0.25,
                            scalar2=None, op0=ALU.max)
            v.drain()
            v.tensor_tensor(out=t_r4[:], in0=t_r4[:],
                            in1=t_gbb4[0:1, 0:HID], op=ALU.mult)
            v.drain()
            return v.tensor_tensor(out=t_r4[:], in0=t_r4[:], in1=t_ob4[:],
                                   op=ALU.mult)
        m_gb4b = st("vector", _gb4b, after=[m_gb4a], sem=s_v)
        m_gbb4 = bcast_row(t_r4, t_gbb4, HID, False, [m_gb4b, m_tail])


        m_gbc1a = st("scalar", lambda a: a.activation(
            out=t_col[:, 0:16], in_=t_gbgc[:], func=AF.Abs),
            after=[m_gbc0], sem=s_a)
        m_gbc2 = st("scalar", lambda a: a.activation(
            out=t_gbgc[:], in_=t_gbgc[:], func=AF.Sign),
            after=[m_gbc1a], sem=s_a)

        def _gbc3(v):
            v.tensor_scalar(out=t_col[:, 0:16], in0=t_col[:, 0:16],
                            scalar1=0.25, scalar2=None, op0=ALU.max)
            v.drain()
            v.tensor_tensor(out=t_gbgc[:], in0=t_gbgc[:], in1=t_col[:, 0:16],
                            op=ALU.mult)
            v.drain()
            return v.tensor_scalar(out=t_gbgc[:], in0=t_gbgc[:],
                                   scalar1=t_gamma[:, 10:11], scalar2=None,
                                   op0=ALU.mult)
        m_gbgc = st("vector", _gbc3, after=[m_gbc2, m_gamt], sem=s_v)

        m_tg_done = m_gbb4
        for tg in range(2):
            tg0 = 512 * tg
            es4a = ExitStack()
            t_h2 = [b.sb(f"t_h2_{tg}_{k}", [128, 512], scope=es4a)
                    for k in range(16)]
            t_x4a = [b.sb(f"t_x4a{tg}_{i}", [128, HID], scope=es4a)
                     for i in range(2)]
            t_ut4 = b.sb(f"t_ut4_{tg}", [128, HID], BF16, scope=es4a)

            m_u4 = [None] * 4
            m_tw4 = [None] * 4
            for i in range(4):
                sl = i % 2
                aft_l = [m_tg_done] if i < 2 else [m_u4[i - 2]]
                st("sync", lambda sy, i=i, sl=sl, tg0=tg0: sy.dma_start(
                    out=t_x4a[sl][:],
                    in_=xsc[tg0 + 128 * i:tg0 + 128 * (i + 1), :]),
                   after=aft_l, sem=s_x4[sl], k=16)
                m_xl = (s_x4[sl], s_x4[sl].n)
                aft_x = [m_xl] + ([m_tw4[i - 1]] if i >= 1 else [])
                m_u4[i] = emit_ln(t_x4a[sl], t_ut4, 128, aft_x)
                prev_w = None
                for w in range(2):
                    def _tw(t, w=w):
                        m = None
                        for kk in range(8):
                            k = 8 * w + kk
                            m = t.transpose(
                                ptb[:, 128 * kk:128 * (kk + 1)],
                                t_ut4[:, 128 * k:128 * (k + 1)],
                                t_identb[:])
                        return m
                    aft = [m_u4[i]] + ([prev_w] if prev_w else
                                       ([m_tw4[i - 1]] if i >= 1 else
                                        [m_tg_done]))
                    m_t = st("tensor", _tw, after=aft, sem=s_pe)

                    def _ev(v, i=i, w=w):
                        m = None
                        for kk in range(8):
                            k = 8 * w + kk
                            m = v.tensor_scalar(
                                out=t_h2[k][:, 128 * i:128 * (i + 1)],
                                in0=ptb[:, 128 * kk:128 * (kk + 1)],
                                scalar1=t_modc[:, 48 + k:49 + k],
                                scalar2=t_modc[:, 32 + k:33 + k],
                                op0=ALU.mult, op1=ALU.add)
                            v.drain()
                        return m
                    prev_w = st("vector", _ev, after=[m_t, m_modc1], sem=s_v)
                m_tw4[i] = prev_w
            m_h2 = m_tw4[3]

            def _tree(v):
                v.tensor_copy(out=t_amx[:], in_=t_h2[0][:])
                v.drain()
                v.tensor_copy(out=t_amn[:], in_=t_h2[0][:])
                v.drain()
                for k in range(1, 16):
                    v.tensor_tensor(out=t_amx[:], in0=t_amx[:],
                                    in1=t_h2[k][:], op=ALU.max)
                    v.drain()
                    v.tensor_tensor(out=t_amn[:], in0=t_amn[:],
                                    in1=t_h2[k][:], op=ALU.min)
                    v.drain()
                v.tensor_scalar(out=t_amn[:], in0=t_amn[:], scalar1=-1.0,
                                scalar2=None, op0=ALU.mult)
                v.drain()
                v.tensor_tensor(out=t_amx[:], in0=t_amx[:], in1=t_amn[:],
                                op=ALU.max)
                v.drain()
                return v.engine_nop()
            m_tree = st("vector", _tree, after=[m_h2], sem=s_v)

            def _tp1(t):
                m = None
                for blk in range(4):
                    m = t.transpose(pt32[:, 128 * blk:128 * (blk + 1)],
                                    t_amx[:, 128 * blk:128 * (blk + 1)],
                                    t_identf[:])
                return m
            m_tp1 = st("tensor", _tp1, after=[m_tree, m_tg_done], sem=s_pe)

            def _red(v):
                for blk in range(4):
                    v.tensor_reduce(out=t_amc[:, blk:blk + 1],
                                    in_=pt32[:, 128 * blk:128 * (blk + 1)],
                                    axis=AXX, op=ALU.max)
                v.drain()
                v.tensor_scalar(out=t_amc[:, 0:4], in0=t_amc[:, 0:4],
                                scalar1=1e-5, scalar2=None, op0=ALU.max)
                v.drain()
                return v.reciprocal(out=t_amc[:, 4:8], in_=t_amc[:, 0:4])
            m_red = st("vector", _red, after=[m_tp1], sem=s_v)

            def _tp2(t):
                m = None
                for blk in range(4):
                    m = t.transpose(pt32[0:1, 128 * blk:128 * (blk + 1)],
                                    t_amc[:, 4 + blk:5 + blk], t_identf[:])
                return m
            m_tp2 = st("tensor", _tp2, after=[m_red], sem=s_pe)
            m_sr1 = st("vector", lambda v: v.tensor_copy(
                out=t_srow[0:1, 0:512], in_=pt32[0:1, 0:512]),
                after=[m_tp2], sem=s_v)

            def _tp3(t):
                m = None
                for blk in range(4):
                    m = t.transpose(pt32[0:1, 128 * blk:128 * (blk + 1)],
                                    t_amc[:, blk:blk + 1], t_identf[:])
                return m
            m_tp3 = st("tensor", _tp3, after=[m_sr1], sem=s_pe)
            m_sr2 = st("vector", lambda v: v.tensor_copy(
                out=t_srow[0:1, 512:1024], in_=pt32[0:1, 0:512]),
                after=[m_tp3], sem=s_v)

            m_b1 = st("tensor", lambda t: t.matmul(
                pt32[:, 0:512], t_ones_f[:], t_srow[0:1, 0:512],
                start=True, stop=True), after=[m_sr2], sem=s_pe)
            m_s1b = st("vector", lambda v: v.tensor_scalar(
                out=t_s1b[:], in0=pt32[:, 0:512], scalar1=127.0, scalar2=None,
                op0=ALU.mult), after=[m_b1], sem=s_v)
            m_b2 = st("tensor", lambda t: t.matmul(
                pt32[:, 0:512], t_ones_f[:], t_srow[0:1, 512:1024],
                start=True, stop=True), after=[m_s1b], sem=s_pe)
            m_s1ib = st("vector", lambda v: v.tensor_scalar(
                out=t_s1ib[:], in0=pt32[:, 0:512], scalar1=1.0 / 127.0,
                scalar2=None, op0=ALU.mult), after=[m_b2], sem=s_v)

            qsc = ((t_amx, t_qq), (t_amn, t_vv))
            m_xdq_all = None
            for kp in range(8):
                def _q(v, kp=kp):
                    for j, (sa, sq) in enumerate(qsc):
                        v.tensor_tensor(out=sa[:], in0=t_h2[2 * kp + j][:],
                                        in1=t_s1b[:], op=ALU.mult)
                    v.drain()
                    for j, (sa, sq) in enumerate(qsc):
                        v.tensor_scalar(out=sq[:], in0=sa[:], scalar1=MAGIC,
                                        scalar2=-MAGIC, op0=ALU.add,
                                        op1=ALU.add)
                    v.drain()
                    m = None
                    for j, (sa, sq) in enumerate(qsc):
                        m = v.tensor_tensor(out=t_xdq[2 * kp + j][:],
                                            in0=sq[:], in1=t_s1ib[:],
                                            op=ALU.mult)
                    return m
                m_xdq_all = st("vector", _q, after=[m_s1b, m_s1ib], sem=s_v)

            es4a.close()
            es4b = ExitStack()
            t_gv = [b.sb(f"t_gv{tg}_{h}", [128, 512], BF16, scope=es4b)
                    for h in range(64)]
            t_x4 = [b.sb(f"t_x4_{tg}_{i}", [128, HID], scope=es4b)
                    for i in range(4)]

            GV = {
                "g": dict(stg=(t_stg[0], t_stg[1]), q3=(t_q3[0], t_q3[1]),
                          src=gwB, tdr=tern_g,
                          ssem=(s_stg[0], s_stg[1]), tsem=(s_tn[0], s_tn[1]),
                          ghi=t_gamma[:, 0:1], glo=t_gamma[:, 4:5]),
                "v": dict(stg=(t_stg[2], t_stg[3]), q3=(t_q3[2], t_q3[3]),
                          src=vwB, tdr=tern_v,
                          ssem=(s_stg[2], s_stg[3]), tsem=(s_tn[2], s_tn[3]),
                          ghi=t_gamma[:, 1:2], glo=t_gamma[:, 5:6]),
            }
            ST = {k: dict(stgfree=[m_xdq_all, m_xdq_all],
                          q3free=[m_xdq_all, m_xdq_all],
                          ld={}, tn={}, wr={}) for k in GV}

            def eload(k, h):
                d, s = GV[k], ST[k]
                sl = h % 2
                st("sync", lambda sy, d=d, h=h, sl=sl: sy.dma_start(
                    out=d["stg"][sl][:], in_=d["src"][h]),
                   after=[s["stgfree"][sl]], sem=d["ssem"][sl], k=16)
                s["ld"][h] = d["ssem"][sl].mark()

            def etern(k, h):
                d, s = GV[k], ST[k]
                sl = h % 2

                def _tq(v, d=d, sl=sl):
                    v.tensor_scalar(out=d["q3"][sl][:], in0=d["stg"][sl][:],
                                    scalar1=d["glo"], scalar2=None,
                                    op0=ALU.is_lt)
                    v.drain()
                    return v.scalar_tensor_tensor(
                        out=d["q3"][sl][:], in0=d["stg"][sl][:],
                        scalar=d["ghi"], in1=d["q3"][sl][:],
                        op0=ALU.is_gt, op1=ALU.subtract)
                m1 = st("vector", _tq,
                        after=[s["ld"][h], s["q3free"][sl]], sem=s_v)
                s["stgfree"][sl] = m1
                s["tn"][h] = m1
                st("sync", lambda sy, d=d, h=h, sl=sl: sy.dma_start(
                    out=d["tdr"][h], in_=d["q3"][sl][:]),
                   after=[m1], sem=d["tsem"][sl], k=16)
                s["wr"][h] = d["tsem"][sl].mark()

            def eload_t(k, h):
                d, s = GV[k], ST[k]
                sl = h % 2
                st("sync", lambda sy, d=d, h=h, sl=sl: sy.dma_start(
                    out=d["q3"][sl][:], in_=d["tdr"][h]),
                   after=[s["q3free"][sl]], sem=d["tsem"][sl], k=16)
                s["tn"][h] = d["tsem"][sl].mark()

            def etern_gv(h):
                sl = h % 2
                dg, dv = GV["g"], GV["v"]

                def _tq(v, sl=sl):
                    v.tensor_scalar(out=dg["q3"][sl][:], in0=dg["stg"][sl][:],
                                    scalar1=dg["glo"], scalar2=None,
                                    op0=ALU.is_lt)
                    v.tensor_scalar(out=dv["q3"][sl][:], in0=dv["stg"][sl][:],
                                    scalar1=dv["glo"], scalar2=None,
                                    op0=ALU.is_lt)
                    v.drain()
                    v.scalar_tensor_tensor(
                        out=dg["q3"][sl][:], in0=dg["stg"][sl][:],
                        scalar=dg["ghi"], in1=dg["q3"][sl][:],
                        op0=ALU.is_gt, op1=ALU.subtract)
                    return v.scalar_tensor_tensor(
                        out=dv["q3"][sl][:], in0=dv["stg"][sl][:],
                        scalar=dv["ghi"], in1=dv["q3"][sl][:],
                        op0=ALU.is_gt, op1=ALU.subtract)
                m1 = st("vector", _tq,
                        after=[ST["g"]["ld"][h], ST["v"]["ld"][h],
                               ST["g"]["q3free"][sl], ST["v"]["q3free"][sl]],
                        sem=s_v)
                for k in ("g", "v"):
                    d, s = GV[k], ST[k]
                    s["stgfree"][sl] = m1
                    s["tn"][h] = m1
                    st("sync", lambda sy, d=d, h=h, sl=sl: sy.dma_start(
                        out=d["tdr"][h], in_=d["q3"][sl][:]),
                       after=[m1], sem=d["tsem"][sl], k=16)
                    s["wr"][h] = d["tsem"][sl].mark()

            m_qqfree = m_xdq_all
            m_vvfree = m_xdq_all
            m_amax = m_xdq_all
            m_evg_prev = [m_xdq_all, m_xdq_all]
            m_evv_prev = [m_xdq_all, m_xdq_all]
            if tg == 0:
                eload("g", 0), eload("v", 0), eload("g", 1), eload("v", 1)
                etern_gv(0)
            else:
                eload_t("g", 0), eload_t("v", 0)
                eload_t("g", 1), eload_t("v", 1)
            for h in range(64):
                gsl = h % 2
                if tg == 0:
                    if h + 2 < 64:
                        eload("g", h + 2)
                        eload("v", h + 2)
                    if h + 1 < 64:
                        etern_gv(h + 1)
                pG = pb[0] if h % 2 == 0 else pb[1]

                def _gmm(t, h=h, pG=pG, gsl=gsl):
                    m = None
                    for cdx in range(16):
                        m = t.matmul(pG[:, :],
                                     t_q3[gsl][:, 128 * cdx:128 * (cdx + 1)],
                                     t_xdq[cdx][:], start=(cdx == 0),
                                     stop=(cdx == 15))
                    return m
                m_gmm = st("tensor", _gmm,
                           after=[ST["g"]["tn"][h], m_evg_prev[h % 2]],
                           sem=s_pe)
                ST["g"]["q3free"][gsl] = (
                    m_gmm if tg == 1 else
                    st("gpsimd", lambda g: g.engine_nop(),
                       after=[m_gmm, ST["g"]["wr"][h]], sem=s_g))
                if tg == 1 and h + 2 < 64:
                    eload_t("g", h + 2)

                m_evg = st("scalar", lambda a, h=h, pG=pG: a.activation(
                    out=t_qq[:], in_=pG[:, :], func=AF.Silu,
                    scale=t_gamma[:, 8:9], bias=t_gb2[:, h:h + 1]),
                    after=[m_gmm, m_gamt, m_qqfree], sem=s_a)
                m_evg_prev[h % 2] = m_evg

                pV = pb[2] if h % 2 == 0 else pb[3]

                def _vmm(t, h=h, pV=pV, gsl=gsl):
                    m = None
                    for cdx in range(16):
                        m = t.matmul(
                            pV[:, :],
                            t_q3[2 + gsl][:, 128 * cdx:128 * (cdx + 1)],
                            t_xdq[cdx][:], start=(cdx == 0), stop=(cdx == 15))
                    return m
                m_vmm = st("tensor", _vmm,
                           after=[ST["v"]["tn"][h], m_evv_prev[h % 2]],
                           sem=s_pe)
                ST["v"]["q3free"][gsl] = (
                    m_vmm if tg == 1 else
                    st("gpsimd", lambda g: g.engine_nop(),
                       after=[m_vmm, ST["v"]["wr"][h]], sem=s_g))
                if tg == 1 and h + 2 < 64:
                    eload_t("v", h + 2)

                m_vv = st("vector", lambda v, h=h, pV=pV: v.tensor_scalar(
                    out=t_vv[:], in0=pV[:, :], scalar1=t_gamma[:, 9:10],
                    scalar2=t_vb2[:, h:h + 1], op0=ALU.mult, op1=ALU.add),
                    after=[m_vmm, m_gamt, m_vvfree], sem=s_v)
                m_gv = st("gpsimd", lambda g, h=h: g.tensor_tensor(
                    out=t_gv[h][:], in0=t_qq[:], in1=t_vv[:], op=ALU.mult),
                    after=[m_vv, m_evg], sem=s_g)

                def _amax(v, h=h):
                    if tg == 1 or h == 63:
                        v.drain()
                    if h == 0:
                        v.tensor_copy(out=t_amx[:], in_=t_gv[0][:])
                        return v.tensor_copy(out=t_amn[:], in_=t_gv[0][:])
                    v.tensor_tensor(out=t_amx[:], in0=t_amx[:],
                                    in1=t_gv[h][:], op=ALU.max)
                    return v.tensor_tensor(out=t_amn[:], in0=t_amn[:],
                                           in1=t_gv[h][:], op=ALU.min)
                m_amax = st("vector", _amax, after=[m_gv], sem=s_v)
                m_evv_prev[h % 2] = m_vv
                m_qqfree = m_gv
                m_vvfree = m_gv
            m_gvdone = m_amax

            def _cmb2(v):
                v.tensor_scalar(out=t_amn[:], in0=t_amn[:], scalar1=-1.0,
                                scalar2=None, op0=ALU.mult)
                v.drain()
                return v.tensor_tensor(out=t_amx[:], in0=t_amx[:],
                                       in1=t_amn[:], op=ALU.max)
            m_gvdone = st("vector", _cmb2, after=[m_gvdone], sem=s_v)
            m_tp1b = st("tensor", lambda t: [
                t.transpose(pt32[:, 128 * blk:128 * (blk + 1)],
                            t_amx[:, 128 * blk:128 * (blk + 1)],
                            t_identf[:]) for blk in range(4)][-1],
                after=[m_gvdone], sem=s_pe)

            def _red2(v):
                for blk in range(4):
                    v.tensor_reduce(out=t_amc[:, blk:blk + 1],
                                    in_=pt32[:, 128 * blk:128 * (blk + 1)],
                                    axis=AXX, op=ALU.max)
                v.drain()
                v.tensor_scalar(out=t_amc[:, 0:4], in0=t_amc[:, 0:4],
                                scalar1=1e-5, scalar2=None, op0=ALU.max)
                v.drain()
                return v.reciprocal(out=t_amc[:, 4:8], in_=t_amc[:, 0:4])
            m_red2 = st("vector", _red2, after=[m_tp1b], sem=s_v)

            m_tp2b = st("tensor", lambda t: [
                t.transpose(pt32[0:1, 128 * blk:128 * (blk + 1)],
                            t_amc[:, 4 + blk:5 + blk], t_identf[:])
                for blk in range(4)][-1], after=[m_red2], sem=s_pe)
            m_sr1b = st("vector", lambda v: v.tensor_copy(
                out=t_srow[0:1, 0:512], in_=pt32[0:1, 0:512]),
                after=[m_tp2b], sem=s_v)
            m_tp3b = st("tensor", lambda t: [
                t.transpose(pt32[0:1, 128 * blk:128 * (blk + 1)],
                            t_amc[:, blk:blk + 1], t_identf[:])
                for blk in range(4)][-1], after=[m_sr1b], sem=s_pe)
            m_sr2b = st("vector", lambda v: v.tensor_copy(
                out=t_srow[0:1, 512:1024], in_=pt32[0:1, 0:512]),
                after=[m_tp3b], sem=s_v)
            m_b1b = st("tensor", lambda t: t.matmul(
                pt32[:, 0:512], t_ones_f[:], t_srow[0:1, 0:512],
                start=True, stop=True), after=[m_sr2b], sem=s_pe)
            m_s2b = st("vector", lambda v: v.tensor_scalar(
                out=t_s2b[:], in0=pt32[:, 0:512], scalar1=127.0, scalar2=None,
                op0=ALU.mult), after=[m_b1b], sem=s_v)
            m_b2b = st("tensor", lambda t: t.matmul(
                pt32[:, 0:512], t_ones_f[:], t_srow[0:1, 512:1024],
                start=True, stop=True), after=[m_s2b], sem=s_pe)
            m_s2ib = st("vector", lambda v: v.tensor_scalar(
                out=t_s2ib[:], in0=pt32[:, 0:512], scalar1=1.0 / 127.0,
                scalar2=None, op0=ALU.mult), after=[m_b2b], sem=s_v)

            m_q2_all = None
            for hp in range(32):
                def _q2(v, hp=hp):
                    for j, (sa, sq) in enumerate(qsc):
                        v.tensor_tensor(out=sa[:], in0=t_gv[2 * hp + j][:],
                                        in1=t_s2b[:], op=ALU.mult)
                    v.drain()
                    for j, (sa, sq) in enumerate(qsc):
                        v.tensor_scalar(out=sq[:], in0=sa[:], scalar1=MAGIC,
                                        scalar2=-MAGIC, op0=ALU.add,
                                        op1=ALU.add)
                    v.drain()
                    m = None
                    for j, (sa, sq) in enumerate(qsc):
                        m = v.tensor_tensor(out=t_gv[2 * hp + j][:],
                                            in0=sq[:], in1=t_s2ib[:],
                                            op=ALU.mult)
                    return m
                m_q2_all = st("vector", _q2, after=[m_s2b, m_s2ib], sem=s_v)

            m_x4r = [None] * 4
            for i in range(4):
                st("sync", lambda sy, i=i, tg0=tg0: sy.dma_start(
                    out=t_x4[i][:],
                    in_=xsc[tg0 + 128 * i:tg0 + 128 * (i + 1), :]),
                   after=[m_xdq_all], sem=s_x4[i], k=16)
                m_xl = (s_x4[i], s_x4[i].n)
                m_x4r[i] = st("vector", lambda v, i=i: v.tensor_tensor(
                    out=t_x4[i][:], in0=t_x4[i][:], in1=t_gbb4[:],
                    op=ALU.add), after=[m_xl, m_gbb4], sem=s_v)

            OD = dict(stgfree=[ST["g"]["stgfree"][0], ST["g"]["stgfree"][1]],
                      q3free=[ST["g"]["q3free"][0], ST["g"]["q3free"][1]],
                      ld={}, tn={}, wr={})

            def eload_o(n):
                sl = n % 2
                do, sub = divmod(n, 4)
                st("sync", lambda sy, do=do, sub=sub, sl=sl: sy.dma_start(
                    out=t_stg[sl][:], in_=owB[do, sub]),
                   after=[OD["stgfree"][sl]], sem=s_stg[sl], k=16)
                OD["ld"][n] = s_stg[sl].mark()

            def etern_o(n):
                sl = n % 2
                do, sub = divmod(n, 4)

                def _tq(v, sl=sl):
                    v.tensor_scalar(out=t_q3[sl][:], in0=t_stg[sl][:],
                                    scalar1=t_gamma[:, 6:7], scalar2=None,
                                    op0=ALU.is_lt)
                    v.drain()
                    return v.scalar_tensor_tensor(
                        out=t_q3[sl][:], in0=t_stg[sl][:],
                        scalar=t_gamma[:, 2:3], in1=t_q3[sl][:],
                        op0=ALU.is_gt, op1=ALU.subtract)
                m1 = st("vector", _tq,
                        after=[OD["ld"][n], OD["q3free"][sl]], sem=s_v)
                OD["stgfree"][sl] = m1
                OD["tn"][n] = m1
                st("sync", lambda sy, do=do, sub=sub, sl=sl: sy.dma_start(
                    out=tern_o[do, sub], in_=t_q3[sl][:]),
                   after=[m1], sem=s_tn[sl], k=16)
                OD["wr"][n] = s_tn[sl].mark()

            def eload_to(n):
                sl = n % 2
                do, sub = divmod(n, 4)
                st("sync", lambda sy, do=do, sub=sub, sl=sl: sy.dma_start(
                    out=t_q3[sl][:], in_=tern_o[do, sub]),
                   after=[OD["q3free"][sl]], sem=s_tn[sl], k=16)
                OD["tn"][n] = s_tn[sl].mark()

            m_evo_prev = [m_q2_all, m_q2_all]
            m_xadd = list(m_x4r)
            if tg == 0:
                eload_o(0), eload_o(1), etern_o(0)
            else:
                eload_to(0), eload_to(1)
            for n in range(64):
                do, sub = divmod(n, 4)
                sl = n % 2
                if tg == 0:
                    if n + 2 < 64:
                        eload_o(n + 2)
                    if n + 1 < 64:
                        etern_o(n + 1)
                pO = pb[4] if do % 2 == 0 else pb[5]

                def _omm(t, sub=sub, sl=sl, pO=pO):
                    m = None
                    for ci in range(16):
                        m = t.matmul(pO[:, :],
                                     t_q3[sl][:, 128 * ci:128 * (ci + 1)],
                                     t_gv[16 * sub + ci][:],
                                     start=(sub == 0 and ci == 0),
                                     stop=(sub == 3 and ci == 15))
                    return m
                aft_mm = [OD["tn"][n], m_q2_all]
                if sub == 0:
                    aft_mm.append(m_evo_prev[do % 2])
                m_om = st("tensor", _omm, after=aft_mm, sem=s_pe)
                OD["q3free"][sl] = (m_om if tg == 1 else
                                    st("gpsimd", lambda g: g.engine_nop(),
                                       after=[m_om, OD["wr"][n]], sem=s_g))
                if tg == 1 and n + 2 < 64:
                    eload_to(n + 2)
                if sub != 3:
                    continue
                m_oall = m_om

                m_e1 = st("vector", lambda v, do=do, pO=pO: v.tensor_scalar(
                    out=t_e1[:], in0=pO[:, :], scalar1=t_gbgc[:, do:do + 1],
                    scalar2=None, op0=ALU.mult),
                    after=[m_oall, m_gbgc], sem=s_v)
                m_evo_prev[do % 2] = m_e1

                m_tp = st("tensor", lambda t: [
                    t.transpose(pt32[:, 128 * blk:128 * (blk + 1)],
                                t_e1[:, 128 * blk:128 * (blk + 1)],
                                t_identf[:]) for blk in range(4)][-1],
                    after=[m_e1, (m_s2ib if do == 0 else None),
                           m_xadd[3]], sem=s_pe)

                def _xa(v, do=do):
                    m = None
                    for i in range(4):
                        m = v.tensor_tensor(
                            out=t_x4[i][:, 128 * do:128 * (do + 1)],
                            in0=t_x4[i][:, 128 * do:128 * (do + 1)],
                            in1=pt32[:, 128 * i:128 * (i + 1)], op=ALU.add)
                    return m
                m_xa = st("vector", _xa, after=[m_tp] + m_xadd, sem=s_v)
                m_xadd = [m_xa] * 4

            m_sts = []
            for i in range(4):
                st("sync", lambda sy, i=i, tg0=tg0: sy.dma_start(
                    out=xo[tg0 + 128 * i:tg0 + 128 * (i + 1), :],
                    in_=t_x4[i][:]), after=[m_xadd[0]], sem=s_sto, k=16)
            m_st4 = (s_sto, s_sto.n)
            m_tg_done = st("gpsimd", lambda g: g.engine_nop(),
                           after=[m_st4, m_xadd[0]], sem=s_g)
            es4b.close()

            if debug and tg == 0:
                d_h2 = b.ddbg("d_h2", [16 * 128, 512])
        es4.close()

        def _fin(sy):
            sy.wait_ge(s_sto.h, s_sto.n)
            return None
        st("sync", _fin, after=[m_tg_done])
        m_tail = m_tg_done

    M = dict(const=m_const, nwsel=m_nwsel, silc=m_silc, db=m_db, modc=m_modc,
             gamt=m_gamt, mrow=m_mrow)

    if upto_i < len(PHASES) - 1 or not enabled("ph1"):
        m_z = st("gpsimd", lambda g: (g.memset(t_junk[:], 0.0), g.drain(),
                                      g.engine_nop())[-1],
                 after=[m_tail], sem=s_g)

        def _zs(sy):
            m = None
            for i in range(8):
                m = sy.dma_start(out=xo[128 * i:128 * (i + 1), :],
                                 in_=t_junk[:])
                s_st.inc(m, 16)
            sy.wait_ge(s_st.h, s_st.n)
            return None
        st("sync", _zs, after=[m_z])

    es.close()
    return nc, b.dbg


def _host_consts():
    ii = np.arange(256)[None, :]
    jj = np.arange(128)[:, None]
    m0 = np.where(ii >= jj, 0.0, -1.0e30).astype(np.float32)
    m1 = np.where(ii >= jj + 128, 0.0, -1.0e30).astype(np.float32)
    masklog = np.concatenate([m0[:, 0:256], m1[:, 0:256]], axis=1)
    onesel = np.zeros((16, 2048), np.float32)
    for k in range(16):
        onesel[k, 128 * k:128 * (k + 1)] = 1.0
    oco = np.zeros((128, 256), np.float32)
    for h in range(16):
        oco[:, 16 * h + h] = 1.0
    import ml_dtypes
    return {
        "identb": np.eye(128, dtype=ml_dtypes.bfloat16),
        "identf": np.eye(128, dtype=np.float32),
        "masklog": masklog,
        "onesel": onesel,
        "oco": oco,
        "onesrow_b": np.ones((1, 128), ml_dtypes.bfloat16),
        "onesrow_f": np.ones((1, 128), np.float32),
        "onescol_f": np.ones((128, 1), np.float32),
    }


def _shard_inputs(core, I):
    bq, q = divmod(core, 4)
    x = np.asarray(I["L0_attn_hidden_states"])
    start = 1024 * q
    own = x[bq, start:start + T]
    halo = x[bq, start - 3:start] if q > 0 else np.zeros((3, HID), np.float32)
    xe = np.concatenate([halo, own], axis=0)
    pre = np.zeros((NPRE, HID), np.float32)
    if q > 0:
        pre[NPRE - start:] = x[bq, 0:start]
    prem = np.zeros((1, 12), np.float32)
    prem[0, 12 - 4 * q:] = 1.0
    halom = np.full((128, 1), 1.0 if q > 0 else 0.0, np.float32)

    c = np.asarray(I["L0_c"])
    c_pk = np.ascontiguousarray(c[bq].reshape(16, 128).T)

    convw = np.ascontiguousarray(
        np.asarray(I["L0_attn_conv_weight"]).reshape(24, 128, 4)
        .transpose(1, 0, 2).reshape(128, 96))
    convb = np.ascontiguousarray(
        np.asarray(I["L0_attn_conv_bias"]).reshape(24, 128).T)

    gw = np.asarray(I["L0_ffn_gate_w"])
    vw = np.asarray(I["L0_ffn_val_w"])
    ow = np.asarray(I["L0_ffn_out_w"])
    gwB = np.ascontiguousarray(
        gw.T.reshape(16, 128, 64, 128).transpose(2, 1, 0, 3)
        .reshape(64, 128, 2048))
    vwB = np.ascontiguousarray(
        vw.T.reshape(16, 128, 64, 128).transpose(2, 1, 0, 3)
        .reshape(64, 128, 2048))
    owB = np.ascontiguousarray(
        ow.T.reshape(4, 16, 128, 16, 128).transpose(3, 0, 2, 1, 4)
        .reshape(16, 4, 128, 2048))

    d = {
        "xe": xe, "xp": pre, "premrow": prem, "halom": halom,
        "epscol": np.full((128, 1), 1e-5, np.float32),
        "premB": np.broadcast_to(prem, (128, 12)).copy(),
        "w_in": np.asarray(I["L0_attn_W_in"]),
        "wmod": np.asarray(I["L0_wmod"]),
        "c_pk": c_pk, "convw": convw, "convb": convb,
        "acol": np.asarray(I["L0_attn_A_log"]).reshape(16, 1),
        "dtbcol": np.asarray(I["L0_attn_dt_bias"]).reshape(16, 1),
        "drow": np.asarray(I["L0_attn_D"]).reshape(1, 16),
        "nwrow": np.asarray(I["L0_attn_norm_weight"]).reshape(1, HID),
        "w_out": np.asarray(I["L0_attn_W_out"]),
        "gwB": gwB, "vwB": vwB, "owB": owB,
        "gb2": np.ascontiguousarray(
            np.asarray(I["L0_ffn_gate_b"]).reshape(64, 128).T),
        "vb2": np.ascontiguousarray(
            np.asarray(I["L0_ffn_val_b"]).reshape(64, 128).T),
        "obrow": np.asarray(I["L0_ffn_out_b"]).reshape(1, HID),
    }
    d.update(_host_consts())
    for k in d:
        if d[k].dtype == np.float64:
            raise ValueError(f"{k} is float64")
    return d


def kernel(**inputs):
    from concourse.bass_utils import run_bass_kernel_spmd
    nc, _ = build()
    in_maps = [_shard_inputs(c, inputs) for c in range(8)]
    res = run_bass_kernel_spmd(nc, in_maps, core_ids=list(range(8)))
    out = np.zeros((2, 4096, HID), np.float32)
    for c in range(8):
        bq, q = divmod(c, 4)
        out[bq, 1024 * q:1024 * (q + 1)] = res.results[c]["xo"]
    return out


if __name__ == "__main__":
    nc, dbg = build(debug=False, upto="ph0")
    print("built", sum(1 for _ in nc.all_instructions()), "instructions")


# version 25
# speedup vs baseline: 1.0730x; 1.0730x over previous
import numpy as np
from contextlib import ExitStack

import concourse.bass as bass
import concourse.mybir as mybir

F32 = mybir.dt.float32
BF16 = mybir.dt.bfloat16
AF = mybir.ActivationFunctionType
ALU = mybir.AluOpType
AXX = mybir.AxisListType.X

MAGIC = 12582912.0
HID = 2048
PROJ = 5136
NH = 16
Q = 256
T = 1024
NPRE = 3072
EPS = 1e-5
FFH = 8192


class Builder:
    def __init__(self, debug=False, upto="all"):
        self.debug = debug
        self.upto = upto
        self.nc = bass.Bass("TRN2", target_bir_lowering=False)
        self.es = ExitStack()
        self.dbg = {}
        self._semn = 0

    def din(self, name, shape, dt=F32):
        return self.nc.dram_tensor(name, shape, dt, kind="ExternalInput").ap()

    def dint(self, name, shape, dt=F32):
        return self.nc.dram_tensor(name, shape, dt, kind="Internal").ap()

    def dout(self, name, shape, dt=F32):
        ap = self.nc.dram_tensor(name, shape, dt, kind="ExternalOutput").ap()
        return ap

    def ddbg(self, name, shape, dt=F32):
        if self.debug:
            self.dbg[name] = self.nc.dram_tensor(
                name, shape, dt, kind="ExternalOutput").ap()
            return self.dbg[name]
        return None

    def sb(self, name, shape, dt=F32, scope=None, side=None):
        if side is None:
            side = "left" if scope is None else "right"
        cm = self.nc.sbuf_tensor(name, shape, dt, side=side)
        return (scope or self.es).enter_context(cm)

    def sem(self, name=None):
        self._semn += 1
        return Sem(self, name or f"s{self._semn}")

    def defer(self, queue, which, emit, after=(), sem=None, k=1):
        lz = Lazy(sem) if sem is not None else None
        queue.append((which, emit, list(after), sem, k, lz))
        return lz

    def flush_zip(self, queues):
        queues = [list(q) for q in queues]
        idx = [0] * len(queues)

        def ready(a):
            if a is None:
                return True
            if isinstance(a, Lazy):
                return a.val is not None
            return True

        def conv(a):
            if isinstance(a, Lazy):
                return (a.sem, a.val)
            return a

        while any(i < len(q) for i, q in zip(idx, queues)):
            emitted = False
            for qi, q in enumerate(queues):
                if idx[qi] >= len(q):
                    continue
                which, emit, after, sem, k, lz = q[idx[qi]]
                if not all(ready(a) for a in after):
                    continue
                m = self.step(which, emit,
                              after=[conv(a) for a in after if a is not None],
                              sem=sem, k=k)
                if lz is not None:
                    lz.val = m[1]
                idx[qi] += 1
                emitted = True
            if not emitted:
                raise RuntimeError("zipper deadlock")

    def step(self, which, emit, after=(), sem=None, k=1, dbgname=None):
        blk = self.blk
        dec = getattr(blk, which)
        out = {}

        def _(eng):
            for ms in after:
                if ms is not None:
                    eng.wait_ge(ms[0].h, ms[1])
            out["last"] = emit(eng)
        dec(_)
        if sem is not None:
            assert out["last"] is not None, dbgname
            return sem.inc(out["last"], k)
        return None


class Lazy:
    __slots__ = ("sem", "val")

    def __init__(self, sem):
        self.sem = sem
        self.val = None


class Sem:
    def __init__(self, b, name):
        self.h = b.es.enter_context(b.nc.semaphore(name))
        self.n = 0

    def inc(self, instr, k=1):
        instr.then_inc(self.h, k)
        self.n += k
        return (self, self.n)

    def mark(self):
        return (self, self.n)


def build(debug=False, upto="all"):
    b = Builder(debug=debug, upto=upto)
    nc, es = b.nc, b.es
    PHASES = ["ph0", "pre", "ph1", "ph2", "ph3", "ph4"]
    upto_i = len(PHASES) - 1 if upto == "all" else PHASES.index(upto)

    def enabled(ph):
        return PHASES.index(ph) <= upto_i

    xe = b.din("xe", [3 + T, HID])
    xp = b.din("xp", [NPRE, HID])
    w_in = b.din("w_in", [HID, PROJ])
    wmod = b.din("wmod", [HID, 12288])
    c_pk = b.din("c_pk", [128, 16])
    convw = b.din("convw", [128, 96])
    convb = b.din("convb", [128, 24])
    acol = b.din("acol", [16, 1])
    dtbcol = b.din("dtbcol", [16, 1])
    drow = b.din("drow", [1, 16])
    nwrow = b.din("nwrow", [1, HID])
    w_out = b.din("w_out", [HID, HID])
    gwB = b.din("gwB", [64, 128, 2048])
    vwB = b.din("vwB", [64, 128, 2048])
    owB = b.din("owB", [16, 4, 128, 2048])
    gb2 = b.din("gb2", [128, 64])
    vb2 = b.din("vb2", [128, 64])
    obrow = b.din("obrow", [1, HID])
    identb = b.din("identb", [128, 128], BF16)
    identf = b.din("identf", [128, 128])
    masklog = b.din("masklog", [128, 512])
    onesel = b.din("onesel", [16, 2048])
    oco = b.din("oco", [128, 256])
    onesrow_b = b.din("onesrow_b", [1, 128], BF16)
    onesrow_f = b.din("onesrow_f", [1, 128])
    onescol_f = b.din("onescol_f", [128, 1])
    halom = b.din("halom", [128, 1])
    premrow = b.din("premrow", [1, 12])
    premB = b.din("premB", [128, 12])
    epscol = b.din("epscol", [128, 1])

    xo = b.dout("xo", [T, HID], F32)
    modsc = b.dint("modsc", [1, 12288 + 2 * HID])
    xsc = b.dint("xsc", [T, HID])

    sb = b.sb
    t_identb = sb("t_identb", [128, 128], BF16)
    t_identf = sb("t_identf", [128, 128])
    t_ones_b = sb("t_ones_b", [1, 128], BF16)
    t_ones_f = sb("t_ones_f", [1, 128])
    t_onesc = sb("t_onesc", [128, 1])
    t_halom = sb("t_halom", [128, 1])
    t_prem = sb("t_prem", [1, 12])
    t_premB = sb("t_premB", [128, 12])
    t_eps = sb("t_eps", [128, 1])
    t_convw = sb("t_convw", [128, 96])
    t_convb = sb("t_convb", [128, 24])
    t_acol = sb("t_acol", [16, 1])
    t_dtbcol = sb("t_dtbcol", [16, 1])
    t_drow = sb("t_drow", [1, 16])
    t_gb2 = sb("t_gb2", [128, 64])
    t_vb2 = sb("t_vb2", [128, 64])
    t_db = sb("t_db", [128, 16])
    t_cpk = sb("t_cpk", [128, 16])
    t_silc = sb("t_silc", [128, 16], BF16)
    t_modc = sb("t_modc", [128, 64])
    t_gbgc = sb("t_gbgc", [128, 16])
    t_gamma = sb("t_gamma", [128, 12])
    t_gacc = sb("t_gacc", [128, 4])
    t_gq = sb("t_gq", [128, 48])
    t_grow = sb("t_grow", [1, 12])
    t_col = sb("t_col", [128, 16])
    esL = ExitStack()
    t_masklog = b.sb("t_masklog", [128, 512], scope=esL, side="left")
    t_onesel = b.sb("t_onesel", [16, 2048], scope=esL, side="left")
    t_oco = b.sb("t_oco", [128, 256], scope=esL, side="left")
    t_nwsel = b.sb("t_nwsel", [16, 2048], scope=esL, side="left")
    t_junk = b.sb("t_junk", [128, 2048], scope=esL, side="left")
    t_dthm = b.sb("t_dthm", [16, T], scope=esL, side="left")
    t_ldthm = b.sb("t_ldthm", [16, T], scope=esL, side="left")
    t_cumhm = b.sb("t_cumhm", [16, T], scope=esL, side="left")
    t_whm = b.sb("t_whm", [16, T], scope=esL, side="left")
    t_z16 = b.sb("t_z16", [16, 256], scope=esL, side="left")
    t_S = [b.sb(f"t_S{h}", [128, 128], scope=esL, side="left")
           for h in range(NH)]
    t_msrow = b.sb("t_msrow", [1, 2048], scope=esL, side="left")

    pb = [es.enter_context(nc.psum_tensor(f"pb{i}", [128, 512], F32))
          for i in range(6)]
    pt32 = es.enter_context(nc.psum_tensor("pt32", [128, 512], F32))
    ptb = es.enter_context(nc.psum_tensor("ptb", [128, 1024], BF16))

    s_const = b.sem("s_const")
    s_pe = b.sem("s_pe")
    s_v = b.sem("s_v")
    s_a = b.sem("s_a")
    s_g = b.sem("s_g")
    s_ev = b.sem("s_ev")
    s_mod = b.sem("s_mod")
    s_mx1 = b.sem("s_mx1")
    s_mx2 = b.sem("s_mx2")
    s_hd = b.sem("s_hd")
    s_gd = b.sem("s_gd")
    s_st = b.sem("s_st")

    b.blk = es.enter_context(nc.Block())
    st = b.step

    def _c(sy):
        m = None
        for dst, src in [
            (t_identb, identb), (t_identf, identf), (t_masklog, masklog),
            (t_onesel, onesel), (t_oco, oco), (t_ones_b, onesrow_b),
            (t_ones_f, onesrow_f), (t_onesc, onescol_f), (t_halom, halom),
            (t_prem, premrow), (t_premB, premB), (t_eps, epscol), (t_convw, convw), (t_convb, convb),
            (t_acol, acol), (t_dtbcol, dtbcol), (t_drow, drow),
            (t_gb2, gb2), (t_vb2, vb2),
            (t_cpk, c_pk),
        ]:
            m = sy.dma_start(out=dst[:], in_=src)
            s_const.inc(m, 16)
        return None
    st("sync", _c)
    m_const = s_const.mark()

    m_nwz = st("gpsimd", lambda g: (g.memset(t_nwsel[:], 0.0),
                                    g.memset(t_z16[:], 0.0))[-1],
               sem=s_g)

    def _nw(sy):
        m = None
        for k in range(16):
            m = sy.dma_start(out=t_nwsel[k:k + 1, 128 * k:128 * (k + 1)],
                             in_=nwrow[0:1, 128 * k:128 * (k + 1)])
            s_const.inc(m, 16)
        return None
    st("sync", _nw, after=[m_nwz, m_const])
    m_nwsel = s_const.mark()

    m_silc = st("scalar", lambda a: a.activation(out=t_silc[:], in_=t_cpk[:],
                                                 func=AF.Silu),
                after=[m_const], sem=s_a)

    m_acol0 = st("scalar", lambda a: a.activation(out=t_acol[:], in_=t_acol[:],
                                                  func=AF.Exp),
                 after=[m_const], sem=s_a)
    m_acol = st("vector", lambda v: v.tensor_scalar(
        out=t_acol[:], in0=t_acol[:], scalar1=-1.0, scalar2=None,
        op0=ALU.mult), after=[m_acol0], sem=s_v)

    m_pedb = st("tensor", lambda t: t.matmul(pt32[:, 0:16], t_ones_f[:],
                                             t_drow[:], start=True, stop=True),
                after=[m_const], sem=s_pe)
    m_db = st("vector", lambda v: v.tensor_copy(out=t_db[:], in_=pt32[:, 0:16]),
              after=[m_pedb], sem=s_v)

    with nc.sbuf_tensor("t_wm0", [128, 2048], BF16) as _wm0, \
         nc.sbuf_tensor("t_wm1", [128, 2048], BF16) as _wm1:
        t_wm = [_wm0, _wm1]
        s_wm = [b.sem("s_wm0"), b.sem("s_wm1")]
        s_mw = [b.sem(f"s_mw{i}") for i in range(3)]
        mod_evac = []
        mm_done = []
        for cs in range(6):
            for k in range(16):
                i = cs * 16 + k
                sl = i % 2
                aft_l = [mm_done[i - 2]] if i >= 2 else []
                st("gpsimd",
                   lambda g, cs=cs, k=k, sl=sl: g.dma_start(
                       out=t_wm[sl][:],
                       in_=wmod[128 * k:128 * (k + 1),
                                2048 * cs:2048 * (cs + 1)]),
                   after=aft_l, sem=s_wm[sl], k=16)
                m_l = (s_wm[sl], 16 * (i // 2 + 1))

                def _mm(t, cs=cs, k=k, sl=sl):
                    m = None
                    for j in range(4):
                        m = t.matmul(pb[j][0:1, :], t_silc[:, k:k + 1],
                                     t_wm[sl][:, 512 * j:512 * (j + 1)],
                                     start=(k == 0), stop=(k == 15))
                    return m
                aft = [m_l, m_silc, m_db]
                if k == 0 and cs >= 1:
                    aft.append(mod_evac[cs - 1][0])
                st("tensor", _mm, after=aft, sem=s_pe)
                mm_done.append(s_pe.mark())

            def _ev(a, cs=cs):
                m = None
                for j in range(4):
                    m = a.activation(
                        out=t_msrow[0:1, 512 * j:512 * (j + 1)],
                        in_=pb[j][0:1, :], func=AF.Copy)
                return m
            m_ev = st("scalar", _ev, after=[mm_done[-1]] +
                      ([mod_evac[cs - 1][1]] if cs >= 1 else []), sem=s_a)
            grp = 0 if cs < 2 else (1 if cs < 5 else 2)
            m_wr = st("sync", lambda sy, cs=cs: sy.dma_start(
                out=modsc[0:1, 2048 * cs:2048 * (cs + 1)], in_=t_msrow[:]),
                after=[m_ev], sem=s_mw[grp], k=16)
            mod_evac.append((m_ev, m_wr))
        m_mrow = st("gpsimd", lambda g: g.engine_nop(),
                    after=[(s_mw[0], 32), (s_mw[1], 48), (s_mw[2], 16)],
                    sem=s_g)

    def _rt1(sy):
        for i, off in enumerate([0, 2048]):
            m = sy.dma_start(
                out=t_modc[:, 16 * i:16 * (i + 1)],
                in_=modsc[0:1, off:off + 2048].rearrange(
                    "a (k p) -> p a k", p=128).rearrange("p a k -> (p a) k"),
                allow_slow_non_contiguous=True)
            s_mx1.inc(m, 16)
        return None
    st("sync", _rt1, after=[mod_evac[1][1]])
    m_modc0 = s_mx1.mark()

    m_modc1 = st("vector", lambda v: v.tensor_scalar_add(
        out=t_modc[:, 16:32], in0=t_modc[:, 16:32], scalar1=1.0),
        after=[m_modc0], sem=s_v)

    def _rt2(sy):
        for i, off in enumerate([3 * 2048, 4 * 2048]):
            m = sy.dma_start(
                out=t_modc[:, 32 + 16 * i:48 + 16 * i],
                in_=modsc[0:1, off:off + 2048].rearrange(
                    "a (k p) -> p a k", p=128).rearrange("p a k -> (p a) k"),
                allow_slow_non_contiguous=True)
            s_mx2.inc(m, 16)
        return None
    st("sync", _rt2, after=[mod_evac[4][1]])
    m_modc2x = s_mx2.mark()

    m_modc = st("vector", lambda v: v.tensor_scalar_add(
        out=t_modc[:, 48:64], in0=t_modc[:, 48:64], scalar1=1.0),
        after=[m_modc2x, m_modc1], sem=s_v)

    def bcast_row(row_ap, out_tile, width, bf16, after):
        prev = None
        ones = t_ones_b if bf16 else t_ones_f
        for j in range((width + 511) // 512):
            w0, w1 = 512 * j, min(512 * (j + 1), width)

            def _t(t, w0=w0, w1=w1):
                return t.matmul(pt32[:, 0:w1 - w0], ones[:],
                                row_ap[0:1, w0:w1], start=True, stop=True)
            aft = list(after) + ([prev] if prev else [])
            m_b = st("tensor", _t, after=aft, sem=s_pe)

            def _v(v, w0=w0, w1=w1):
                return v.tensor_copy(out=out_tile[:, w0:w1],
                                     in_=pt32[:, 0:w1 - w0])
            prev = st("vector", _v, after=[m_b], sem=s_v)
        return prev

    gam_srcs = []
    for j in range(16):
        gam_srcs.append(gwB[4 * j:4 * (j + 1)].rearrange("a p w -> p a w"))
    for j in range(16):
        gam_srcs.append(vwB[4 * j:4 * (j + 1)].rearrange("a p w -> p a w"))
    for j in range(16):
        gam_srcs.append(owB[j].rearrange("s p w -> p s w"))

    def emit_gamma_tail(aft0):
        m_gr0 = st("tensor", lambda t: t.matmul(pt32[0:1, 0:48], t_onesc[:],
                                                t_gq[:], start=True,
                                                stop=True),
                   after=aft0 + [m_const], sem=s_pe)

        def _grsum(v):
            for mi in range(3):
                v.tensor_reduce(out=pt32[0:1, 48 + mi:49 + mi],
                                in_=pt32[0:1, 16 * mi:16 * (mi + 1)],
                                axis=AXX, op=ALU.add)
            v.drain()
            return v.tensor_copy(out=pt32[0:1, 51:52], in_=pt32[0:1, 48:49])
        m_gr = st("vector", _grsum, after=[m_gr0], sem=s_v)

        def _gfin(v):
            v.tensor_scalar(out=t_grow[0:1, 0:4], in0=pt32[0:1, 48:52],
                            scalar1=1.0 / (FFH * HID), scalar2=1e-5,
                            op0=ALU.mult, op1=ALU.max)
            v.drain()
            v.tensor_scalar(out=t_grow[0:1, 0:4], in0=t_grow[0:1, 0:4],
                            scalar1=0.5, scalar2=None, op0=ALU.mult)
            v.drain()
            v.tensor_scalar(out=t_grow[0:1, 4:8], in0=t_grow[0:1, 0:4],
                            scalar1=-1.0, scalar2=None, op0=ALU.mult)
            v.drain()
            return v.tensor_scalar(out=t_grow[0:1, 8:12],
                                   in0=t_grow[0:1, 0:4],
                                   scalar1=2.0, scalar2=None, op0=ALU.mult)
        m_gfin = st("vector", _gfin, after=[m_gr], sem=s_v)

        m_gcb = st("tensor", lambda t: t.matmul(pt32[:, 0:12], t_ones_f[:],
                                                t_grow[0:1, 0:12], start=True,
                                                stop=True),
                   after=[m_gfin], sem=s_pe)
        return st("vector", lambda v: v.tensor_copy(out=t_gamma[:],
                                                    in_=pt32[:, 0:12]),
                  after=[m_gcb], sem=s_v)

    if debug:
        d_mrow = b.ddbg("d_mrow", [1, 12288])
        st("sync", lambda sy: sy.dma_start(out=d_mrow, in_=modsc[0:1, 0:12288]),
           after=[m_mrow], sem=s_st, k=16)

    m_join0 = st("gpsimd", lambda g: g.engine_nop(),
                 after=[m_nwsel, m_db,
                        (s_st.mark() if debug else None)], sem=s_g)
    m_tail = m_join0
    m_gamt = None

    def emit_ln(xtile, utok, rows, after_x, c0=0, stepf=None):
        cA = t_col[:, c0:c0 + 1]
        cB = t_col[:, c0 + 1:c0 + 2]
        cC = t_col[:, c0 + 2:c0 + 3]
        cD = t_col[:, c0 + 3:c0 + 4]
        cE = t_col[:, c0 + 4:c0 + 5]
        cF = t_col[:, c0 + 5:c0 + 6]
        xs = xtile[0:rows, :]

        def _v1(v):
            v.scalar_tensor_tensor(out=t_junk[0:rows, :], in0=xs, scalar=1.0,
                                   in1=xs, op0=ALU.mult, op1=ALU.mult,
                                   accum_out=cB[0:rows])
            v.tensor_reduce(out=cA[0:rows], in_=xs, axis=AXX, op=ALU.add)
            v.drain()
            v.tensor_scalar(out=cC[0:rows], in0=cA[0:rows],
                            scalar1=1.0 / HID, scalar2=None, op0=ALU.mult)
            v.drain()
            v.tensor_scalar(out=cD[0:rows], in0=cC[0:rows], scalar1=cC[0:rows],
                            scalar2=-1.0, op0=ALU.mult, op1=ALU.mult)
            v.drain()
            return v.tensor_scalar(out=cD[0:rows], in0=cD[0:rows],
                                   scalar1=EPS, scalar2=None, op0=ALU.add)
        stf = stepf or st
        mv1 = stf("vector", _v1, after=after_x, sem=s_v)

        def _a2(a):
            a.activation(out=cE[0:rows], in_=cB[0:rows], func=AF.Ln,
                         bias=cD[0:rows], scale=1.0 / HID)
            a.drain()
            return a.activation(out=cD[0:rows], in_=cE[0:rows], func=AF.Exp,
                                scale=-0.5)
        ma2 = stf("scalar", _a2, after=[mv1], sem=s_a)

        def _v2(v):
            v.tensor_scalar(out=cF[0:rows], in0=cC[0:rows],
                            scalar1=cD[0:rows], scalar2=-1.0,
                            op0=ALU.mult, op1=ALU.mult)
            v.drain()
            return v.tensor_scalar(out=utok[0:rows, :], in0=xs,
                                   scalar1=cD[0:rows], scalar2=cF[0:rows],
                                   op0=ALU.mult, op1=ALU.add)
        mu = stf("vector", _v2, after=[ma2], sem=s_v)
        return mu

    if enabled("pre"):
        esP = ExitStack()
        t_wpre = b.sb("t_wpre", [128, 16 * 1040], BF16, scope=esP)
        t_px = [b.sb(f"t_px{i}", [128, HID], scope=esP) for i in range(2)]
        t_put = b.sb("t_put", [128, HID], BF16, scope=esP)
        t_upf = [b.sb(f"t_upf{k}", [128, 512], BF16, scope=esP)
                 for k in range(16)]
        t_psc = [b.sb(f"t_psc{cc}", [128, 2 * 259], scope=esP)
                 for cc in range(8)]
        t_tails = b.sb("t_tails", [128, 24], scope=esP)
        t_pxb = [b.sb(f"t_pxb{cc}", [128, 256], BF16, scope=esP)
                 for cc in range(8)]
        t_cvp = b.sb("t_cvp", [128, 256], scope=esP)
        t_cgp = b.sb("t_cgp", [128, 256], scope=esP)
        t_csp = b.sb("t_csp", [128, 8 * 256], scope=esP)
        t_dtp = b.sb("t_dtp", [16, 512], scope=esP)
        t_ldp = b.sb("t_ldp", [16, 256], scope=esP)
        t_cup = b.sb("t_cup", [16, 256], scope=esP)
        t_whp = b.sb("t_whp", [16, 256], scope=esP)
        t_xctp = [b.sb(f"t_xctp{i}", [128, 128], BF16, scope=esP)
                  for i in range(8)]
        t_btp = [b.sb(f"t_btp{i}", [128, 128], BF16, scope=esP)
                 for i in range(8)]
        t_bwp = b.sb("t_bwp", [128, 8 * 128], BF16, scope=esP)
        t_wtmp = [b.sb(f"t_wtmp{j}", [128, 16], scope=esP) for j in range(2)]
        t_cdbp = b.sb("t_cdbp", [128, 16], scope=esP)
        t_clrp = b.sb("t_clrp", [1, 16], scope=esP)
        s_px = [b.sem("s_px0"), b.sem("s_px1")]
        s_wpre = b.sem("s_wpre")

        def _pz(g):
            for h in range(NH):
                g.memset(t_S[h][:], 0.0)
            return g.memset(t_tails[:], 0.0)
        m_pz = st("gpsimd", _pz, after=[], sem=s_g)

        st("gpsimd", lambda g: g.dma_start(
            out=t_wpre[:, 0:16 * 1024],
            in_=w_in[:, 2048:3072].rearrange("(c p) w -> p c w", p=128)),
           after=[], sem=s_wpre, k=16)
        st("gpsimd", lambda g: g.dma_start(
            out=t_wpre[:, 16 * 1024:16 * 1040],
            in_=w_in[:, 5120:5136].rearrange("(c p) w -> p c w", p=128)),
           sem=s_wpre, k=16)
        m_wpre = s_wpre.mark()

        stA = {}
        last_mu = [None, None]
        last_wave = None
        last_b6 = None
        last_su = [m_pz, m_pz]
        last_evs = None
        prev_b = {"tails": m_pz, "conv": None, "sil": None}

        def stage_a(q, c):
            par = c % 2
            r0 = 256 * c
            A = {}
            dstp = lambda which, emit, after=(), sem=None, k=1: \
                b.defer(q, which, emit, after, sem, k)
            mus = []
            waves = []
            for i in range(2):
                aft_l = [m_pz, m_join0] if last_mu[i] is None \
                    else [last_mu[i]]
                lx = dstp("sync", lambda sy, r0=r0, i=i: sy.dma_start(
                    out=t_px[i][:],
                    in_=xp[r0 + 128 * i:r0 + 128 * (i + 1), :]),
                    after=aft_l, sem=s_px[i], k=16)
                aft_x = [lx] + ([mus[0]] if i == 1 else [])
                mu = emit_ln(t_px[i], t_put, 128, aft_x, c0=8, stepf=dstp)
                mus.append(mu)
                last_mu[i] = mu
                for w in range(2):
                    def _twp(t, i=i, w=w):
                        m = None
                        for kk in range(8):
                            k = 8 * w + kk
                            m = t.transpose(
                                ptb[:, 128 * kk:128 * (kk + 1)],
                                t_put[:, 128 * k:128 * (k + 1)],
                                t_identb[:])
                        return m
                    aft = [mu] + ([waves[-1]] if waves else
                                  ([last_b6] if last_b6 else []))
                    m_t = dstp("tensor", _twp, after=aft, sem=s_pe)

                    def _evp(a, i=i, w=w, par=par):
                        m = None
                        for kk in range(8):
                            k = 8 * w + kk
                            m = a.activation(
                                out=t_upf[k][:, 256 * par + 128 * i:
                                             256 * par + 128 * (i + 1)],
                                in_=ptb[:, 128 * kk:128 * (kk + 1)],
                                func=AF.Identity,
                                scale=t_modc[:, 16 + k:17 + k],
                                bias=t_modc[:, k:k + 1])
                        return m
                    mw = dstp("scalar", _evp, after=[m_t, m_modc1], sem=s_a)
                    waves.append(mw)
            A["upf"] = waves[-1]
            A["lastwave"] = waves[-1]

            evs = []
            prev_pair = None
            for jp in range(4):
                def _pmm(t, jp=jp, par=par):
                    m = None
                    for j in (2 * jp, 2 * jp + 1):
                        pX = pb[j % 2]
                        for k in range(16):
                            m = t.matmul(
                                pX[:, 0:256],
                                t_wpre[:, 1024 * k + 128 * j:
                                       1024 * k + 128 * (j + 1)],
                                t_upf[k][:, 256 * par:256 * (par + 1)],
                                start=(k == 0), stop=(k == 15))
                    return m
                aft = [A["upf"], m_wpre]
                if prev_pair is not None:
                    aft.append(prev_pair)
                elif last_evs is not None:
                    aft.append(last_evs[-1])
                m_pm = dstp("tensor", _pmm, after=aft, sem=s_pe)

                def _pev(v, jp=jp, par=par, c=c):
                    m = None
                    for j in (2 * jp, 2 * jp + 1):
                        m = v.tensor_scalar(
                            out=t_psc[j][:, 259 * par + 3:259 * par + 259],
                            in0=pb[j % 2][:, 0:256],
                            scalar1=t_premB[:, c:c + 1],
                            scalar2=None, op0=ALU.mult)
                    return m
                aft_e = [m_pm]
                if prev_b["conv"] is not None:
                    aft_e.append(prev_b["conv"])
                pe_ = dstp("vector", _pev, after=aft_e, sem=s_v)
                evs.append(pe_)
                prev_pair = pe_
            A["evs"] = evs

            def _pdt(t):
                m = None
                for k in range(16):
                    m = t.matmul(pb[2][0:16, 0:256],
                                 t_wpre[:, 16 * 1024 + 16 * k:
                                        16 * 1024 + 16 * (k + 1)],
                                 t_upf[k][:, 256 * par:256 * (par + 1)],
                                 start=(k == 0), stop=(k == 15))
                return m
            aft_dt = [A["upf"], m_wpre]
            if "dtev" in stA.get(c - 1, {}):
                aft_dt.append(stA[c - 1]["dtev"])
            m_pdt = dstp("tensor", _pdt, after=aft_dt, sem=s_pe)
            A["dtev"] = dstp("vector", lambda v, par=par: v.tensor_copy(
                out=t_dtp[0:16, 256 * par:256 * (par + 1)],
                in_=pb[2][0:16, 0:256]), after=[m_pdt], sem=s_v)
            stA[c] = A
            return A

        def stage_b(q, c):
            par = c % 2
            A = stA[c]
            dstp = lambda which, emit, after=(), sem=None, k=1: \
                b.defer(q, which, emit, after, sem, k)
            def _tl(v, par=par):
                m = None
                for cc in range(8):
                    m = v.tensor_copy(
                        out=t_psc[cc][:, 259 * par:259 * par + 3],
                        in_=t_tails[:, 3 * cc:3 * cc + 3])
                return m
            m_tl = dstp("vector", _tl, after=[prev_b["tails"], A["evs"][0]],
                        sem=s_v)

            convs = []
            for half in range(2):
                def _cvp(v, half=half, par=par):
                    m = None
                    for cc in range(4 * half, 4 * half + 4):
                        sc = t_psc[cc][:, 259 * par:259 * (par + 1)]
                        v.tensor_scalar(out=t_cvp[:], in0=sc[:, 0:256],
                                        scalar1=t_convw[:, 4 * cc:4 * cc + 1],
                                        scalar2=None, op0=ALU.mult)
                        v.tensor_scalar(out=t_cgp[:], in0=sc[:, 2:258],
                                        scalar1=t_convw[:, 4 * cc + 2:
                                                        4 * cc + 3],
                                        scalar2=None, op0=ALU.mult)
                        v.drain()
                        v.scalar_tensor_tensor(
                            out=t_cvp[:], in0=sc[:, 1:257],
                            scalar=t_convw[:, 4 * cc + 1:4 * cc + 2],
                            in1=t_cvp[:], op0=ALU.mult, op1=ALU.add)
                        v.scalar_tensor_tensor(
                            out=t_cgp[:], in0=sc[:, 3:259],
                            scalar=t_convw[:, 4 * cc + 3:4 * cc + 4],
                            in1=t_cgp[:], op0=ALU.mult, op1=ALU.add)
                        v.drain()
                        m = v.tensor_tensor(
                            out=t_csp[:, 256 * cc:256 * (cc + 1)],
                            in0=t_cvp[:], in1=t_cgp[:], op=ALU.add)
                    return m
                cv = dstp("vector", _cvp,
                          after=[A["evs"][2 * half + 1], m_tl], sem=s_v)
                convs.append(cv)
            prev_b["conv"] = convs[-1]

            def _sil(a):
                m = None
                for cc in range(8):
                    m = a.activation(out=t_pxb[cc][:],
                                     in_=t_csp[:, 256 * cc:256 * (cc + 1)],
                                     func=AF.Silu, bias=t_convb[:, cc:cc + 1])
                return m
            m_si = dstp("scalar", _sil, after=convs, sem=s_a)
            prev_b["sil"] = m_si

            def _tw(v, par=par):
                m = None
                for cc in range(8):
                    m = v.tensor_copy(
                        out=t_tails[:, 3 * cc:3 * cc + 3],
                        in_=t_psc[cc][:, 259 * par + 256:259 * par + 259])
                return m
            prev_b["tails"] = dstp("vector", _tw, after=[m_si], sem=s_v)

            ps1 = dstp("vector", lambda v, par=par: v.tensor_scalar(
                out=t_dtp[0:16, 256 * par:256 * (par + 1)],
                in0=t_dtp[0:16, 256 * par:256 * (par + 1)],
                scalar1=t_dtbcol[0:16, 0:1], scalar2=None, op0=ALU.add),
                after=[A["dtev"]], sem=s_v)

            def _ps2(a, par=par):
                dtv = t_dtp[0:16, 256 * par:256 * (par + 1)]
                a.activation(out=t_ldp[:], in_=dtv, func=AF.Abs)
                a.drain()
                a.activation(out=t_ldp[:], in_=t_ldp[:], func=AF.Exp,
                             scale=-1.0)
                a.drain()
                return a.activation(out=t_ldp[:], in_=t_ldp[:], func=AF.Ln,
                                    bias=1.0)
            ps2 = dstp("scalar", _ps2, after=[ps1], sem=s_a)

            def _ps3(v, par=par):
                dtv = t_dtp[0:16, 256 * par:256 * (par + 1)]
                v.tensor_scalar(out=t_cup[:], in0=dtv, scalar1=0.0,
                                scalar2=None, op0=ALU.max)
                v.drain()
                return v.tensor_tensor(out=dtv, in0=t_cup[:], in1=t_ldp[:],
                                       op=ALU.add)
            ps3 = dstp("vector", _ps3, after=[ps2], sem=s_v)
            pld = dstp("scalar", lambda a, par=par: a.activation(
                out=t_ldp[:], in_=t_dtp[0:16, 256 * par:256 * (par + 1)],
                func=AF.Ln), after=[ps3], sem=s_a)

            def _pcum(v, par=par):
                dtv = t_dtp[0:16, 256 * par:256 * (par + 1)]
                v.tensor_scalar(out=t_whp[:], in0=dtv,
                                scalar1=t_acol[0:16, 0:1], scalar2=None,
                                op0=ALU.mult)
                v.drain()
                v.tensor_tensor_scan(out=t_cup[:], data0=t_whp[:],
                                     data1=t_z16[:, 0:256], initial=0.0,
                                     op0=ALU.add, op1=ALU.bypass)
                v.drain()
                v.tensor_tensor(out=t_whp[:], in0=t_ldp[:], in1=t_cup[:],
                                op=ALU.subtract)
                v.drain()
                return v.tensor_scalar(out=t_whp[:], in0=t_whp[:],
                                       scalar1=t_cup[:, 255:256],
                                       scalar2=None, op0=ALU.add)
            pcum = dstp("vector", _pcum, after=[pld, m_acol], sem=s_v)

            pc1 = dstp("tensor", lambda t: t.transpose(
                pt32[0:1, 0:16], t_cup[:, 255:256], t_identf[0:16, 0:16]),
                after=[pcum, last_su[0], last_su[1]], sem=s_pe)

            def _pclr2(v, c=c):
                v.tensor_copy(out=t_clrp[:], in_=pt32[0:1, 0:16])
                v.drain()
                return v.tensor_scalar(out=t_clrp[:], in0=t_clrp[:],
                                       scalar1=t_prem[0:1, c:c + 1],
                                       scalar2=None, op0=ALU.mult)
            pc2 = dstp("vector", _pclr2, after=[pc1], sem=s_v)
            pc3 = dstp("tensor", lambda t: t.matmul(
                pt32[:, 64:80], t_ones_f[:], t_clrp[:], start=True,
                stop=True), after=[pc2], sem=s_pe)

            def _pexp(a):
                a.activation(out=t_whp[:], in_=t_whp[:], func=AF.Exp)
                return a.activation(out=t_cdbp[:], in_=pt32[:, 64:80],
                                    func=AF.Exp)
            pcdb = dstp("scalar", _pexp, after=[pc3, pcum], sem=s_a)

            def _pwt(t):
                t.transpose(pt32[:, 0:16], t_whp[0:16, 0:128],
                            t_identf[0:16, 0:16])
                return t.transpose(pt32[:, 16:32], t_whp[0:16, 128:256],
                                   t_identf[0:16, 0:16])
            pwt = dstp("tensor", _pwt, after=[pcdb], sem=s_pe)

            def _pwt2(v):
                v.tensor_copy(out=t_wtmp[0][:], in_=pt32[:, 0:16])
                return v.tensor_copy(out=t_wtmp[1][:], in_=pt32[:, 16:32])
            pwt2 = dstp("vector", _pwt2, after=[pwt], sem=s_v)

            nonloc = {}

            def _pt1(t):
                m = None
                for g_ in range(4):
                    for jc in range(2):
                        m = t.transpose(
                            ptb[:, 128 * (2 * g_ + jc):
                                128 * (2 * g_ + jc + 1)],
                            t_pxb[g_][:, 128 * jc:128 * (jc + 1)],
                            t_identb[:])
                return m
            aft1 = [m_si, last_su[0], last_su[1]]
            if c + 1 in stA:
                aft1.append(stA[c + 1]["lastwave"])
            pt1 = dstp("tensor", _pt1, after=aft1, sem=s_pe)

            def _pe1(a):
                m = None
                for i in range(8):
                    m = a.activation(out=t_xctp[i][:],
                                     in_=ptb[:, 128 * i:128 * (i + 1)],
                                     func=AF.Copy)
                return m
            pe1 = dstp("scalar", _pe1, after=[pt1], sem=s_a)

            def _pt2(t):
                m = None
                for g_ in range(4):
                    for jc in range(2):
                        m = t.transpose(
                            ptb[:, 128 * (2 * g_ + jc):
                                128 * (2 * g_ + jc + 1)],
                            t_pxb[4 + g_][:, 128 * jc:128 * (jc + 1)],
                            t_identb[:])
                return m
            pt2 = dstp("tensor", _pt2, after=[pe1], sem=s_pe)

            def _pe2(v):
                m = None
                for i in range(8):
                    m = v.tensor_copy(out=t_btp[i][:],
                                      in_=ptb[:, 128 * i:128 * (i + 1)])
                return m
            pe2 = dstp("vector", _pe2, after=[pt2], sem=s_v)
            b6 = pe2

            for grp in range(4):
                gpar = grp % 2
                pSc = pb[3 + gpar]

                def _gbw(v, grp=grp):
                    m = None
                    for hh in range(4):
                        h = 4 * grp + hh
                        g_ = h // 4
                        po = 256 * hh
                        v.tensor_scalar(
                            out=t_bwp[:, po:po + 128],
                            in0=t_btp[2 * g_][:],
                            scalar1=t_wtmp[0][:, h:h + 1],
                            scalar2=None, op0=ALU.mult)
                        m = v.tensor_scalar(
                            out=t_bwp[:, po + 128:po + 256],
                            in0=t_btp[2 * g_ + 1][:],
                            scalar1=t_wtmp[1][:, h:h + 1],
                            scalar2=None, op0=ALU.mult)
                    return m
                gbw = dstp("vector", _gbw,
                           after=[pe2, pwt2, last_su[gpar]], sem=s_v)

                def _gsc(t, grp=grp, pSc=pSc):
                    m = None
                    for hh in range(4):
                        h = 4 * grp + hh
                        g_ = h // 4
                        po = 256 * hh
                        t.matmul(pSc[:, 128 * hh:128 * (hh + 1)],
                                 t_bwp[:, po:po + 128],
                                 t_xctp[2 * g_][:], start=True, stop=False)
                        m = t.matmul(pSc[:, 128 * hh:128 * (hh + 1)],
                                     t_bwp[:, po + 128:po + 256],
                                     t_xctp[2 * g_ + 1][:], start=False,
                                     stop=True)
                    return m
                gsc = dstp("tensor", _gsc, after=[gbw, last_su[gpar]],
                           sem=s_pe)

                def _gsu(v, grp=grp, pSc=pSc):
                    m = None
                    for hh in range(4):
                        h = 4 * grp + hh
                        m = v.scalar_tensor_tensor(
                            out=t_S[h][:], in0=t_S[h][:],
                            scalar=t_cdbp[:, h:h + 1],
                            in1=pSc[:, 128 * hh:128 * (hh + 1)],
                            op0=ALU.mult, op1=ALU.add)
                    return m
                gsu = dstp("vector", _gsu, after=[gsc, pcdb], sem=s_v)
                last_su[gpar] = gsu
            return b6

        t_gmb = [b.sb(f"t_gmb{i}", [128, 8192], BF16, scope=esP)
                 for i in range(2)]
        s_gam = [b.sem("s_gam0"), b.sem("s_gam1")]
        gam_red = []
        qa = []
        stage_a(qa, 0)
        b.flush_zip([qa])
        for c in range(12):
            qa, qb = [], []
            if c + 1 < 12:
                stage_a(qa, c + 1)
            last_b6 = stage_b(qb, c)
            b.flush_zip([qa, qb])
            for i in range(4 * c, 4 * c + 4):
                sl = i % 2
                aft = [gam_red[i - 2]] if i >= 2 else []
                st("gpsimd", lambda g, i=i, sl=sl: g.dma_start(
                    out=t_gmb[sl][:], in_=gam_srcs[i]),
                   after=aft, sem=s_gam[sl], k=16)
                m_l = s_gam[sl].mark()

                def _gred(a, sl=sl, i=i):
                    return a.activation(out=t_gmb[sl][:], in_=t_gmb[sl][:],
                                        func=AF.Abs,
                                        accum_out=t_gq[:, i:i + 1])
                gam_red.append(st("scalar", _gred, after=[m_l], sem=s_a))

        m_pre = st("gpsimd", lambda g: g.engine_nop(),
                   after=[(last_su[0].sem, last_su[0].val),
                          (last_su[1].sem, last_su[1].val),
                          (prev_b["tails"].sem, prev_b["tails"].val)],
                   sem=s_g)
        m_tail = m_pre
        m_gamt = emit_gamma_tail([gam_red[-1], m_pre])
        esP.close()

        if debug:
            d_sinit = b.ddbg("d_sinit", [NH * 128, 128])

            def _dbgp(sy):
                for h in range(16):
                    s_st.inc(sy.dma_start(
                        out=d_sinit[128 * h:128 * (h + 1), :],
                        in_=t_S[h][:]), 16)
                sy.wait_ge(s_st.h, s_st.n)
                return None
            st("sync", _dbgp, after=[m_pre])
            m_pre = st("gpsimd", lambda g: g.engine_nop(),
                       after=[m_pre, (s_st.mark())], sem=s_g)
            m_tail = m_pre

    if enabled("ph1"):
        esA = ExitStack()
        t_sg = [b.sb(f"t_sg{k}", [128, T], BF16, scope=esA) for k in range(16)]
        esA2 = ExitStack()
        t_xbc = [b.sb(f"t_xbc{c}", [128, T], BF16, scope=esA2)
                 for c in range(24)]
        t_bias_hm = b.sb("t_bias_hm", [16, T], scope=esA2)
        esP1 = ExitStack()
        t_x = [b.sb("t_x0", [128, HID], scope=esP1)]
        t_utok = b.sb("t_utok", [128, HID], BF16, scope=esP1)
        t_u1 = [b.sb(f"t_u1_{k}", [128, 3 + T], BF16, scope=esP1)
                for k in range(16)]
        t_wsl = [b.sb(f"t_wsl{i}", [128, 16 * 128], BF16, scope=esP1)
                 for i in range(2)]
        t_wdt = b.sb("t_wdt", [128, 256], BF16, scope=esP1)
        t_scr = [b.sb("t_scr0", [128, 3 + T], scope=esP1)]
        t_cv = t_junk[:, 0:1024]
        t_cg2 = t_x[0][:, 1024:2048]
        t_cs = [t_junk[:, 1024:2048]]
        s_x = [b.sem("s_x0")]
        s_wsl = [b.sem("s_wsl0"), b.sem("s_wsl1")]

        m_u = [None] * 9
        m_tw = [None] * 9
        prev_evac = None
        for i in range(9):
            rows = 128 if i < 8 else 3
            sl = 0
            aft_l = [m_tail] if i < 1 else [m_u[i - 1]]
            st("sync", lambda sy, i=i, sl=sl, rows=rows: sy.dma_start(
                out=t_x[sl][0:rows, :], in_=xe[128 * i:128 * i + rows, :]),
               after=aft_l, sem=s_x[sl], k=16)
            m_xl = (s_x[sl], s_x[sl].n)
            aft_x = [m_xl] + ([m_tw[i - 1]] if i >= 1 else [])
            m_u[i] = emit_ln(t_x[sl], t_utok, rows, aft_x)

            prev_w = None
            for w in range(2):
                def _tw(t, i=i, w=w, rows=rows):
                    m = None
                    for kk in range(8):
                        k = 8 * w + kk
                        m = t.transpose(
                            ptb[:, 128 * kk:128 * kk + rows],
                            t_utok[0:rows, 128 * k:128 * (k + 1)],
                            t_identb[0:rows, 0:rows])
                    return m
                aft = [m_u[i]] + ([prev_w] if prev_w else
                                  ([m_tw[i - 1]] if i >= 1 else []))
                m_t = st("tensor", _tw, after=aft, sem=s_pe)

                def _ev(v, i=i, w=w, rows=rows):
                    m = None
                    off = 128 * i
                    for kk in range(8):
                        k = 8 * w + kk
                        m = v.tensor_scalar(
                            out=t_u1[k][:, off:off + rows],
                            in0=ptb[:, 128 * kk:128 * kk + rows],
                            scalar1=t_modc[:, 16 + k:17 + k],
                            scalar2=t_modc[:, k:k + 1],
                            op0=ALU.mult, op1=ALU.add)
                    return m
                prev_w = st("vector", _ev, after=[m_t, m_modc1], sem=s_v)
            m_tw[i] = prev_w
        m_u1all = m_tw[8]

        wfree = [None, None]
        mm_ct = [None] * 41
        ev_ct = [None] * 41
        prev_silu = None
        for cs in range(40):
            sl = cs % 2
            aft_l = [wfree[sl]] if wfree[sl] else [m_tail, m_u1all]
            st("gpsimd", lambda g, cs=cs, sl=sl: g.dma_start(
                out=t_wsl[sl][:],
                in_=w_in[:, 128 * cs:128 * (cs + 1)].rearrange(
                    "(c p) w -> p c w", p=128)),
               after=aft_l, sem=s_wsl[sl], k=16)
            m_wl = (s_wsl[sl], 16 * (cs // 2 + 1))
            for j in range(1):
                ct = cs
                is_hbc = 16 <= ct < 40
                pA = pb[2 * (ct % 2)]
                pB = pb[2 * (ct % 2) + 1]
                pH = pb[4 + ct % 2]
                ph_off = 0

                def _mm(t, ct=ct, sl=sl, j=j, is_hbc=is_hbc, pA=pA, pB=pB,
                        pH=pH, ph_off=ph_off):
                    m = None
                    for k in range(16):
                        lh = t_wsl[sl][:, 128 * k:128 * (k + 1)]
                        m = t.matmul(pA[:, :], lh, t_u1[k][:, 3:515],
                                     start=(k == 0), stop=(k == 15))
                        m = t.matmul(pB[:, :], lh, t_u1[k][:, 515:1027],
                                     start=(k == 0), stop=(k == 15))
                        if is_hbc:
                            m = t.matmul(pH[:, ph_off:ph_off + 3], lh,
                                         t_u1[k][:, 0:3],
                                         start=(k == 0), stop=(k == 15))
                    return m
                aft = [m_u1all, m_wl]
                if ct >= 2:
                    aft.append(ev_ct[ct - 2])
                mm_ct[ct] = st("tensor", _mm, after=aft, sem=s_pe)

                if ct < 16:
                    def _evg(a, ct=ct, pA=pA, pB=pB):
                        a.activation(out=t_sg[ct][:, 0:512], in_=pA[:, :],
                                     func=AF.Silu)
                        return a.activation(out=t_sg[ct][:, 512:1024],
                                            in_=pB[:, :], func=AF.Silu)
                    ev_ct[ct] = st("scalar", _evg, after=[mm_ct[ct]], sem=s_a)
                else:
                    cc = ct - 16
                    ssl = 0

                    def _eva(a, ct=ct, pA=pA, pB=pB, ssl=ssl):
                        a.activation(out=t_scr[ssl][:, 3:515], in_=pA[:, :],
                                     func=AF.Copy)
                        return a.activation(out=t_scr[ssl][:, 515:1027],
                                            in_=pB[:, :], func=AF.Copy)
                    m_ea = st("scalar", _eva, after=[mm_ct[ct]], sem=s_a)

                    def _evh(v, ct=ct, ssl=ssl, pH=pH, ph_off=ph_off):
                        return v.tensor_scalar(
                            out=t_scr[ssl][:, 0:3],
                            in0=pH[:, ph_off:ph_off + 3],
                            scalar1=t_halom[:, 0:1], scalar2=None,
                            op0=ALU.mult)
                    m_eh = st("vector", _evh, after=[mm_ct[ct], m_const],
                              sem=s_v)
                    ev_ct[ct] = m_ea

                    def _cv(v, cc=cc, ssl=ssl):
                        sc = t_scr[ssl]
                        v.tensor_scalar(out=t_cv, in0=sc[:, 0:1024],
                                        scalar1=t_convw[:, 4 * cc:4 * cc + 1],
                                        scalar2=None, op0=ALU.mult)
                        v.tensor_scalar(out=t_cg2, in0=sc[:, 2:1026],
                                        scalar1=t_convw[:, 4 * cc + 2:4 * cc + 3],
                                        scalar2=None, op0=ALU.mult)
                        v.drain()
                        v.scalar_tensor_tensor(
                            out=t_cv, in0=sc[:, 1:1025],
                            scalar=t_convw[:, 4 * cc + 1:4 * cc + 2],
                            in1=t_cv, op0=ALU.mult, op1=ALU.add)
                        return v.scalar_tensor_tensor(
                            out=t_cg2, in0=sc[:, 3:1027],
                            scalar=t_convw[:, 4 * cc + 3:4 * cc + 4],
                            in1=t_cg2, op0=ALU.mult, op1=ALU.add)
                    m_cv = st("vector", _cv, after=[m_ea, m_eh], sem=s_v)

                    m_cm = st("gpsimd",
                              lambda g, ssl=ssl: g.tensor_tensor(
                                  out=t_cs[0], in0=t_cv, in1=t_cg2,
                                  op=ALU.add),
                              after=[m_cv, prev_silu], sem=s_g)
                    st("scalar",
                       lambda a, cc=cc, ssl=ssl: a.activation(
                           out=t_xbc[cc][:], in_=t_cs[0], func=AF.Silu,
                           bias=t_convb[:, cc:cc + 1]),
                       after=[m_cm], sem=s_a)
                    ev_ct[ct] = s_a.mark()
                    prev_silu = ev_ct[ct]
            wfree[sl] = ev_ct[cs]

        st("gpsimd", lambda g: g.dma_start(
            out=t_wdt[:],
            in_=w_in[:, 5120:5136].rearrange("(c p) w -> p c w", p=128)),
           after=[m_tail], sem=s_gd, k=16)
        m_wdt = s_gd.mark()

        def _mmdt(t):
            m = None
            for k in range(16):
                lh = t_wdt[:, 16 * k:16 * (k + 1)]
                m = t.matmul(pb[4][0:16, :], lh, t_u1[k][:, 3:515],
                             start=(k == 0), stop=(k == 15))
                m = t.matmul(pb[5][0:16, :], lh, t_u1[k][:, 515:1027],
                             start=(k == 0), stop=(k == 15))
            return m
        m_mmdt = st("tensor", _mmdt,
                    after=[m_u1all, m_wdt, ev_ct[39], ev_ct[40 - 2]],
                    sem=s_pe)

        def _evdt(a):
            a.activation(out=t_dthm[:, 0:512], in_=pb[4][0:16, :],
                         func=AF.Copy)
            return a.activation(out=t_dthm[:, 512:1024], in_=pb[5][0:16, :],
                                func=AF.Copy)
        m_dt0 = st("scalar", _evdt, after=[m_mmdt], sem=s_a)

        def _sp1(v):
            return v.tensor_scalar(out=t_dthm[:], in0=t_dthm[:],
                                   scalar1=t_dtbcol[0:16, 0:1], scalar2=None,
                                   op0=ALU.add)
        m_sp1 = st("vector", _sp1, after=[m_dt0, m_const], sem=s_v)

        def _sp2(a):
            a.activation(out=t_ldthm[:], in_=t_dthm[:], func=AF.Abs)
            a.drain()
            a.activation(out=t_ldthm[:], in_=t_ldthm[:],
                         func=AF.Exp, scale=-1.0)
            a.drain()
            return a.activation(out=t_ldthm[:], in_=t_ldthm[:],
                                func=AF.Ln, bias=1.0)
        m_sp2 = st("scalar", _sp2, after=[m_sp1], sem=s_a)

        def _sp3(v):
            v.tensor_scalar(out=t_junk[0:16, 0:1024], in0=t_dthm[:],
                            scalar1=0.0, scalar2=None, op0=ALU.max)
            v.drain()
            return v.tensor_tensor(out=t_dthm[:], in0=t_junk[0:16, 0:1024],
                                   in1=t_ldthm[:], op=ALU.add)
        m_sp3 = st("vector", _sp3, after=[m_sp2], sem=s_v)

        m_ldt = st("scalar", lambda a: a.activation(out=t_ldthm[:],
                                                    in_=t_dthm[:], func=AF.Ln),
                   after=[m_sp3], sem=s_a)

        def _cum(v):
            v.tensor_scalar(out=t_junk[0:16, 0:1024], in0=t_dthm[:],
                            scalar1=t_acol[0:16, 0:1], scalar2=None,
                            op0=ALU.mult)
            v.drain()
            m = None
            for c in range(4):
                m = v.tensor_tensor_scan(
                    out=t_cumhm[:, 256 * c:256 * (c + 1)],
                    data0=t_junk[0:16, 256 * c:256 * (c + 1)],
                    data1=t_z16[:, 0:256],
                    initial=0.0, op0=ALU.add, op1=ALU.bypass)
            return m
        m_cum = st("vector", _cum, after=[m_ldt, m_acol, m_nwz], sem=s_v)

        def _whm(v):
            v.tensor_tensor(out=t_bias_hm[:], in0=t_ldthm[:], in1=t_cumhm[:],
                            op=ALU.subtract)
            v.drain()
            m = None
            for c in range(4):
                m = v.tensor_scalar(
                    out=t_whm[:, 256 * c:256 * (c + 1)],
                    in0=t_bias_hm[:, 256 * c:256 * (c + 1)],
                    scalar1=t_cumhm[:, 256 * c + 255:256 * c + 256],
                    scalar2=None, op0=ALU.add)
            return m
        m_whm0 = st("vector", _whm, after=[m_cum], sem=s_v)
        m_whm = st("scalar", lambda a: a.activation(out=t_whm[:], in_=t_whm[:],
                                                    func=AF.Exp),
                   after=[m_whm0], sem=s_a)

        m_ph1 = m_whm
        m_tail = m_ph1

        if debug:
            d_u1 = b.ddbg("d_u1", [16 * 128, 3 + T], BF16)
            d_sg = b.ddbg("d_sg", [16 * 128, T], BF16)
            d_xbc = b.ddbg("d_xbc", [24 * 128, T], BF16)
            d_dtq = b.ddbg("d_dtq", [64, T])

            def _dbg1(sy):
                m = None
                for k in range(16):
                    m = sy.dma_start(out=d_u1[128 * k:128 * (k + 1), :],
                                     in_=t_u1[k][:])
                    s_st.inc(m, 16)
                for k in range(16):
                    m = sy.dma_start(out=d_sg[128 * k:128 * (k + 1), :],
                                     in_=t_sg[k][:])
                    s_st.inc(m, 16)
                for cck in range(24):
                    m = sy.dma_start(out=d_xbc[128 * cck:128 * (cck + 1), :],
                                     in_=t_xbc[cck][:])
                    s_st.inc(m, 16)
                for nm, tt in [(0, t_dthm), (1, t_ldthm), (2, t_cumhm),
                               (3, t_whm)]:
                    m = sy.dma_start(out=d_dtq[16 * nm:16 * (nm + 1), :],
                                     in_=tt[:])
                    s_st.inc(m, 16)
                sy.wait_ge(s_st.h, s_st.n)
                return None
            st("sync", _dbg1, after=[m_ph1, ev_ct[15], m_u1all])
            m_ph1 = st("gpsimd", lambda g: g.engine_nop(),
                       after=[m_ph1, (s_st, s_st.n)], sem=s_g)
            m_tail = m_ph1

        esP1.close()

    if enabled("ph2"):
        esS = ExitStack()
        t_xct = [b.sb(f"t_xct{i}", [128, 128], BF16, scope=esS)
                 for i in range(8)]
        t_bt = [b.sb(f"t_bt{i}", [128, 128], BF16, scope=esS)
                for i in range(8)]
        t_bw = b.sb("t_bw", [128, 512], BF16, scope=esS)
        t_Sb = b.sb("t_Sb", [128, 256], BF16, scope=esS)
        t_e2 = b.sb("t_e2", [128, 512], BF16, scope=esS)
        t_ep = [b.sb(f"t_ep{j}", [128, 512], scope=esS) for j in range(2)]
        t_et = b.sb("t_et", [128, 1024], scope=esS)
        t_mt = [b.sb(f"t_mt{j}", [128, 512], BF16, scope=esS)
                for j in range(2)]
        t_ce = b.sb("t_ce", [128, 512], BF16, scope=esS)
        t_xd = b.sb("t_xd", [128, 256], scope=esS)
        t_y = b.sb("t_y", [128, 256], scope=esS)
        t_g = [b.sb(f"t_g{h}", [128, 256], BF16, scope=esS)
               for h in range(NH)]
        t_sq = b.sb("t_sq", [128, 256], scope=esS)
        t_rhm = b.sb("t_rhm", [16, 256], scope=esS)
        t_biastm = [b.sb(f"t_biastm{j}", [128, 16], scope=esS)
                    for j in range(2)]
        t_wtm = [b.sb(f"t_wtm{j}", [128, 16], scope=esS) for j in range(2)]
        t_cdb = b.sb("t_cdb", [128, 16], scope=esS)
        t_cumlr = b.sb("t_cumlr", [1, 16], scope=esS)
        t_gn = t_sg
        t_dI = [b.sb(f"t_dI{h}", [128, 128], BF16, scope=esS)
                for h in range(NH)]

        def _dib(v):
            m = None
            for h in range(NH):
                m = v.tensor_scalar(out=t_dI[h][:], in0=t_identb[:],
                                    scalar1=t_db[:, h:h + 1], scalar2=None,
                                    op0=ALU.mult)
            return m
        m_dI = st("vector", _dib, after=[m_const, m_db, m_tail], sem=s_v)

        if enabled("pre"):
            m_sz = m_tail
        else:
            def _sz(g):
                for h in range(NH):
                    g.memset(t_S[h][:], 0.0)
                    g.drain()
                return g.engine_nop()
            m_sz = st("gpsimd", _sz, after=[m_tail], sem=s_g)

        m_prev_y = m_tail
        m_prev_mt = m_tail
        m_prev_bc = [m_tail, m_tail]
        m_prev_sc = m_tail
        m_prev_gn = m_tail
        m_prev_gne = m_tail
        m_tr_free = m_tail
        for c in range(4):
            c0 = 256 * c
            def _tr1(t, c0=c0):
                m = None
                for g_ in range(4):
                    for jc in range(2):
                        m = t.transpose(
                            ptb[:, 128 * (2 * g_ + jc):128 * (2 * g_ + jc + 1)],
                            t_xbc[g_][:, c0 + 128 * jc:c0 + 128 * (jc + 1)],
                            t_identb[:])
                return m
            m_t1 = st("tensor", _tr1, after=[m_tail, m_prev_y, m_tr_free,
                                             m_prev_gn], sem=s_pe)

            def _te1(v):
                m = None
                for i in range(8):
                    m = v.tensor_copy(out=t_xct[i][:],
                                      in_=ptb[:, 128 * i:128 * (i + 1)])
                return m
            m_e1 = st("vector", _te1, after=[m_t1], sem=s_v)

            def _tr2(t, c0=c0):
                m = None
                for g_ in range(4):
                    for jc in range(2):
                        m = t.transpose(
                            ptb[:, 128 * (2 * g_ + jc):128 * (2 * g_ + jc + 1)],
                            t_xbc[4 + g_][:, c0 + 128 * jc:c0 + 128 * (jc + 1)],
                            t_identb[:])
                return m
            m_t2 = st("tensor", _tr2, after=[m_e1], sem=s_pe)

            def _te2(v):
                m = None
                for i in range(8):
                    m = v.tensor_copy(out=t_bt[i][:],
                                      in_=ptb[:, 128 * i:128 * (i + 1)])
                return m
            m_e2m = st("vector", _te2, after=[m_t2], sem=s_v)

            def _tr3(t, c0=c0):
                t.transpose(pt32[:, 0:16],
                            t_bias_hm[0:16, c0:c0 + 128], t_identf[0:16, 0:16])
                t.transpose(pt32[:, 16:32],
                            t_bias_hm[0:16, c0 + 128:c0 + 256],
                            t_identf[0:16, 0:16])
                t.transpose(pt32[:, 32:48],
                            t_whm[0:16, c0:c0 + 128], t_identf[0:16, 0:16])
                t.transpose(pt32[:, 48:64],
                            t_whm[0:16, c0 + 128:c0 + 256],
                            t_identf[0:16, 0:16])
                return t.transpose(pt32[0:1, 64:80],
                                   t_cumhm[0:16, c0 + 255:c0 + 256],
                                   t_identf[0:16, 0:16])
            m_t3 = st("tensor", _tr3, after=[m_tail, m_e2m], sem=s_pe)

            def _te3(v):
                v.tensor_copy(out=t_biastm[0][:], in_=pt32[:, 0:16])
                v.tensor_copy(out=t_biastm[1][:], in_=pt32[:, 16:32])
                v.tensor_copy(out=t_wtm[0][:], in_=pt32[:, 32:48])
                v.tensor_copy(out=t_wtm[1][:], in_=pt32[:, 48:64])
                return v.tensor_copy(out=t_cumlr[:], in_=pt32[0:1, 64:80])
            m_e3 = st("vector", _te3, after=[m_t3], sem=s_v)

            m_cdb0 = st("tensor", lambda t: t.matmul(
                pt32[:, 64:80], t_ones_f[:], t_cumlr[:],
                start=True, stop=True), after=[m_e3], sem=s_pe)
            m_cdb = st("scalar", lambda a: a.activation(
                out=t_cdb[:], in_=pt32[:, 64:80], func=AF.Exp),
                after=[m_cdb0], sem=s_a)

            m_ss = m_prev_gn
            m_mtm_p = [m_prev_mt, m_prev_mt]
            m_bc_p = [m_prev_bc[0], m_prev_bc[1]]
            m_y_p = [m_prev_y, m_prev_y]
            m_sc_p = [m_prev_sc, m_prev_sc]
            m_et_prev = m_prev_bc[0]
            m_e2_prev = [m_prev_bc[0], m_prev_bc[1]]
            for h in range(NH):
                g_ = h // 4
                par = h % 2
                pMt = pb[par]
                pBc = pb[2 + par]
                pYS = pb[4 + par]
                yo = 0
                so = 256

                def _mt(t, h=h, g_=g_, c0=c0, pMt=pMt):
                    m = None
                    for jc in range(2):
                        m = t.matmul(
                            pMt[:, 256 * jc:256 * (jc + 1)],
                            t_xbc[4 + g_][:, c0 + 128 * jc:c0 + 128 * (jc + 1)],
                            t_xbc[8 + h][:, c0:c0 + 256],
                            start=True, stop=True)
                    return m
                m_mtp = st("tensor", _mt, after=[m_mtm_p[par], m_e2m],
                           sem=s_pe)

                m_bcp = st("tensor", lambda t, h=h, c0=c0, pBc=pBc: t.matmul(
                    pBc[:, 0:256], t_onesel[0:16, 128 * h:128 * (h + 1)],
                    t_cumhm[0:16, c0:c0 + 256], start=True, stop=True),
                    after=[m_bc_p[par], m_e3], sem=s_pe)

                m_e2b = st("scalar", lambda a, pBc=pBc, par=par: a.activation(
                    out=t_e2[:, 256 * par:256 * (par + 1)], in_=pBc[:, 0:256],
                    func=AF.Exp),
                    after=[m_bcp, m_e2_prev[par], m_mtm_p[par]], sem=s_a)
                m_e2_prev[par] = m_e2b

                def _ep(v, pBc=pBc, par=par):
                    v.tensor_tensor(out=t_ep[par][:, 0:256],
                                    in0=pBc[:, 0:256],
                                    in1=t_masklog[:, 0:256], op=ALU.add)
                    return v.tensor_tensor(out=t_ep[par][:, 256:512],
                                           in0=pBc[:, 0:256],
                                           in1=t_masklog[:, 256:512],
                                           op=ALU.add)
                m_ep = st("vector", _ep, after=[m_bcp], sem=s_v)
                m_bc_p[par] = st("gpsimd", lambda g: g.engine_nop(),
                                 after=[m_ep, m_e2b], sem=s_g)

                def _et(a, h=h, par=par):
                    a.activation(out=t_et[:, 512 * par:512 * par + 256],
                                 in_=t_ep[par][:, 0:256],
                                 func=AF.Exp, bias=t_biastm[0][:, h:h + 1])
                    return a.activation(
                        out=t_et[:, 512 * par + 256:512 * (par + 1)],
                        in_=t_ep[par][:, 256:512],
                        func=AF.Exp, bias=t_biastm[1][:, h:h + 1])
                m_et = st("scalar", _et,
                          after=[m_ep, m_et_prev, m_mtm_p[par]], sem=s_a)
                m_et_prev = m_et

                def _v2(v, h=h, g_=g_, c0=c0, par=par, pMt=pMt):
                    v.tensor_tensor(out=t_mt[0][:, 256 * par:256 * (par + 1)],
                                    in0=pMt[:, 0:256],
                                    in1=t_et[:, 512 * par:512 * par + 256],
                                    op=ALU.mult)
                    v.tensor_tensor(out=t_mt[1][:, 256 * par:256 * (par + 1)],
                                    in0=pMt[:, 256:512],
                                    in1=t_et[:, 512 * par + 256:512 * (par + 1)],
                                    op=ALU.mult)
                    v.tensor_tensor(out=t_ce[:, 256 * par:256 * (par + 1)],
                                    in0=t_xbc[8 + h][:, c0:c0 + 256],
                                    in1=t_e2[:, 256 * par:256 * (par + 1)],
                                    op=ALU.mult)
                    v.tensor_copy(out=t_Sb[:, 128 * par:128 * (par + 1)],
                                  in_=t_S[h][:])
                    v.tensor_scalar(out=t_bw[:, 256 * par:256 * par + 128],
                                    in0=t_bt[2 * g_][:],
                                    scalar1=t_wtm[0][:, h:h + 1], scalar2=None,
                                    op0=ALU.mult)
                    return v.tensor_scalar(
                        out=t_bw[:, 256 * par + 128:256 * (par + 1)],
                        in0=t_bt[2 * g_ + 1][:],
                        scalar1=t_wtm[1][:, h:h + 1], scalar2=None,
                        op0=ALU.mult)
                aft_v2 = [m_et, m_mtp, m_e2b, m_mtm_p[par], m_sc_p[par]]
                if c == 0 and h < 2:
                    aft_v2.append(m_sz)
                m_v2 = st("vector", _v2, after=aft_v2, sem=s_v)
                m_mtm_p[par] = m_v2

                def _y(t, h=h, g_=g_, c0=c0, par=par, pYS=pYS, yo=yo, so=so):
                    t.matmul(pYS[:, yo:yo + 256], t_xct[2 * g_][:],
                             t_mt[0][:, 256 * par:256 * (par + 1)],
                             start=True, stop=False)
                    t.matmul(pYS[:, yo:yo + 256], t_xct[2 * g_ + 1][:],
                             t_mt[1][:, 256 * par:256 * (par + 1)],
                             start=False, stop=False)
                    t.matmul(pYS[:, yo:yo + 256],
                             t_Sb[:, 128 * par:128 * (par + 1)],
                             t_ce[:, 256 * par:256 * (par + 1)],
                             start=False, stop=False)
                    t.matmul(pYS[:, yo:yo + 256], t_dI[h][:],
                             t_xbc[g_][:, c0:c0 + 256],
                             start=False, stop=True)
                    t.matmul(pYS[:, so:so + 128],
                             t_bw[:, 256 * par:256 * par + 128],
                             t_xct[2 * g_][:], start=True, stop=False)
                    return t.matmul(pYS[:, so:so + 128],
                                    t_bw[:, 256 * par + 128:256 * (par + 1)],
                                    t_xct[2 * g_ + 1][:], start=False,
                                    stop=True)
                m_yp = st("tensor", _y,
                          after=[m_v2, m_y_p[par], m_sc_p[par], m_e1] +
                          ([m_dI] if (c == 0 and h == 0) else []), sem=s_pe)

                def _v3(v, h=h, g_=g_, c0=c0, par=par, pYS=pYS, yo=yo,
                        so=so):
                    v.tensor_tensor(out=t_g[h][:], in0=pYS[:, yo:yo + 256],
                                    in1=t_sg[h][:, c0:c0 + 256],
                                    op=ALU.mult)
                    v.scalar_tensor_tensor(
                        out=t_S[h][:], in0=t_S[h][:],
                        scalar=t_cdb[:, h:h + 1], in1=pYS[:, so:so + 128],
                        op0=ALU.mult, op1=ALU.add)
                    v.drain()
                    return v.tensor_tensor(out=t_sq[:], in0=t_g[h][:],
                                           in1=t_g[h][:], op=ALU.mult)
                m_v3 = st("vector", _v3, after=[m_yp, m_cdb, m_ss], sem=s_v)
                m_y_p[par] = m_v3
                m_sc_p[par] = m_v3

                m_ss = st("tensor", lambda t, h=h: t.matmul(
                    pt32[0:16, 256:512], t_oco[:, 16 * h:16 * (h + 1)],
                    t_sq[:], start=(h == 0), stop=(h == 15)),
                    after=[m_v3] + ([m_prev_gn] if h == 0 else []),
                    sem=s_pe)
            m_prev_mt = m_mtm_p[1]
            m_prev_bc = m_bc_p
            m_prev_y = m_y_p[1]
            m_prev_sc = m_sc_p[1]
            m_bw = m_v2
            m_tr_free = m_bw

            def _rhm(a):
                a.activation(out=t_rhm[:], in_=pt32[0:16, 256:512],
                             func=AF.Ln, scale=1.0 / 128.0,
                             bias=t_eps[0:16, 0:1])
                a.drain()
                return a.activation(out=t_rhm[:], in_=t_rhm[:],
                                    func=AF.Exp, scale=-0.5)
            m_rhm = st("scalar", _rhm, after=[m_ss], sem=s_a)

            m_gne_p = [m_prev_gne, m_prev_gne]
            for h in range(NH):
                par = h % 2
                m_gnp = st("tensor", lambda t, h=h, par=par: t.matmul(
                    pb[par][:, 0:256], t_nwsel[0:16, 128 * h:128 * (h + 1)],
                    t_rhm[0:16, 0:256], start=True, stop=True),
                    after=[m_rhm, m_gne_p[par], m_prev_mt], sem=s_pe)
                m_gne = st("vector", lambda v, h=h, c0=c0, par=par:
                           v.tensor_tensor(
                               out=t_gn[h][:, c0:c0 + 256], in0=t_g[h][:],
                               in1=pb[par][:, 0:256], op=ALU.mult),
                           after=[m_gnp], sem=s_v)
                m_gne_p[par] = m_gne
            m_prev_gne = st("gpsimd", lambda g: g.engine_nop(),
                            after=[m_gne_p[0], m_gne_p[1]], sem=s_g)
            m_prev_gn = m_prev_gne
            m_prev_mt = m_prev_gne
            m_prev_bc = [m_prev_gne, m_prev_gne]

        esS.close()
        m_ph2 = m_prev_gne
        m_tail = m_ph2

        if debug:
            d_gn = b.ddbg("d_gn", [16 * 128, T], BF16)
            d_S = b.ddbg("d_S", [16 * 128, 128])

            def _dbg2(sy):
                for k in range(16):
                    s_st.inc(sy.dma_start(
                        out=d_gn[128 * k:128 * (k + 1), :], in_=t_gn[k][:]),
                        16)
                for h in range(16):
                    s_st.inc(sy.dma_start(
                        out=d_S[128 * h:128 * (h + 1), :], in_=t_S[h][:]), 16)
                sy.wait_ge(s_st.h, s_st.n)
                return None
            st("sync", _dbg2, after=[m_ph2, m_prev_sc])

    if enabled("ph3"):
        esA2.close()
        es3 = ExitStack()
        t_wo = [b.sb(f"t_wo{k}", [128, HID], BF16, scope=es3)
                for k in range(16)]
        t_x3 = [b.sb(f"t_x3_{i}", [128, HID], scope=es3) for i in range(2)]
        t_ev3 = b.sb("t_ev3", [128, 1024], scope=es3)
        t_gar = b.sb("t_gar", [1, HID], BF16, scope=es3)
        t_gaj = b.sb("t_gaj", [1, HID], scope=es3)
        t_gab = b.sb("t_gab", [128, HID], BF16, scope=es3)
        s_wo = b.sem("s_wo")
        s_x3 = [b.sem("s_x3_0"), b.sem("s_x3_1")]
        s_xss = [b.sem("s_xss0"), b.sem("s_xss1")]

        st("sync", lambda sy: sy.dma_start(
            out=t_gaj[:], in_=modsc[0:1, 2 * 2048:3 * 2048]),
           after=[m_tail], sem=s_hd, k=16)
        m_gl = s_hd.mark()
        def _gs(a):
            a.activation(out=t_gar[:], in_=t_gaj[:], func=AF.Sign)
            return a.activation(out=t_gaj[:], in_=t_gaj[:], func=AF.Abs)
        m_gs = st("scalar", _gs, after=[m_gl], sem=s_a)

        def _gm(v):
            v.tensor_scalar(out=t_gaj[:], in0=t_gaj[:], scalar1=0.25,
                            scalar2=None, op0=ALU.max)
            v.drain()
            return v.tensor_tensor(out=t_gar[:], in0=t_gaj[:], in1=t_gar[:],
                                   op=ALU.mult)
        m_gm = st("vector", _gm, after=[m_gs], sem=s_v)
        m_gab3 = bcast_row(t_gar, t_gab, HID, True, [m_gm, m_tail])

        for k in range(16):
            st("gpsimd", lambda g, k=k: g.dma_start(
                out=t_wo[k][:], in_=w_out[128 * k:128 * (k + 1), :]),
               after=([m_tail] if k == 0 else []), sem=s_wo, k=16)
        m_wol = s_wo.mark()

        m_ev_prev = [m_gab3, m_gab3]
        m_xs_done = []
        for tc in range(8):
            sl = tc % 2
            aft = [m_tail] if tc < 2 else [m_xs_done[tc - 2]]
            st("sync", lambda sy, tc=tc, sl=sl: sy.dma_start(
                out=t_x3[sl][:], in_=xe[3 + 128 * tc:3 + 128 * (tc + 1), :]),
               after=aft, sem=s_x3[sl], k=16)
            m_xl = (s_x3[sl], s_x3[sl].n)
            for half in range(2):
                pA, pB = (pb[0], pb[1]) if half == 0 else (pb[2], pb[3])

                def _wo(t, tc=tc, half=half, pA=pA, pB=pB):
                    m = None
                    for k in range(16):
                        lh = t_gn[k][:, 128 * tc:128 * (tc + 1)]
                        m = t.matmul(pA[:, :], lh,
                                     t_wo[k][:, 1024 * half:1024 * half + 512],
                                     start=(k == 0), stop=(k == 15))
                        m = t.matmul(pB[:, :], lh,
                                     t_wo[k][:, 1024 * half + 512:
                                             1024 * (half + 1)],
                                     start=(k == 0), stop=(k == 15))
                    return m
                m_mm = st("tensor", _wo,
                          after=[m_wol, m_tail, m_ev_prev[half]], sem=s_pe)

                def _ev(v, tc=tc, half=half, pA=pA, pB=pB, sl=sl):
                    m = None
                    for j, pX in enumerate([pA, pB]):
                        off = 1024 * half + 512 * j
                        v.tensor_tensor(out=t_ev3[:, 512 * j:512 * (j + 1)],
                                        in0=pX[:, :],
                                        in1=t_gab[:, off:off + 512],
                                        op=ALU.mult)
                    v.drain()
                    for j in range(2):
                        off = 1024 * half + 512 * j
                        m = v.tensor_tensor(out=t_x3[sl][:, off:off + 512],
                                            in0=t_x3[sl][:, off:off + 512],
                                            in1=t_ev3[:, 512 * j:512 * (j + 1)],
                                            op=ALU.add)
                    return m
                m_ev_prev[half] = st("vector", _ev,
                                     after=[m_mm, m_xl, m_gab3], sem=s_v)
            st("sync", lambda sy, tc=tc, sl=sl: sy.dma_start(
                out=xsc[128 * tc:128 * (tc + 1), :], in_=t_x3[sl][:]),
               after=[m_ev_prev[0], m_ev_prev[1]], sem=s_xss[sl], k=16)
            m_xs_done.append((s_xss[sl], s_xss[sl].n))

        m_ph3 = st("gpsimd", lambda g: g.engine_nop(),
                   after=[(s_xss[0], s_xss[0].n), (s_xss[1], s_xss[1].n),
                          m_ev_prev[0], m_ev_prev[1]], sem=s_g)
        m_tail = m_ph3

        if debug:
            d_x3 = b.ddbg("d_x3", [T, HID])

            def _dbg3(sy):
                m = sy.dma_start(out=d_x3, in_=xsc)
                s_st.inc(m, 16)
                sy.wait_ge(s_st.h, s_st.n)
                return None
            st("sync", _dbg3, after=[m_ph3])
            m_ph3 = st("gpsimd", lambda g: g.engine_nop(),
                       after=[m_ph3, (s_st, s_st.n)], sem=s_g)
            m_tail = m_ph3

        es3.close()
        esA.close()

    if enabled("ph4"):
        esL.close()
        tern_g = b.dint("tern_g", [64, 128, 2048], BF16)
        tern_v = b.dint("tern_v", [64, 128, 2048], BF16)
        tern_o = b.dint("tern_o", [16, 4, 128, 2048], BF16)
        es4 = ExitStack()
        t_gbb4 = b.sb("t_gbb4", [128, HID], BF16, scope=es4)
        t_r4 = b.sb("t_r4", [1, HID], scope=es4)
        t_amx = b.sb("t_amx", [128, 512], scope=es4)
        t_amn = b.sb("t_amn", [128, 512], scope=es4)
        t_amc = b.sb("t_amc", [128, 8], scope=es4)
        t_srow = b.sb("t_srow", [1, 1024], scope=es4)
        t_s1b = b.sb("t_s1b", [128, 512], scope=es4)
        t_s1ib = b.sb("t_s1ib", [128, 512], scope=es4)
        t_s2b = t_s1b
        t_s2ib = t_s1ib
        t_qq = b.sb("t_qq", [128, 512], BF16, scope=es4)
        t_vv = b.sb("t_vv", [128, 512], BF16, scope=es4)
        t_e1 = t_s1b
        t_stg = [b.sb(f"t_stg{i}", [128, 2048], scope=es4) for i in range(4)]
        t_q3 = [b.sb(f"t_q3_{i}", [128, 2048], BF16, scope=es4)
                for i in range(4)]
        t_xdq = [b.sb(f"t_xdq{k}", [128, 512], BF16, scope=es4)
                 for k in range(16)]
        s_stg = [b.sem(f"s_stg{i}") for i in range(4)]
        s_tn = [b.sem(f"s_tn{i}") for i in range(4)]
        s_x4 = [b.sem(f"s_x4_{i}") for i in range(4)]
        s_sto = b.sem("s_sto")

        t_ob4 = b.sb("t_ob4", [1, HID], scope=es4)
        st("sync", lambda sy: sy.dma_start(
            out=t_r4[:], in_=modsc[0:1, 5 * 2048:6 * 2048]),
           after=[m_tail], sem=s_hd, k=16)
        st("sync", lambda sy: sy.dma_start(out=t_ob4[:], in_=obrow),
           sem=s_hd, k=16)
        m_r4 = None

        st("sync", lambda sy: sy.dma_start(
            out=t_gbgc[:],
            in_=modsc[0:1, 5 * 2048:6 * 2048].rearrange(
                "a (k p) -> p a k", p=128).rearrange("p a k -> (p a) k"),
            allow_slow_non_contiguous=True),
           after=[m_tail], sem=s_hd, k=16)
        m_gbc0 = s_hd.mark()
        m_r4 = m_gbc0

        def _gb4(a):
            a.activation(out=t_gbb4[0:1, 0:HID], in_=t_r4[:], func=AF.Sign)
            return a.activation(out=t_r4[:], in_=t_r4[:], func=AF.Abs)
        m_gb4a = st("scalar", _gb4, after=[m_r4], sem=s_a)

        def _gb4b(v):
            v.tensor_scalar(out=t_r4[:], in0=t_r4[:], scalar1=0.25,
                            scalar2=None, op0=ALU.max)
            v.drain()
            v.tensor_tensor(out=t_r4[:], in0=t_r4[:],
                            in1=t_gbb4[0:1, 0:HID], op=ALU.mult)
            v.drain()
            return v.tensor_tensor(out=t_r4[:], in0=t_r4[:], in1=t_ob4[:],
                                   op=ALU.mult)
        m_gb4b = st("vector", _gb4b, after=[m_gb4a], sem=s_v)
        m_gbb4 = bcast_row(t_r4, t_gbb4, HID, False, [m_gb4b, m_tail])


        m_gbc1a = st("scalar", lambda a: a.activation(
            out=t_col[:, 0:16], in_=t_gbgc[:], func=AF.Abs),
            after=[m_gbc0], sem=s_a)
        m_gbc2 = st("scalar", lambda a: a.activation(
            out=t_gbgc[:], in_=t_gbgc[:], func=AF.Sign),
            after=[m_gbc1a], sem=s_a)

        def _gbc3(v):
            v.tensor_scalar(out=t_col[:, 0:16], in0=t_col[:, 0:16],
                            scalar1=0.25, scalar2=None, op0=ALU.max)
            v.drain()
            v.tensor_tensor(out=t_gbgc[:], in0=t_gbgc[:], in1=t_col[:, 0:16],
                            op=ALU.mult)
            v.drain()
            return v.tensor_scalar(out=t_gbgc[:], in0=t_gbgc[:],
                                   scalar1=t_gamma[:, 10:11], scalar2=None,
                                   op0=ALU.mult)
        m_gbgc = st("vector", _gbc3, after=[m_gbc2, m_gamt], sem=s_v)

        m_tg_done = m_gbb4
        for tg in range(2):
            tg0 = 512 * tg
            es4a = ExitStack()
            t_h2 = [b.sb(f"t_h2_{tg}_{k}", [128, 512], scope=es4a)
                    for k in range(16)]
            t_x4a = [b.sb(f"t_x4a{tg}_{i}", [128, HID], scope=es4a)
                     for i in range(2)]
            t_ut4 = b.sb(f"t_ut4_{tg}", [128, HID], BF16, scope=es4a)

            m_u4 = [None] * 4
            m_tw4 = [None] * 4
            for i in range(4):
                sl = i % 2
                aft_l = [m_tg_done] if i < 2 else [m_u4[i - 2]]
                st("sync", lambda sy, i=i, sl=sl, tg0=tg0: sy.dma_start(
                    out=t_x4a[sl][:],
                    in_=xsc[tg0 + 128 * i:tg0 + 128 * (i + 1), :]),
                   after=aft_l, sem=s_x4[sl], k=16)
                m_xl = (s_x4[sl], s_x4[sl].n)
                aft_x = [m_xl] + ([m_tw4[i - 1]] if i >= 1 else [])
                m_u4[i] = emit_ln(t_x4a[sl], t_ut4, 128, aft_x)
                prev_w = None
                for w in range(2):
                    def _tw(t, w=w):
                        m = None
                        for kk in range(8):
                            k = 8 * w + kk
                            m = t.transpose(
                                ptb[:, 128 * kk:128 * (kk + 1)],
                                t_ut4[:, 128 * k:128 * (k + 1)],
                                t_identb[:])
                        return m
                    aft = [m_u4[i]] + ([prev_w] if prev_w else
                                       ([m_tw4[i - 1]] if i >= 1 else
                                        [m_tg_done]))
                    m_t = st("tensor", _tw, after=aft, sem=s_pe)

                    def _ev(v, i=i, w=w):
                        m = None
                        for kk in range(8):
                            k = 8 * w + kk
                            m = v.tensor_scalar(
                                out=t_h2[k][:, 128 * i:128 * (i + 1)],
                                in0=ptb[:, 128 * kk:128 * (kk + 1)],
                                scalar1=t_modc[:, 48 + k:49 + k],
                                scalar2=t_modc[:, 32 + k:33 + k],
                                op0=ALU.mult, op1=ALU.add)
                            v.drain()
                        return m
                    prev_w = st("vector", _ev, after=[m_t, m_modc1], sem=s_v)
                m_tw4[i] = prev_w
            m_h2 = m_tw4[3]

            def _tree(v):
                v.tensor_copy(out=t_amx[:], in_=t_h2[0][:])
                v.drain()
                v.tensor_copy(out=t_amn[:], in_=t_h2[0][:])
                v.drain()
                for k in range(1, 16):
                    v.tensor_tensor(out=t_amx[:], in0=t_amx[:],
                                    in1=t_h2[k][:], op=ALU.max)
                    v.drain()
                    v.tensor_tensor(out=t_amn[:], in0=t_amn[:],
                                    in1=t_h2[k][:], op=ALU.min)
                    v.drain()
                v.tensor_scalar(out=t_amn[:], in0=t_amn[:], scalar1=-1.0,
                                scalar2=None, op0=ALU.mult)
                v.drain()
                v.tensor_tensor(out=t_amx[:], in0=t_amx[:], in1=t_amn[:],
                                op=ALU.max)
                v.drain()
                return v.engine_nop()
            m_tree = st("vector", _tree, after=[m_h2], sem=s_v)

            def _tp1(t):
                m = None
                for blk in range(4):
                    m = t.transpose(pt32[:, 128 * blk:128 * (blk + 1)],
                                    t_amx[:, 128 * blk:128 * (blk + 1)],
                                    t_identf[:])
                return m
            m_tp1 = st("tensor", _tp1, after=[m_tree, m_tg_done], sem=s_pe)

            def _red(v):
                for blk in range(4):
                    v.tensor_reduce(out=t_amc[:, blk:blk + 1],
                                    in_=pt32[:, 128 * blk:128 * (blk + 1)],
                                    axis=AXX, op=ALU.max)
                v.drain()
                v.tensor_scalar(out=t_amc[:, 0:4], in0=t_amc[:, 0:4],
                                scalar1=1e-5, scalar2=None, op0=ALU.max)
                v.drain()
                return v.reciprocal(out=t_amc[:, 4:8], in_=t_amc[:, 0:4])
            m_red = st("vector", _red, after=[m_tp1], sem=s_v)

            def _tp2(t):
                m = None
                for blk in range(4):
                    m = t.transpose(pt32[0:1, 128 * blk:128 * (blk + 1)],
                                    t_amc[:, 4 + blk:5 + blk], t_identf[:])
                return m
            m_tp2 = st("tensor", _tp2, after=[m_red], sem=s_pe)
            m_sr1 = st("vector", lambda v: v.tensor_copy(
                out=t_srow[0:1, 0:512], in_=pt32[0:1, 0:512]),
                after=[m_tp2], sem=s_v)

            def _tp3(t):
                m = None
                for blk in range(4):
                    m = t.transpose(pt32[0:1, 128 * blk:128 * (blk + 1)],
                                    t_amc[:, blk:blk + 1], t_identf[:])
                return m
            m_tp3 = st("tensor", _tp3, after=[m_sr1], sem=s_pe)
            m_sr2 = st("vector", lambda v: v.tensor_copy(
                out=t_srow[0:1, 512:1024], in_=pt32[0:1, 0:512]),
                after=[m_tp3], sem=s_v)

            m_b1 = st("tensor", lambda t: t.matmul(
                pt32[:, 0:512], t_ones_f[:], t_srow[0:1, 0:512],
                start=True, stop=True), after=[m_sr2], sem=s_pe)
            m_s1b = st("vector", lambda v: v.tensor_scalar(
                out=t_s1b[:], in0=pt32[:, 0:512], scalar1=127.0, scalar2=None,
                op0=ALU.mult), after=[m_b1], sem=s_v)
            m_b2 = st("tensor", lambda t: t.matmul(
                pt32[:, 0:512], t_ones_f[:], t_srow[0:1, 512:1024],
                start=True, stop=True), after=[m_s1b], sem=s_pe)
            m_s1ib = st("vector", lambda v: v.tensor_scalar(
                out=t_s1ib[:], in0=pt32[:, 0:512], scalar1=1.0 / 127.0,
                scalar2=None, op0=ALU.mult), after=[m_b2], sem=s_v)

            qsc = ((t_amx, t_qq), (t_amn, t_vv))
            m_xdq_all = None
            for kp in range(8):
                def _q(v, kp=kp):
                    for j, (sa, sq) in enumerate(qsc):
                        v.tensor_tensor(out=sa[:], in0=t_h2[2 * kp + j][:],
                                        in1=t_s1b[:], op=ALU.mult)
                    v.drain()
                    for j, (sa, sq) in enumerate(qsc):
                        v.tensor_scalar(out=sq[:], in0=sa[:], scalar1=MAGIC,
                                        scalar2=-MAGIC, op0=ALU.add,
                                        op1=ALU.add)
                    v.drain()
                    m = None
                    for j, (sa, sq) in enumerate(qsc):
                        m = v.tensor_tensor(out=t_xdq[2 * kp + j][:],
                                            in0=sq[:], in1=t_s1ib[:],
                                            op=ALU.mult)
                    return m
                m_xdq_all = st("vector", _q, after=[m_s1b, m_s1ib], sem=s_v)

            es4a.close()
            es4b = ExitStack()
            t_gv = [b.sb(f"t_gv{tg}_{h}", [128, 512], BF16, scope=es4b)
                    for h in range(64)]
            t_x4 = [b.sb(f"t_x4_{tg}_{i}", [128, HID], scope=es4b)
                    for i in range(4)]

            GV = {
                "g": dict(stg=(t_stg[0], t_stg[1]), q3=(t_q3[0], t_q3[1]),
                          src=gwB, tdr=tern_g,
                          ssem=(s_stg[0], s_stg[1]), tsem=(s_tn[0], s_tn[1]),
                          ghi=t_gamma[:, 0:1], glo=t_gamma[:, 4:5]),
                "v": dict(stg=(t_stg[2], t_stg[3]), q3=(t_q3[2], t_q3[3]),
                          src=vwB, tdr=tern_v,
                          ssem=(s_stg[2], s_stg[3]), tsem=(s_tn[2], s_tn[3]),
                          ghi=t_gamma[:, 1:2], glo=t_gamma[:, 5:6]),
            }
            ST = {k: dict(stgfree=[m_xdq_all, m_xdq_all],
                          q3free=[m_xdq_all, m_xdq_all],
                          ld={}, tn={}, wr={}) for k in GV}

            def eload(k, h):
                d, s = GV[k], ST[k]
                sl = h % 2
                st("sync", lambda sy, d=d, h=h, sl=sl: sy.dma_start(
                    out=d["stg"][sl][:], in_=d["src"][h]),
                   after=[s["stgfree"][sl]], sem=d["ssem"][sl], k=16)
                s["ld"][h] = d["ssem"][sl].mark()

            def etern(k, h):
                d, s = GV[k], ST[k]
                sl = h % 2

                def _tq(v, d=d, sl=sl):
                    v.tensor_scalar(out=d["q3"][sl][:], in0=d["stg"][sl][:],
                                    scalar1=d["glo"], scalar2=None,
                                    op0=ALU.is_lt)
                    v.drain()
                    return v.scalar_tensor_tensor(
                        out=d["q3"][sl][:], in0=d["stg"][sl][:],
                        scalar=d["ghi"], in1=d["q3"][sl][:],
                        op0=ALU.is_gt, op1=ALU.subtract)
                m1 = st("vector", _tq,
                        after=[s["ld"][h], s["q3free"][sl]], sem=s_v)
                s["stgfree"][sl] = m1
                s["tn"][h] = m1
                st("sync", lambda sy, d=d, h=h, sl=sl: sy.dma_start(
                    out=d["tdr"][h], in_=d["q3"][sl][:]),
                   after=[m1], sem=d["tsem"][sl], k=16)
                s["wr"][h] = d["tsem"][sl].mark()

            def eload_t(k, h):
                d, s = GV[k], ST[k]
                sl = h % 2
                st("sync", lambda sy, d=d, h=h, sl=sl: sy.dma_start(
                    out=d["q3"][sl][:], in_=d["tdr"][h]),
                   after=[s["q3free"][sl]], sem=d["tsem"][sl], k=16)
                s["tn"][h] = d["tsem"][sl].mark()

            def etern_gv(h):
                sl = h % 2
                dg, dv = GV["g"], GV["v"]

                def _tq(v, sl=sl):
                    v.tensor_scalar(out=dg["q3"][sl][:], in0=dg["stg"][sl][:],
                                    scalar1=dg["glo"], scalar2=None,
                                    op0=ALU.is_lt)
                    v.tensor_scalar(out=dv["q3"][sl][:], in0=dv["stg"][sl][:],
                                    scalar1=dv["glo"], scalar2=None,
                                    op0=ALU.is_lt)
                    v.drain()
                    v.scalar_tensor_tensor(
                        out=dg["q3"][sl][:], in0=dg["stg"][sl][:],
                        scalar=dg["ghi"], in1=dg["q3"][sl][:],
                        op0=ALU.is_gt, op1=ALU.subtract)
                    return v.scalar_tensor_tensor(
                        out=dv["q3"][sl][:], in0=dv["stg"][sl][:],
                        scalar=dv["ghi"], in1=dv["q3"][sl][:],
                        op0=ALU.is_gt, op1=ALU.subtract)
                m1 = st("vector", _tq,
                        after=[ST["g"]["ld"][h], ST["v"]["ld"][h],
                               ST["g"]["q3free"][sl], ST["v"]["q3free"][sl]],
                        sem=s_v)
                for k in ("g", "v"):
                    d, s = GV[k], ST[k]
                    s["stgfree"][sl] = m1
                    s["tn"][h] = m1
                    st("sync", lambda sy, d=d, h=h, sl=sl: sy.dma_start(
                        out=d["tdr"][h], in_=d["q3"][sl][:]),
                       after=[m1], sem=d["tsem"][sl], k=16)
                    s["wr"][h] = d["tsem"][sl].mark()

            m_qqfree = m_xdq_all
            m_vvfree = m_xdq_all
            m_amax = m_xdq_all
            m_evg_prev = [m_xdq_all, m_xdq_all]
            m_evv_prev = [m_xdq_all, m_xdq_all]
            if tg == 0:
                eload("g", 0), eload("v", 0), eload("g", 1), eload("v", 1)
                etern_gv(0)
            else:
                eload_t("g", 0), eload_t("v", 0)
                eload_t("g", 1), eload_t("v", 1)
            for h in range(64):
                gsl = h % 2
                if tg == 0:
                    if h + 2 < 64:
                        eload("g", h + 2)
                        eload("v", h + 2)
                    if h + 1 < 64:
                        etern_gv(h + 1)
                pG = pb[0] if h % 2 == 0 else pb[1]

                def _gmm(t, h=h, pG=pG, gsl=gsl):
                    m = None
                    for cdx in range(16):
                        m = t.matmul(pG[:, :],
                                     t_q3[gsl][:, 128 * cdx:128 * (cdx + 1)],
                                     t_xdq[cdx][:], start=(cdx == 0),
                                     stop=(cdx == 15))
                    return m
                m_gmm = st("tensor", _gmm,
                           after=[ST["g"]["tn"][h], m_evg_prev[h % 2]],
                           sem=s_pe)
                ST["g"]["q3free"][gsl] = (
                    m_gmm if tg == 1 else
                    st("gpsimd", lambda g: g.engine_nop(),
                       after=[m_gmm, ST["g"]["wr"][h]], sem=s_g))
                if tg == 1 and h + 2 < 64:
                    eload_t("g", h + 2)

                m_evg = st("scalar", lambda a, h=h, pG=pG: a.activation(
                    out=t_qq[:], in_=pG[:, :], func=AF.Silu,
                    scale=t_gamma[:, 8:9], bias=t_gb2[:, h:h + 1]),
                    after=[m_gmm, m_gamt, m_qqfree], sem=s_a)
                m_evg_prev[h % 2] = m_evg

                pV = pb[2] if h % 2 == 0 else pb[3]

                def _vmm(t, h=h, pV=pV, gsl=gsl):
                    m = None
                    for cdx in range(16):
                        m = t.matmul(
                            pV[:, :],
                            t_q3[2 + gsl][:, 128 * cdx:128 * (cdx + 1)],
                            t_xdq[cdx][:], start=(cdx == 0), stop=(cdx == 15))
                    return m
                m_vmm = st("tensor", _vmm,
                           after=[ST["v"]["tn"][h], m_evv_prev[h % 2]],
                           sem=s_pe)
                ST["v"]["q3free"][gsl] = (
                    m_vmm if tg == 1 else
                    st("gpsimd", lambda g: g.engine_nop(),
                       after=[m_vmm, ST["v"]["wr"][h]], sem=s_g))
                if tg == 1 and h + 2 < 64:
                    eload_t("v", h + 2)

                def _evv(v, h=h, pV=pV):
                    v.tensor_scalar(out=t_vv[:], in0=pV[:, :],
                                    scalar1=t_gamma[:, 9:10],
                                    scalar2=t_vb2[:, h:h + 1],
                                    op0=ALU.mult, op1=ALU.add)
                    v.drain()
                    v.tensor_tensor(out=t_gv[h][:], in0=t_qq[:], in1=t_vv[:],
                                    op=ALU.mult)
                    v.drain()
                    if h == 0:
                        v.tensor_copy(out=t_amx[:], in_=t_gv[0][:])
                        return v.tensor_copy(out=t_amn[:], in_=t_gv[0][:])
                    v.tensor_tensor(out=t_amx[:], in0=t_amx[:],
                                    in1=t_gv[h][:], op=ALU.max)
                    return v.tensor_tensor(out=t_amn[:], in0=t_amn[:],
                                           in1=t_gv[h][:], op=ALU.min)
                m_evv = st("vector", _evv,
                           after=[m_vmm, m_evg, m_gamt, m_vvfree], sem=s_v)
                m_evv_prev[h % 2] = m_evv
                m_qqfree = m_evv
                m_vvfree = m_evv
            m_gvdone = m_evv_prev[(64 - 1) % 2]

            def _cmb2(v):
                v.tensor_scalar(out=t_amn[:], in0=t_amn[:], scalar1=-1.0,
                                scalar2=None, op0=ALU.mult)
                v.drain()
                return v.tensor_tensor(out=t_amx[:], in0=t_amx[:],
                                       in1=t_amn[:], op=ALU.max)
            m_gvdone = st("vector", _cmb2, after=[m_gvdone], sem=s_v)
            m_tp1b = st("tensor", lambda t: [
                t.transpose(pt32[:, 128 * blk:128 * (blk + 1)],
                            t_amx[:, 128 * blk:128 * (blk + 1)],
                            t_identf[:]) for blk in range(4)][-1],
                after=[m_gvdone], sem=s_pe)

            def _red2(v):
                for blk in range(4):
                    v.tensor_reduce(out=t_amc[:, blk:blk + 1],
                                    in_=pt32[:, 128 * blk:128 * (blk + 1)],
                                    axis=AXX, op=ALU.max)
                v.drain()
                v.tensor_scalar(out=t_amc[:, 0:4], in0=t_amc[:, 0:4],
                                scalar1=1e-5, scalar2=None, op0=ALU.max)
                v.drain()
                return v.reciprocal(out=t_amc[:, 4:8], in_=t_amc[:, 0:4])
            m_red2 = st("vector", _red2, after=[m_tp1b], sem=s_v)

            m_tp2b = st("tensor", lambda t: [
                t.transpose(pt32[0:1, 128 * blk:128 * (blk + 1)],
                            t_amc[:, 4 + blk:5 + blk], t_identf[:])
                for blk in range(4)][-1], after=[m_red2], sem=s_pe)
            m_sr1b = st("vector", lambda v: v.tensor_copy(
                out=t_srow[0:1, 0:512], in_=pt32[0:1, 0:512]),
                after=[m_tp2b], sem=s_v)
            m_tp3b = st("tensor", lambda t: [
                t.transpose(pt32[0:1, 128 * blk:128 * (blk + 1)],
                            t_amc[:, blk:blk + 1], t_identf[:])
                for blk in range(4)][-1], after=[m_sr1b], sem=s_pe)
            m_sr2b = st("vector", lambda v: v.tensor_copy(
                out=t_srow[0:1, 512:1024], in_=pt32[0:1, 0:512]),
                after=[m_tp3b], sem=s_v)
            m_b1b = st("tensor", lambda t: t.matmul(
                pt32[:, 0:512], t_ones_f[:], t_srow[0:1, 0:512],
                start=True, stop=True), after=[m_sr2b], sem=s_pe)
            m_s2b = st("vector", lambda v: v.tensor_scalar(
                out=t_s2b[:], in0=pt32[:, 0:512], scalar1=127.0, scalar2=None,
                op0=ALU.mult), after=[m_b1b], sem=s_v)
            m_b2b = st("tensor", lambda t: t.matmul(
                pt32[:, 0:512], t_ones_f[:], t_srow[0:1, 512:1024],
                start=True, stop=True), after=[m_s2b], sem=s_pe)
            m_s2ib = st("vector", lambda v: v.tensor_scalar(
                out=t_s2ib[:], in0=pt32[:, 0:512], scalar1=1.0 / 127.0,
                scalar2=None, op0=ALU.mult), after=[m_b2b], sem=s_v)

            m_q2_all = None
            for hp in range(32):
                def _q2(v, hp=hp):
                    for j, (sa, sq) in enumerate(qsc):
                        v.tensor_tensor(out=sa[:], in0=t_gv[2 * hp + j][:],
                                        in1=t_s2b[:], op=ALU.mult)
                    v.drain()
                    for j, (sa, sq) in enumerate(qsc):
                        v.tensor_scalar(out=sq[:], in0=sa[:], scalar1=MAGIC,
                                        scalar2=-MAGIC, op0=ALU.add,
                                        op1=ALU.add)
                    v.drain()
                    m = None
                    for j, (sa, sq) in enumerate(qsc):
                        m = v.tensor_tensor(out=t_gv[2 * hp + j][:],
                                            in0=sq[:], in1=t_s2ib[:],
                                            op=ALU.mult)
                    return m
                m_q2_all = st("vector", _q2, after=[m_s2b, m_s2ib], sem=s_v)

            m_x4r = [None] * 4
            for i in range(4):
                st("sync", lambda sy, i=i, tg0=tg0: sy.dma_start(
                    out=t_x4[i][:],
                    in_=xsc[tg0 + 128 * i:tg0 + 128 * (i + 1), :]),
                   after=[m_xdq_all], sem=s_x4[i], k=16)
                m_xl = (s_x4[i], s_x4[i].n)
                m_x4r[i] = st("vector", lambda v, i=i: v.tensor_tensor(
                    out=t_x4[i][:], in0=t_x4[i][:], in1=t_gbb4[:],
                    op=ALU.add), after=[m_xl, m_gbb4], sem=s_v)

            OD = dict(stgfree=[ST["g"]["stgfree"][0], ST["g"]["stgfree"][1]],
                      q3free=[ST["g"]["q3free"][0], ST["g"]["q3free"][1]],
                      ld={}, tn={}, wr={})

            def eload_o(n):
                sl = n % 2
                do, sub = divmod(n, 4)
                st("sync", lambda sy, do=do, sub=sub, sl=sl: sy.dma_start(
                    out=t_stg[sl][:], in_=owB[do, sub]),
                   after=[OD["stgfree"][sl]], sem=s_stg[sl], k=16)
                OD["ld"][n] = s_stg[sl].mark()

            def etern_o(n):
                sl = n % 2
                do, sub = divmod(n, 4)

                def _tq(v, sl=sl):
                    v.tensor_scalar(out=t_q3[sl][:], in0=t_stg[sl][:],
                                    scalar1=t_gamma[:, 6:7], scalar2=None,
                                    op0=ALU.is_lt)
                    v.drain()
                    return v.scalar_tensor_tensor(
                        out=t_q3[sl][:], in0=t_stg[sl][:],
                        scalar=t_gamma[:, 2:3], in1=t_q3[sl][:],
                        op0=ALU.is_gt, op1=ALU.subtract)
                m1 = st("vector", _tq,
                        after=[OD["ld"][n], OD["q3free"][sl]], sem=s_v)
                OD["stgfree"][sl] = m1
                OD["tn"][n] = m1
                st("sync", lambda sy, do=do, sub=sub, sl=sl: sy.dma_start(
                    out=tern_o[do, sub], in_=t_q3[sl][:]),
                   after=[m1], sem=s_tn[sl], k=16)
                OD["wr"][n] = s_tn[sl].mark()

            def eload_to(n):
                sl = n % 2
                do, sub = divmod(n, 4)
                st("sync", lambda sy, do=do, sub=sub, sl=sl: sy.dma_start(
                    out=t_q3[sl][:], in_=tern_o[do, sub]),
                   after=[OD["q3free"][sl]], sem=s_tn[sl], k=16)
                OD["tn"][n] = s_tn[sl].mark()

            m_evo_prev = [m_q2_all, m_q2_all]
            m_xadd = list(m_x4r)
            if tg == 0:
                eload_o(0), eload_o(1), etern_o(0)
            else:
                eload_to(0), eload_to(1)
            for n in range(64):
                do, sub = divmod(n, 4)
                sl = n % 2
                if tg == 0:
                    if n + 2 < 64:
                        eload_o(n + 2)
                    if n + 1 < 64:
                        etern_o(n + 1)
                pO = pb[4] if do % 2 == 0 else pb[5]

                def _omm(t, sub=sub, sl=sl, pO=pO):
                    m = None
                    for ci in range(16):
                        m = t.matmul(pO[:, :],
                                     t_q3[sl][:, 128 * ci:128 * (ci + 1)],
                                     t_gv[16 * sub + ci][:],
                                     start=(sub == 0 and ci == 0),
                                     stop=(sub == 3 and ci == 15))
                    return m
                aft_mm = [OD["tn"][n], m_q2_all]
                if sub == 0:
                    aft_mm.append(m_evo_prev[do % 2])
                m_om = st("tensor", _omm, after=aft_mm, sem=s_pe)
                OD["q3free"][sl] = (m_om if tg == 1 else
                                    st("gpsimd", lambda g: g.engine_nop(),
                                       after=[m_om, OD["wr"][n]], sem=s_g))
                if tg == 1 and n + 2 < 64:
                    eload_to(n + 2)
                if sub != 3:
                    continue
                m_oall = m_om

                m_e1 = st("vector", lambda v, do=do, pO=pO: v.tensor_scalar(
                    out=t_e1[:], in0=pO[:, :], scalar1=t_gbgc[:, do:do + 1],
                    scalar2=None, op0=ALU.mult),
                    after=[m_oall, m_gbgc], sem=s_v)
                m_evo_prev[do % 2] = m_e1

                m_tp = st("tensor", lambda t: [
                    t.transpose(pt32[:, 128 * blk:128 * (blk + 1)],
                                t_e1[:, 128 * blk:128 * (blk + 1)],
                                t_identf[:]) for blk in range(4)][-1],
                    after=[m_e1, (m_s2ib if do == 0 else None),
                           m_xadd[3]], sem=s_pe)

                def _xa(v, do=do):
                    m = None
                    for i in range(4):
                        m = v.tensor_tensor(
                            out=t_x4[i][:, 128 * do:128 * (do + 1)],
                            in0=t_x4[i][:, 128 * do:128 * (do + 1)],
                            in1=pt32[:, 128 * i:128 * (i + 1)], op=ALU.add)
                    return m
                m_xa = st("vector", _xa, after=[m_tp] + m_xadd, sem=s_v)
                m_xadd = [m_xa] * 4

            m_sts = []
            for i in range(4):
                st("sync", lambda sy, i=i, tg0=tg0: sy.dma_start(
                    out=xo[tg0 + 128 * i:tg0 + 128 * (i + 1), :],
                    in_=t_x4[i][:]), after=[m_xadd[0]], sem=s_sto, k=16)
            m_st4 = (s_sto, s_sto.n)
            m_tg_done = st("gpsimd", lambda g: g.engine_nop(),
                           after=[m_st4, m_xadd[0]], sem=s_g)
            es4b.close()

            if debug and tg == 0:
                d_h2 = b.ddbg("d_h2", [16 * 128, 512])
        es4.close()

        def _fin(sy):
            sy.wait_ge(s_sto.h, s_sto.n)
            return None
        st("sync", _fin, after=[m_tg_done])
        m_tail = m_tg_done

    M = dict(const=m_const, nwsel=m_nwsel, silc=m_silc, db=m_db, modc=m_modc,
             gamt=m_gamt, mrow=m_mrow)

    if upto_i < len(PHASES) - 1 or not enabled("ph1"):
        m_z = st("gpsimd", lambda g: (g.memset(t_junk[:], 0.0), g.drain(),
                                      g.engine_nop())[-1],
                 after=[m_tail], sem=s_g)

        def _zs(sy):
            m = None
            for i in range(8):
                m = sy.dma_start(out=xo[128 * i:128 * (i + 1), :],
                                 in_=t_junk[:])
                s_st.inc(m, 16)
            sy.wait_ge(s_st.h, s_st.n)
            return None
        st("sync", _zs, after=[m_z])

    es.close()
    return nc, b.dbg


def _host_consts():
    ii = np.arange(256)[None, :]
    jj = np.arange(128)[:, None]
    m0 = np.where(ii >= jj, 0.0, -1.0e30).astype(np.float32)
    m1 = np.where(ii >= jj + 128, 0.0, -1.0e30).astype(np.float32)
    masklog = np.concatenate([m0[:, 0:256], m1[:, 0:256]], axis=1)
    onesel = np.zeros((16, 2048), np.float32)
    for k in range(16):
        onesel[k, 128 * k:128 * (k + 1)] = 1.0
    oco = np.zeros((128, 256), np.float32)
    for h in range(16):
        oco[:, 16 * h + h] = 1.0
    import ml_dtypes
    return {
        "identb": np.eye(128, dtype=ml_dtypes.bfloat16),
        "identf": np.eye(128, dtype=np.float32),
        "masklog": masklog,
        "onesel": onesel,
        "oco": oco,
        "onesrow_b": np.ones((1, 128), ml_dtypes.bfloat16),
        "onesrow_f": np.ones((1, 128), np.float32),
        "onescol_f": np.ones((128, 1), np.float32),
    }


def _shard_inputs(core, I):
    bq, q = divmod(core, 4)
    x = np.asarray(I["L0_attn_hidden_states"])
    start = 1024 * q
    own = x[bq, start:start + T]
    halo = x[bq, start - 3:start] if q > 0 else np.zeros((3, HID), np.float32)
    xe = np.concatenate([halo, own], axis=0)
    pre = np.zeros((NPRE, HID), np.float32)
    if q > 0:
        pre[NPRE - start:] = x[bq, 0:start]
    prem = np.zeros((1, 12), np.float32)
    prem[0, 12 - 4 * q:] = 1.0
    halom = np.full((128, 1), 1.0 if q > 0 else 0.0, np.float32)

    c = np.asarray(I["L0_c"])
    c_pk = np.ascontiguousarray(c[bq].reshape(16, 128).T)

    convw = np.ascontiguousarray(
        np.asarray(I["L0_attn_conv_weight"]).reshape(24, 128, 4)
        .transpose(1, 0, 2).reshape(128, 96))
    convb = np.ascontiguousarray(
        np.asarray(I["L0_attn_conv_bias"]).reshape(24, 128).T)

    gw = np.asarray(I["L0_ffn_gate_w"])
    vw = np.asarray(I["L0_ffn_val_w"])
    ow = np.asarray(I["L0_ffn_out_w"])
    gwB = np.ascontiguousarray(
        gw.T.reshape(16, 128, 64, 128).transpose(2, 1, 0, 3)
        .reshape(64, 128, 2048))
    vwB = np.ascontiguousarray(
        vw.T.reshape(16, 128, 64, 128).transpose(2, 1, 0, 3)
        .reshape(64, 128, 2048))
    owB = np.ascontiguousarray(
        ow.T.reshape(4, 16, 128, 16, 128).transpose(3, 0, 2, 1, 4)
        .reshape(16, 4, 128, 2048))

    d = {
        "xe": xe, "xp": pre, "premrow": prem, "halom": halom,
        "epscol": np.full((128, 1), 1e-5, np.float32),
        "premB": np.broadcast_to(prem, (128, 12)).copy(),
        "w_in": np.asarray(I["L0_attn_W_in"]),
        "wmod": np.asarray(I["L0_wmod"]),
        "c_pk": c_pk, "convw": convw, "convb": convb,
        "acol": np.asarray(I["L0_attn_A_log"]).reshape(16, 1),
        "dtbcol": np.asarray(I["L0_attn_dt_bias"]).reshape(16, 1),
        "drow": np.asarray(I["L0_attn_D"]).reshape(1, 16),
        "nwrow": np.asarray(I["L0_attn_norm_weight"]).reshape(1, HID),
        "w_out": np.asarray(I["L0_attn_W_out"]),
        "gwB": gwB, "vwB": vwB, "owB": owB,
        "gb2": np.ascontiguousarray(
            np.asarray(I["L0_ffn_gate_b"]).reshape(64, 128).T),
        "vb2": np.ascontiguousarray(
            np.asarray(I["L0_ffn_val_b"]).reshape(64, 128).T),
        "obrow": np.asarray(I["L0_ffn_out_b"]).reshape(1, HID),
    }
    d.update(_host_consts())
    for k in d:
        if d[k].dtype == np.float64:
            raise ValueError(f"{k} is float64")
    return d


def kernel(**inputs):
    from concourse.bass_utils import run_bass_kernel_spmd
    nc, _ = build()
    in_maps = [_shard_inputs(c, inputs) for c in range(8)]
    res = run_bass_kernel_spmd(nc, in_maps, core_ids=list(range(8)))
    out = np.zeros((2, 4096, HID), np.float32)
    for c in range(8):
        bq, q = divmod(c, 4)
        out[bq, 1024 * q:1024 * (q + 1)] = res.results[c]["xo"]
    return out


if __name__ == "__main__":
    nc, dbg = build(debug=False, upto="ph0")
    print("built", sum(1 for _ in nc.all_instructions()), "instructions")


# version 26 (final)
# speedup vs baseline: 1.0932x; 1.0189x over previous
import numpy as np
from contextlib import ExitStack

import concourse.bass as bass
import concourse.mybir as mybir

F32 = mybir.dt.float32
BF16 = mybir.dt.bfloat16
AF = mybir.ActivationFunctionType
ALU = mybir.AluOpType
AXX = mybir.AxisListType.X

MAGIC = 12582912.0
HID = 2048
PROJ = 5136
NH = 16
Q = 256
T = 1024
NPRE = 3072
EPS = 1e-5
FFH = 8192


class Builder:
    def __init__(self, debug=False, upto="all"):
        self.debug = debug
        self.upto = upto
        self.nc = bass.Bass("TRN2", target_bir_lowering=False)
        self.es = ExitStack()
        self.dbg = {}
        self._semn = 0

    def din(self, name, shape, dt=F32):
        return self.nc.dram_tensor(name, shape, dt, kind="ExternalInput").ap()

    def dint(self, name, shape, dt=F32):
        return self.nc.dram_tensor(name, shape, dt, kind="Internal").ap()

    def dout(self, name, shape, dt=F32):
        ap = self.nc.dram_tensor(name, shape, dt, kind="ExternalOutput").ap()
        return ap

    def ddbg(self, name, shape, dt=F32):
        if self.debug:
            self.dbg[name] = self.nc.dram_tensor(
                name, shape, dt, kind="ExternalOutput").ap()
            return self.dbg[name]
        return None

    def sb(self, name, shape, dt=F32, scope=None, side=None):
        if side is None:
            side = "left" if scope is None else "right"
        cm = self.nc.sbuf_tensor(name, shape, dt, side=side)
        return (scope or self.es).enter_context(cm)

    def sem(self, name=None):
        self._semn += 1
        return Sem(self, name or f"s{self._semn}")

    def defer(self, queue, which, emit, after=(), sem=None, k=1):
        lz = Lazy(sem) if sem is not None else None
        queue.append((which, emit, list(after), sem, k, lz))
        return lz

    def flush_zip(self, queues):
        queues = [list(q) for q in queues]
        idx = [0] * len(queues)

        def ready(a):
            if a is None:
                return True
            if isinstance(a, Lazy):
                return a.val is not None
            return True

        def conv(a):
            if isinstance(a, Lazy):
                return (a.sem, a.val)
            return a

        while any(i < len(q) for i, q in zip(idx, queues)):
            emitted = False
            for qi, q in enumerate(queues):
                if idx[qi] >= len(q):
                    continue
                which, emit, after, sem, k, lz = q[idx[qi]]
                if not all(ready(a) for a in after):
                    continue
                m = self.step(which, emit,
                              after=[conv(a) for a in after if a is not None],
                              sem=sem, k=k)
                if lz is not None:
                    lz.val = m[1]
                idx[qi] += 1
                emitted = True
            if not emitted:
                raise RuntimeError("zipper deadlock")

    def step(self, which, emit, after=(), sem=None, k=1, dbgname=None):
        blk = self.blk
        dec = getattr(blk, which)
        out = {}

        def _(eng):
            for ms in after:
                if ms is not None:
                    eng.wait_ge(ms[0].h, ms[1])
            out["last"] = emit(eng)
        dec(_)
        if sem is not None:
            assert out["last"] is not None, dbgname
            return sem.inc(out["last"], k)
        return None


class Lazy:
    __slots__ = ("sem", "val")

    def __init__(self, sem):
        self.sem = sem
        self.val = None


class Sem:
    def __init__(self, b, name):
        self.h = b.es.enter_context(b.nc.semaphore(name))
        self.n = 0

    def inc(self, instr, k=1):
        instr.then_inc(self.h, k)
        self.n += k
        return (self, self.n)

    def mark(self):
        return (self, self.n)


def build(debug=False, upto="all"):
    b = Builder(debug=debug, upto=upto)
    nc, es = b.nc, b.es
    PHASES = ["ph0", "pre", "ph1", "ph2", "ph3", "ph4"]
    upto_i = len(PHASES) - 1 if upto == "all" else PHASES.index(upto)

    def enabled(ph):
        return PHASES.index(ph) <= upto_i

    xe = b.din("xe", [3 + T, HID])
    xp = b.din("xp", [NPRE, HID])
    w_in = b.din("w_in", [HID, PROJ])
    wmod = b.din("wmod", [HID, 12288])
    c_pk = b.din("c_pk", [128, 16])
    convw = b.din("convw", [128, 96])
    convb = b.din("convb", [128, 24])
    acol = b.din("acol", [16, 1])
    dtbcol = b.din("dtbcol", [16, 1])
    drow = b.din("drow", [1, 16])
    nwrow = b.din("nwrow", [1, HID])
    w_out = b.din("w_out", [HID, HID])
    gwB = b.din("gwB", [64, 128, 2048])
    vwB = b.din("vwB", [64, 128, 2048])
    owB = b.din("owB", [16, 4, 128, 2048])
    gb2 = b.din("gb2", [128, 64])
    vb2 = b.din("vb2", [128, 64])
    obrow = b.din("obrow", [1, HID])
    identb = b.din("identb", [128, 128], BF16)
    identf = b.din("identf", [128, 128])
    masklog = b.din("masklog", [128, 512])
    onesel = b.din("onesel", [16, 2048])
    oco = b.din("oco", [128, 256])
    onesrow_b = b.din("onesrow_b", [1, 128], BF16)
    onesrow_f = b.din("onesrow_f", [1, 128])
    onescol_f = b.din("onescol_f", [128, 1])
    halom = b.din("halom", [128, 1])
    premrow = b.din("premrow", [1, 12])
    premB = b.din("premB", [128, 12])
    epscol = b.din("epscol", [128, 1])

    xo = b.dout("xo", [T, HID], F32)
    modsc = b.dint("modsc", [1, 12288 + 2 * HID])
    xsc = b.dint("xsc", [T, HID])

    sb = b.sb
    t_identb = sb("t_identb", [128, 128], BF16)
    t_identf = sb("t_identf", [128, 128])
    t_ones_b = sb("t_ones_b", [1, 128], BF16)
    t_ones_f = sb("t_ones_f", [1, 128])
    t_onesc = sb("t_onesc", [128, 1])
    t_halom = sb("t_halom", [128, 1])
    t_prem = sb("t_prem", [1, 12])
    t_premB = sb("t_premB", [128, 12])
    t_eps = sb("t_eps", [128, 1])
    t_convw = sb("t_convw", [128, 96])
    t_convb = sb("t_convb", [128, 24])
    t_acol = sb("t_acol", [16, 1])
    t_dtbcol = sb("t_dtbcol", [16, 1])
    t_drow = sb("t_drow", [1, 16])
    t_gb2 = sb("t_gb2", [128, 64])
    t_vb2 = sb("t_vb2", [128, 64])
    t_db = sb("t_db", [128, 16])
    t_cpk = sb("t_cpk", [128, 16])
    t_silc = sb("t_silc", [128, 16], BF16)
    t_modc = sb("t_modc", [128, 64])
    t_gbgc = sb("t_gbgc", [128, 16])
    t_gamma = sb("t_gamma", [128, 12])
    t_gacc = sb("t_gacc", [128, 4])
    t_gq = sb("t_gq", [128, 48])
    t_grow = sb("t_grow", [1, 12])
    t_col = sb("t_col", [128, 16])
    esL = ExitStack()
    t_masklog = b.sb("t_masklog", [128, 512], scope=esL, side="left")
    t_onesel = b.sb("t_onesel", [16, 2048], scope=esL, side="left")
    t_oco = b.sb("t_oco", [128, 256], scope=esL, side="left")
    t_nwsel = b.sb("t_nwsel", [16, 2048], scope=esL, side="left")
    t_junk = b.sb("t_junk", [128, 2048], scope=esL, side="left")
    t_dthm = b.sb("t_dthm", [16, T], scope=esL, side="left")
    t_ldthm = b.sb("t_ldthm", [16, T], scope=esL, side="left")
    t_cumhm = b.sb("t_cumhm", [16, T], scope=esL, side="left")
    t_whm = b.sb("t_whm", [16, T], scope=esL, side="left")
    t_z16 = b.sb("t_z16", [16, 256], scope=esL, side="left")
    t_S = [b.sb(f"t_S{h}", [128, 128], scope=esL, side="left")
           for h in range(NH)]
    t_msrow = b.sb("t_msrow", [1, 2048], scope=esL, side="left")

    pb = [es.enter_context(nc.psum_tensor(f"pb{i}", [128, 512], F32))
          for i in range(6)]
    pt32 = es.enter_context(nc.psum_tensor("pt32", [128, 512], F32))
    ptb = es.enter_context(nc.psum_tensor("ptb", [128, 1024], BF16))

    s_const = b.sem("s_const")
    s_pe = b.sem("s_pe")
    s_v = b.sem("s_v")
    s_a = b.sem("s_a")
    s_g = b.sem("s_g")
    s_ev = b.sem("s_ev")
    s_mod = b.sem("s_mod")
    s_mx1 = b.sem("s_mx1")
    s_mx2 = b.sem("s_mx2")
    s_hd = b.sem("s_hd")
    s_gd = b.sem("s_gd")
    s_st = b.sem("s_st")

    b.blk = es.enter_context(nc.Block())
    st = b.step

    def _c(sy):
        m = None
        for dst, src in [
            (t_identb, identb), (t_identf, identf), (t_masklog, masklog),
            (t_onesel, onesel), (t_oco, oco), (t_ones_b, onesrow_b),
            (t_ones_f, onesrow_f), (t_onesc, onescol_f), (t_halom, halom),
            (t_prem, premrow), (t_premB, premB), (t_eps, epscol), (t_convw, convw), (t_convb, convb),
            (t_acol, acol), (t_dtbcol, dtbcol), (t_drow, drow),
            (t_gb2, gb2), (t_vb2, vb2),
            (t_cpk, c_pk),
        ]:
            m = sy.dma_start(out=dst[:], in_=src)
            s_const.inc(m, 16)
        return None
    st("sync", _c)
    m_const = s_const.mark()

    m_nwz = st("gpsimd", lambda g: (g.memset(t_nwsel[:], 0.0),
                                    g.memset(t_z16[:], 0.0))[-1],
               sem=s_g)

    def _nw(sy):
        m = None
        for k in range(16):
            m = sy.dma_start(out=t_nwsel[k:k + 1, 128 * k:128 * (k + 1)],
                             in_=nwrow[0:1, 128 * k:128 * (k + 1)])
            s_const.inc(m, 16)
        return None
    st("sync", _nw, after=[m_nwz, m_const])
    m_nwsel = s_const.mark()

    m_silc = st("scalar", lambda a: a.activation(out=t_silc[:], in_=t_cpk[:],
                                                 func=AF.Silu),
                after=[m_const], sem=s_a)

    m_acol0 = st("scalar", lambda a: a.activation(out=t_acol[:], in_=t_acol[:],
                                                  func=AF.Exp),
                 after=[m_const], sem=s_a)
    m_acol = st("vector", lambda v: v.tensor_scalar(
        out=t_acol[:], in0=t_acol[:], scalar1=-1.0, scalar2=None,
        op0=ALU.mult), after=[m_acol0], sem=s_v)

    m_pedb = st("tensor", lambda t: t.matmul(pt32[:, 0:16], t_ones_f[:],
                                             t_drow[:], start=True, stop=True),
                after=[m_const], sem=s_pe)
    m_db = st("vector", lambda v: v.tensor_copy(out=t_db[:], in_=pt32[:, 0:16]),
              after=[m_pedb], sem=s_v)

    with nc.sbuf_tensor("t_wm0", [128, 2048], BF16) as _wm0, \
         nc.sbuf_tensor("t_wm1", [128, 2048], BF16) as _wm1, \
         nc.sbuf_tensor("t_wm2", [128, 2048], BF16) as _wm2, \
         nc.sbuf_tensor("t_wm3", [128, 2048], BF16) as _wm3:
        t_wm = [_wm0, _wm1, _wm2, _wm3]
        s_wm = [b.sem(f"s_wm{i}") for i in range(4)]
        s_mw = [b.sem(f"s_mw{i}") for i in range(3)]
        mod_evac = []
        mm_done = []
        for cs in range(6):
            for k in range(16):
                i = cs * 16 + k
                sl = i % 4
                aft_l = [mm_done[i - 4]] if i >= 4 else []
                st("gpsimd",
                   lambda g, cs=cs, k=k, sl=sl: g.dma_start(
                       out=t_wm[sl][:],
                       in_=wmod[128 * k:128 * (k + 1),
                                2048 * cs:2048 * (cs + 1)]),
                   after=aft_l, sem=s_wm[sl], k=16)
                m_l = (s_wm[sl], 16 * (i // 4 + 1))

                def _mm(t, cs=cs, k=k, sl=sl):
                    m = None
                    for j in range(4):
                        m = t.matmul(pb[j][0:1, :], t_silc[:, k:k + 1],
                                     t_wm[sl][:, 512 * j:512 * (j + 1)],
                                     start=(k == 0), stop=(k == 15))
                    return m
                aft = [m_l, m_silc, m_db]
                if k == 0 and cs >= 1:
                    aft.append(mod_evac[cs - 1][0])
                st("tensor", _mm, after=aft, sem=s_pe)
                mm_done.append(s_pe.mark())

            def _ev(a, cs=cs):
                m = None
                for j in range(4):
                    m = a.activation(
                        out=t_msrow[0:1, 512 * j:512 * (j + 1)],
                        in_=pb[j][0:1, :], func=AF.Copy)
                return m
            m_ev = st("scalar", _ev, after=[mm_done[-1]] +
                      ([mod_evac[cs - 1][1]] if cs >= 1 else []), sem=s_a)
            grp = 0 if cs < 2 else (1 if cs < 5 else 2)
            m_wr = st("sync", lambda sy, cs=cs: sy.dma_start(
                out=modsc[0:1, 2048 * cs:2048 * (cs + 1)], in_=t_msrow[:]),
                after=[m_ev], sem=s_mw[grp], k=16)
            mod_evac.append((m_ev, m_wr))

    def _rt1(sy):
        for i, off in enumerate([0, 2048]):
            m = sy.dma_start(
                out=t_modc[:, 16 * i:16 * (i + 1)],
                in_=modsc[0:1, off:off + 2048].rearrange(
                    "a (k p) -> p a k", p=128).rearrange("p a k -> (p a) k"),
                allow_slow_non_contiguous=True)
            s_mx1.inc(m, 16)
        return None
    st("sync", _rt1, after=[mod_evac[1][1]])
    m_modc0 = s_mx1.mark()

    m_modc1 = st("vector", lambda v: v.tensor_scalar_add(
        out=t_modc[:, 16:32], in0=t_modc[:, 16:32], scalar1=1.0),
        after=[m_modc0], sem=s_v)

    def _rt2(sy):
        for i, off in enumerate([3 * 2048, 4 * 2048]):
            m = sy.dma_start(
                out=t_modc[:, 32 + 16 * i:48 + 16 * i],
                in_=modsc[0:1, off:off + 2048].rearrange(
                    "a (k p) -> p a k", p=128).rearrange("p a k -> (p a) k"),
                allow_slow_non_contiguous=True)
            s_mx2.inc(m, 16)
        return None
    st("sync", _rt2, after=[mod_evac[4][1]])
    m_modc2x = s_mx2.mark()

    m_modc = st("vector", lambda v: v.tensor_scalar_add(
        out=t_modc[:, 48:64], in0=t_modc[:, 48:64], scalar1=1.0),
        after=[m_modc2x, m_modc1], sem=s_v)

    def bcast_row(row_ap, out_tile, width, bf16, after):
        prev = None
        ones = t_ones_b if bf16 else t_ones_f
        for j in range((width + 511) // 512):
            w0, w1 = 512 * j, min(512 * (j + 1), width)

            def _t(t, w0=w0, w1=w1):
                return t.matmul(pt32[:, 0:w1 - w0], ones[:],
                                row_ap[0:1, w0:w1], start=True, stop=True)
            aft = list(after) + ([prev] if prev else [])
            m_b = st("tensor", _t, after=aft, sem=s_pe)

            def _v(v, w0=w0, w1=w1):
                return v.tensor_copy(out=out_tile[:, w0:w1],
                                     in_=pt32[:, 0:w1 - w0])
            prev = st("vector", _v, after=[m_b], sem=s_v)
        return prev

    gam_srcs = []
    for j in range(16):
        gam_srcs.append(gwB[4 * j:4 * (j + 1)].rearrange("a p w -> p a w"))
    for j in range(16):
        gam_srcs.append(vwB[4 * j:4 * (j + 1)].rearrange("a p w -> p a w"))
    for j in range(16):
        gam_srcs.append(owB[j].rearrange("s p w -> p s w"))

    def emit_gamma_tail(aft0):
        m_gr0 = st("tensor", lambda t: t.matmul(pt32[0:1, 0:48], t_onesc[:],
                                                t_gq[:], start=True,
                                                stop=True),
                   after=aft0 + [m_const], sem=s_pe)

        def _grsum(v):
            for mi in range(3):
                v.tensor_reduce(out=pt32[0:1, 48 + mi:49 + mi],
                                in_=pt32[0:1, 16 * mi:16 * (mi + 1)],
                                axis=AXX, op=ALU.add)
            v.drain()
            return v.tensor_copy(out=pt32[0:1, 51:52], in_=pt32[0:1, 48:49])
        m_gr = st("vector", _grsum, after=[m_gr0], sem=s_v)

        def _gfin(v):
            v.tensor_scalar(out=t_grow[0:1, 0:4], in0=pt32[0:1, 48:52],
                            scalar1=1.0 / (FFH * HID), scalar2=1e-5,
                            op0=ALU.mult, op1=ALU.max)
            v.drain()
            v.tensor_scalar(out=t_grow[0:1, 0:4], in0=t_grow[0:1, 0:4],
                            scalar1=0.5, scalar2=None, op0=ALU.mult)
            v.drain()
            v.tensor_scalar(out=t_grow[0:1, 4:8], in0=t_grow[0:1, 0:4],
                            scalar1=-1.0, scalar2=None, op0=ALU.mult)
            v.drain()
            return v.tensor_scalar(out=t_grow[0:1, 8:12],
                                   in0=t_grow[0:1, 0:4],
                                   scalar1=2.0, scalar2=None, op0=ALU.mult)
        m_gfin = st("vector", _gfin, after=[m_gr], sem=s_v)

        m_gcb = st("tensor", lambda t: t.matmul(pt32[:, 0:12], t_ones_f[:],
                                                t_grow[0:1, 0:12], start=True,
                                                stop=True),
                   after=[m_gfin], sem=s_pe)
        return st("vector", lambda v: v.tensor_copy(out=t_gamma[:],
                                                    in_=pt32[:, 0:12]),
                  after=[m_gcb], sem=s_v)

    if debug:
        d_mrow = b.ddbg("d_mrow", [1, 12288])
        st("sync", lambda sy: sy.dma_start(out=d_mrow, in_=modsc[0:1, 0:12288]),
           after=[mod_evac[5][1]], sem=s_st, k=16)

    m_join0 = st("gpsimd", lambda g: g.engine_nop(),
                 after=[m_nwsel, m_db,
                        (s_st.mark() if debug else None)], sem=s_g)
    m_tail = m_join0
    m_gamt = None

    def emit_ln(xtile, utok, rows, after_x, c0=0, stepf=None):
        cA = t_col[:, c0:c0 + 1]
        cB = t_col[:, c0 + 1:c0 + 2]
        cC = t_col[:, c0 + 2:c0 + 3]
        cD = t_col[:, c0 + 3:c0 + 4]
        cE = t_col[:, c0 + 4:c0 + 5]
        cF = t_col[:, c0 + 5:c0 + 6]
        xs = xtile[0:rows, :]

        def _v1(v):
            v.scalar_tensor_tensor(out=t_junk[0:rows, :], in0=xs, scalar=1.0,
                                   in1=xs, op0=ALU.mult, op1=ALU.mult,
                                   accum_out=cB[0:rows])
            v.tensor_reduce(out=cA[0:rows], in_=xs, axis=AXX, op=ALU.add)
            v.drain()
            v.tensor_scalar(out=cC[0:rows], in0=cA[0:rows],
                            scalar1=1.0 / HID, scalar2=None, op0=ALU.mult)
            v.drain()
            v.tensor_scalar(out=cD[0:rows], in0=cC[0:rows], scalar1=cC[0:rows],
                            scalar2=-1.0, op0=ALU.mult, op1=ALU.mult)
            v.drain()
            return v.tensor_scalar(out=cD[0:rows], in0=cD[0:rows],
                                   scalar1=EPS, scalar2=None, op0=ALU.add)
        stf = stepf or st
        mv1 = stf("vector", _v1, after=after_x, sem=s_v)

        def _a2(a):
            a.activation(out=cE[0:rows], in_=cB[0:rows], func=AF.Ln,
                         bias=cD[0:rows], scale=1.0 / HID)
            a.drain()
            return a.activation(out=cD[0:rows], in_=cE[0:rows], func=AF.Exp,
                                scale=-0.5)
        ma2 = stf("scalar", _a2, after=[mv1], sem=s_a)

        def _v2(v):
            v.tensor_scalar(out=cF[0:rows], in0=cC[0:rows],
                            scalar1=cD[0:rows], scalar2=-1.0,
                            op0=ALU.mult, op1=ALU.mult)
            v.drain()
            return v.tensor_scalar(out=utok[0:rows, :], in0=xs,
                                   scalar1=cD[0:rows], scalar2=cF[0:rows],
                                   op0=ALU.mult, op1=ALU.add)
        mu = stf("vector", _v2, after=[ma2], sem=s_v)
        return mu

    if enabled("pre"):
        esP = ExitStack()
        t_wpre = b.sb("t_wpre", [128, 16 * 1040], BF16, scope=esP)
        t_px = [b.sb(f"t_px{i}", [128, HID], scope=esP) for i in range(2)]
        t_put = b.sb("t_put", [128, HID], BF16, scope=esP)
        t_upf = [b.sb(f"t_upf{k}", [128, 512], BF16, scope=esP)
                 for k in range(16)]
        t_psc = [b.sb(f"t_psc{cc}", [128, 2 * 259], scope=esP)
                 for cc in range(8)]
        t_tails = b.sb("t_tails", [128, 24], scope=esP)
        t_pxb = [b.sb(f"t_pxb{cc}", [128, 256], BF16, scope=esP)
                 for cc in range(8)]
        t_cvp = b.sb("t_cvp", [128, 256], scope=esP)
        t_cgp = b.sb("t_cgp", [128, 256], scope=esP)
        t_csp = b.sb("t_csp", [128, 8 * 256], scope=esP)
        t_dtp = b.sb("t_dtp", [16, 512], scope=esP)
        t_ldp = b.sb("t_ldp", [16, 256], scope=esP)
        t_cup = b.sb("t_cup", [16, 256], scope=esP)
        t_whp = b.sb("t_whp", [16, 256], scope=esP)
        t_xctp = [b.sb(f"t_xctp{i}", [128, 128], BF16, scope=esP)
                  for i in range(8)]
        t_btp = [b.sb(f"t_btp{i}", [128, 128], BF16, scope=esP)
                 for i in range(8)]
        t_bwp = b.sb("t_bwp", [128, 8 * 128], BF16, scope=esP)
        t_wtmp = [b.sb(f"t_wtmp{j}", [128, 16], scope=esP) for j in range(2)]
        t_cdbp = b.sb("t_cdbp", [128, 16], scope=esP)
        t_clrp = b.sb("t_clrp", [1, 16], scope=esP)
        s_px = [b.sem("s_px0"), b.sem("s_px1")]
        s_wpre = b.sem("s_wpre")

        def _pz(g):
            for h in range(NH):
                g.memset(t_S[h][:], 0.0)
            return g.memset(t_tails[:], 0.0)
        m_pz = st("gpsimd", _pz, after=[], sem=s_g)

        st("gpsimd", lambda g: g.dma_start(
            out=t_wpre[:, 0:16 * 1024],
            in_=w_in[:, 2048:3072].rearrange("(c p) w -> p c w", p=128)),
           after=[], sem=s_wpre, k=16)
        st("gpsimd", lambda g: g.dma_start(
            out=t_wpre[:, 16 * 1024:16 * 1040],
            in_=w_in[:, 5120:5136].rearrange("(c p) w -> p c w", p=128)),
           sem=s_wpre, k=16)
        m_wpre = s_wpre.mark()

        stA = {}
        last_mu = [None, None]
        last_wave = None
        last_b6 = None
        last_su = [m_pz, m_pz]
        last_evs = None
        prev_b = {"tails": m_pz, "conv": None, "sil": None}

        def stage_a(q, c):
            par = c % 2
            r0 = 256 * c
            A = {}
            dstp = lambda which, emit, after=(), sem=None, k=1: \
                b.defer(q, which, emit, after, sem, k)
            mus = []
            waves = []
            for i in range(2):
                aft_l = [m_pz, m_join0] if last_mu[i] is None \
                    else [last_mu[i]]
                lx = dstp("sync", lambda sy, r0=r0, i=i: sy.dma_start(
                    out=t_px[i][:],
                    in_=xp[r0 + 128 * i:r0 + 128 * (i + 1), :]),
                    after=aft_l, sem=s_px[i], k=16)
                aft_x = [lx] + ([mus[0]] if i == 1 else [])
                mu = emit_ln(t_px[i], t_put, 128, aft_x, c0=8, stepf=dstp)
                mus.append(mu)
                last_mu[i] = mu
                for w in range(2):
                    def _twp(t, i=i, w=w):
                        m = None
                        for kk in range(8):
                            k = 8 * w + kk
                            m = t.transpose(
                                ptb[:, 128 * kk:128 * (kk + 1)],
                                t_put[:, 128 * k:128 * (k + 1)],
                                t_identb[:])
                        return m
                    aft = [mu] + ([waves[-1]] if waves else
                                  ([last_b6] if last_b6 else []))
                    m_t = dstp("tensor", _twp, after=aft, sem=s_pe)

                    def _evp(a, i=i, w=w, par=par):
                        m = None
                        for kk in range(8):
                            k = 8 * w + kk
                            m = a.activation(
                                out=t_upf[k][:, 256 * par + 128 * i:
                                             256 * par + 128 * (i + 1)],
                                in_=ptb[:, 128 * kk:128 * (kk + 1)],
                                func=AF.Identity,
                                scale=t_modc[:, 16 + k:17 + k],
                                bias=t_modc[:, k:k + 1])
                        return m
                    mw = dstp("scalar", _evp, after=[m_t, m_modc1], sem=s_a)
                    waves.append(mw)
            A["upf"] = waves[-1]
            A["lastwave"] = waves[-1]

            evs = []
            prev_pair = None
            for jp in range(4):
                def _pmm(t, jp=jp, par=par):
                    m = None
                    for j in (2 * jp, 2 * jp + 1):
                        pX = pb[j % 2]
                        for k in range(16):
                            m = t.matmul(
                                pX[:, 0:256],
                                t_wpre[:, 1024 * k + 128 * j:
                                       1024 * k + 128 * (j + 1)],
                                t_upf[k][:, 256 * par:256 * (par + 1)],
                                start=(k == 0), stop=(k == 15))
                    return m
                aft = [A["upf"], m_wpre]
                if prev_pair is not None:
                    aft.append(prev_pair)
                elif last_evs is not None:
                    aft.append(last_evs[-1])
                m_pm = dstp("tensor", _pmm, after=aft, sem=s_pe)

                def _pev(v, jp=jp, par=par, c=c):
                    m = None
                    for j in (2 * jp, 2 * jp + 1):
                        m = v.tensor_scalar(
                            out=t_psc[j][:, 259 * par + 3:259 * par + 259],
                            in0=pb[j % 2][:, 0:256],
                            scalar1=t_premB[:, c:c + 1],
                            scalar2=None, op0=ALU.mult)
                    return m
                aft_e = [m_pm]
                if prev_b["conv"] is not None:
                    aft_e.append(prev_b["conv"])
                pe_ = dstp("vector", _pev, after=aft_e, sem=s_v)
                evs.append(pe_)
                prev_pair = pe_
            A["evs"] = evs

            def _pdt(t):
                m = None
                for k in range(16):
                    m = t.matmul(pb[2][0:16, 0:256],
                                 t_wpre[:, 16 * 1024 + 16 * k:
                                        16 * 1024 + 16 * (k + 1)],
                                 t_upf[k][:, 256 * par:256 * (par + 1)],
                                 start=(k == 0), stop=(k == 15))
                return m
            aft_dt = [A["upf"], m_wpre]
            if "dtev" in stA.get(c - 1, {}):
                aft_dt.append(stA[c - 1]["dtev"])
            m_pdt = dstp("tensor", _pdt, after=aft_dt, sem=s_pe)
            A["dtev"] = dstp("vector", lambda v, par=par: v.tensor_copy(
                out=t_dtp[0:16, 256 * par:256 * (par + 1)],
                in_=pb[2][0:16, 0:256]), after=[m_pdt], sem=s_v)
            stA[c] = A
            return A

        def stage_b(q, c):
            par = c % 2
            A = stA[c]
            dstp = lambda which, emit, after=(), sem=None, k=1: \
                b.defer(q, which, emit, after, sem, k)
            def _tl(v, par=par):
                m = None
                for cc in range(8):
                    m = v.tensor_copy(
                        out=t_psc[cc][:, 259 * par:259 * par + 3],
                        in_=t_tails[:, 3 * cc:3 * cc + 3])
                return m
            m_tl = dstp("vector", _tl, after=[prev_b["tails"], A["evs"][0]],
                        sem=s_v)

            convs = []
            for half in range(2):
                def _cvp(v, half=half, par=par):
                    m = None
                    for cc in range(4 * half, 4 * half + 4):
                        sc = t_psc[cc][:, 259 * par:259 * (par + 1)]
                        v.tensor_scalar(out=t_cvp[:], in0=sc[:, 0:256],
                                        scalar1=t_convw[:, 4 * cc:4 * cc + 1],
                                        scalar2=None, op0=ALU.mult)
                        v.tensor_scalar(out=t_cgp[:], in0=sc[:, 2:258],
                                        scalar1=t_convw[:, 4 * cc + 2:
                                                        4 * cc + 3],
                                        scalar2=None, op0=ALU.mult)
                        v.drain()
                        v.scalar_tensor_tensor(
                            out=t_cvp[:], in0=sc[:, 1:257],
                            scalar=t_convw[:, 4 * cc + 1:4 * cc + 2],
                            in1=t_cvp[:], op0=ALU.mult, op1=ALU.add)
                        v.scalar_tensor_tensor(
                            out=t_cgp[:], in0=sc[:, 3:259],
                            scalar=t_convw[:, 4 * cc + 3:4 * cc + 4],
                            in1=t_cgp[:], op0=ALU.mult, op1=ALU.add)
                        v.drain()
                        m = v.tensor_tensor(
                            out=t_csp[:, 256 * cc:256 * (cc + 1)],
                            in0=t_cvp[:], in1=t_cgp[:], op=ALU.add)
                    return m
                cv = dstp("vector", _cvp,
                          after=[A["evs"][2 * half + 1], m_tl], sem=s_v)
                convs.append(cv)
            prev_b["conv"] = convs[-1]

            def _sil(a):
                m = None
                for cc in range(8):
                    m = a.activation(out=t_pxb[cc][:],
                                     in_=t_csp[:, 256 * cc:256 * (cc + 1)],
                                     func=AF.Silu, bias=t_convb[:, cc:cc + 1])
                return m
            m_si = dstp("scalar", _sil, after=convs, sem=s_a)
            prev_b["sil"] = m_si

            def _tw(v, par=par):
                m = None
                for cc in range(8):
                    m = v.tensor_copy(
                        out=t_tails[:, 3 * cc:3 * cc + 3],
                        in_=t_psc[cc][:, 259 * par + 256:259 * par + 259])
                return m
            prev_b["tails"] = dstp("vector", _tw, after=[m_si], sem=s_v)

            ps1 = dstp("vector", lambda v, par=par: v.tensor_scalar(
                out=t_dtp[0:16, 256 * par:256 * (par + 1)],
                in0=t_dtp[0:16, 256 * par:256 * (par + 1)],
                scalar1=t_dtbcol[0:16, 0:1], scalar2=None, op0=ALU.add),
                after=[A["dtev"]], sem=s_v)

            def _ps2(a, par=par):
                dtv = t_dtp[0:16, 256 * par:256 * (par + 1)]
                a.activation(out=t_ldp[:], in_=dtv, func=AF.Abs)
                a.drain()
                a.activation(out=t_ldp[:], in_=t_ldp[:], func=AF.Exp,
                             scale=-1.0)
                a.drain()
                return a.activation(out=t_ldp[:], in_=t_ldp[:], func=AF.Ln,
                                    bias=1.0)
            ps2 = dstp("scalar", _ps2, after=[ps1], sem=s_a)

            def _ps3(v, par=par):
                dtv = t_dtp[0:16, 256 * par:256 * (par + 1)]
                v.tensor_scalar(out=t_cup[:], in0=dtv, scalar1=0.0,
                                scalar2=None, op0=ALU.max)
                v.drain()
                return v.tensor_tensor(out=dtv, in0=t_cup[:], in1=t_ldp[:],
                                       op=ALU.add)
            ps3 = dstp("vector", _ps3, after=[ps2], sem=s_v)
            pld = dstp("scalar", lambda a, par=par: a.activation(
                out=t_ldp[:], in_=t_dtp[0:16, 256 * par:256 * (par + 1)],
                func=AF.Ln), after=[ps3], sem=s_a)

            def _pcum(v, par=par):
                dtv = t_dtp[0:16, 256 * par:256 * (par + 1)]
                v.tensor_scalar(out=t_whp[:], in0=dtv,
                                scalar1=t_acol[0:16, 0:1], scalar2=None,
                                op0=ALU.mult)
                v.drain()
                v.tensor_tensor_scan(out=t_cup[:], data0=t_whp[:],
                                     data1=t_z16[:, 0:256], initial=0.0,
                                     op0=ALU.add, op1=ALU.bypass)
                v.drain()
                v.tensor_tensor(out=t_whp[:], in0=t_ldp[:], in1=t_cup[:],
                                op=ALU.subtract)
                v.drain()
                return v.tensor_scalar(out=t_whp[:], in0=t_whp[:],
                                       scalar1=t_cup[:, 255:256],
                                       scalar2=None, op0=ALU.add)
            pcum = dstp("vector", _pcum, after=[pld, m_acol], sem=s_v)

            pc1 = dstp("tensor", lambda t: t.transpose(
                pt32[0:1, 0:16], t_cup[:, 255:256], t_identf[0:16, 0:16]),
                after=[pcum, last_su[0], last_su[1]], sem=s_pe)

            def _pclr2(v, c=c):
                v.tensor_copy(out=t_clrp[:], in_=pt32[0:1, 0:16])
                v.drain()
                return v.tensor_scalar(out=t_clrp[:], in0=t_clrp[:],
                                       scalar1=t_prem[0:1, c:c + 1],
                                       scalar2=None, op0=ALU.mult)
            pc2 = dstp("vector", _pclr2, after=[pc1], sem=s_v)
            pc3 = dstp("tensor", lambda t: t.matmul(
                pt32[:, 64:80], t_ones_f[:], t_clrp[:], start=True,
                stop=True), after=[pc2], sem=s_pe)

            def _pexp(a):
                a.activation(out=t_whp[:], in_=t_whp[:], func=AF.Exp)
                return a.activation(out=t_cdbp[:], in_=pt32[:, 64:80],
                                    func=AF.Exp)
            pcdb = dstp("scalar", _pexp, after=[pc3, pcum], sem=s_a)

            def _pwt(t):
                t.transpose(pt32[:, 0:16], t_whp[0:16, 0:128],
                            t_identf[0:16, 0:16])
                return t.transpose(pt32[:, 16:32], t_whp[0:16, 128:256],
                                   t_identf[0:16, 0:16])
            pwt = dstp("tensor", _pwt, after=[pcdb], sem=s_pe)

            def _pwt2(v):
                v.tensor_copy(out=t_wtmp[0][:], in_=pt32[:, 0:16])
                return v.tensor_copy(out=t_wtmp[1][:], in_=pt32[:, 16:32])
            pwt2 = dstp("vector", _pwt2, after=[pwt], sem=s_v)

            nonloc = {}

            def _pt1(t):
                m = None
                for g_ in range(4):
                    for jc in range(2):
                        m = t.transpose(
                            ptb[:, 128 * (2 * g_ + jc):
                                128 * (2 * g_ + jc + 1)],
                            t_pxb[g_][:, 128 * jc:128 * (jc + 1)],
                            t_identb[:])
                return m
            aft1 = [m_si, last_su[0], last_su[1]]
            if c + 1 in stA:
                aft1.append(stA[c + 1]["lastwave"])
            pt1 = dstp("tensor", _pt1, after=aft1, sem=s_pe)

            def _pe1(a):
                m = None
                for i in range(8):
                    m = a.activation(out=t_xctp[i][:],
                                     in_=ptb[:, 128 * i:128 * (i + 1)],
                                     func=AF.Copy)
                return m
            pe1 = dstp("scalar", _pe1, after=[pt1], sem=s_a)

            def _pt2(t):
                m = None
                for g_ in range(4):
                    for jc in range(2):
                        m = t.transpose(
                            ptb[:, 128 * (2 * g_ + jc):
                                128 * (2 * g_ + jc + 1)],
                            t_pxb[4 + g_][:, 128 * jc:128 * (jc + 1)],
                            t_identb[:])
                return m
            pt2 = dstp("tensor", _pt2, after=[pe1], sem=s_pe)

            def _pe2(v):
                m = None
                for i in range(8):
                    m = v.tensor_copy(out=t_btp[i][:],
                                      in_=ptb[:, 128 * i:128 * (i + 1)])
                return m
            pe2 = dstp("vector", _pe2, after=[pt2], sem=s_v)
            b6 = pe2

            for grp in range(4):
                gpar = grp % 2
                pSc = pb[3 + gpar]

                def _gbw(v, grp=grp):
                    m = None
                    for hh in range(4):
                        h = 4 * grp + hh
                        g_ = h // 4
                        po = 256 * hh
                        v.tensor_scalar(
                            out=t_bwp[:, po:po + 128],
                            in0=t_btp[2 * g_][:],
                            scalar1=t_wtmp[0][:, h:h + 1],
                            scalar2=None, op0=ALU.mult)
                        m = v.tensor_scalar(
                            out=t_bwp[:, po + 128:po + 256],
                            in0=t_btp[2 * g_ + 1][:],
                            scalar1=t_wtmp[1][:, h:h + 1],
                            scalar2=None, op0=ALU.mult)
                    return m
                gbw = dstp("vector", _gbw,
                           after=[pe2, pwt2, last_su[gpar]], sem=s_v)

                def _gsc(t, grp=grp, pSc=pSc):
                    m = None
                    for hh in range(4):
                        h = 4 * grp + hh
                        g_ = h // 4
                        po = 256 * hh
                        t.matmul(pSc[:, 128 * hh:128 * (hh + 1)],
                                 t_bwp[:, po:po + 128],
                                 t_xctp[2 * g_][:], start=True, stop=False)
                        m = t.matmul(pSc[:, 128 * hh:128 * (hh + 1)],
                                     t_bwp[:, po + 128:po + 256],
                                     t_xctp[2 * g_ + 1][:], start=False,
                                     stop=True)
                    return m
                gsc = dstp("tensor", _gsc, after=[gbw, last_su[gpar]],
                           sem=s_pe)

                def _gsu(v, grp=grp, pSc=pSc):
                    m = None
                    for hh in range(4):
                        h = 4 * grp + hh
                        m = v.scalar_tensor_tensor(
                            out=t_S[h][:], in0=t_S[h][:],
                            scalar=t_cdbp[:, h:h + 1],
                            in1=pSc[:, 128 * hh:128 * (hh + 1)],
                            op0=ALU.mult, op1=ALU.add)
                    return m
                gsu = dstp("vector", _gsu, after=[gsc, pcdb], sem=s_v)
                last_su[gpar] = gsu
            return b6

        t_gmb = [b.sb(f"t_gmb{i}", [128, 8192], BF16, scope=esP)
                 for i in range(2)]
        s_gam = [b.sem("s_gam0"), b.sem("s_gam1")]
        gam_red = []
        qa = []
        stage_a(qa, 0)
        b.flush_zip([qa])
        for c in range(12):
            qa, qb = [], []
            if c + 1 < 12:
                stage_a(qa, c + 1)
            last_b6 = stage_b(qb, c)
            b.flush_zip([qa, qb])
            for i in range(4 * c, 4 * c + 4):
                sl = i % 2
                aft = [gam_red[i - 2]] if i >= 2 else []
                st("gpsimd", lambda g, i=i, sl=sl: g.dma_start(
                    out=t_gmb[sl][:], in_=gam_srcs[i]),
                   after=aft, sem=s_gam[sl], k=16)
                m_l = s_gam[sl].mark()

                def _gred(a, sl=sl, i=i):
                    return a.activation(out=t_gmb[sl][:], in_=t_gmb[sl][:],
                                        func=AF.Abs,
                                        accum_out=t_gq[:, i:i + 1])
                gam_red.append(st("scalar", _gred, after=[m_l], sem=s_a))

        m_pre = st("gpsimd", lambda g: g.engine_nop(),
                   after=[(last_su[0].sem, last_su[0].val),
                          (last_su[1].sem, last_su[1].val),
                          (prev_b["tails"].sem, prev_b["tails"].val)],
                   sem=s_g)
        m_tail = m_pre
        m_gamt = emit_gamma_tail([gam_red[-1], m_pre])
        m_mrow = st("gpsimd", lambda g: g.engine_nop(),
                    after=[(s_mw[0], 32), (s_mw[1], 48), (s_mw[2], 16)],
                    sem=s_g)
        esP.close()

        if debug:
            d_sinit = b.ddbg("d_sinit", [NH * 128, 128])

            def _dbgp(sy):
                for h in range(16):
                    s_st.inc(sy.dma_start(
                        out=d_sinit[128 * h:128 * (h + 1), :],
                        in_=t_S[h][:]), 16)
                sy.wait_ge(s_st.h, s_st.n)
                return None
            st("sync", _dbgp, after=[m_pre])
            m_pre = st("gpsimd", lambda g: g.engine_nop(),
                       after=[m_pre, (s_st.mark())], sem=s_g)
            m_tail = m_pre

    if enabled("ph1"):
        esA = ExitStack()
        t_sg = [b.sb(f"t_sg{k}", [128, T], BF16, scope=esA) for k in range(16)]
        esA2 = ExitStack()
        t_xbc = [b.sb(f"t_xbc{c}", [128, T], BF16, scope=esA2)
                 for c in range(24)]
        t_bias_hm = b.sb("t_bias_hm", [16, T], scope=esA2)
        esP1 = ExitStack()
        t_x = [b.sb("t_x0", [128, HID], scope=esP1)]
        t_utok = b.sb("t_utok", [128, HID], BF16, scope=esP1)
        t_u1 = [b.sb(f"t_u1_{k}", [128, 3 + T], BF16, scope=esP1)
                for k in range(16)]
        t_wsl = [b.sb(f"t_wsl{i}", [128, 16 * 128], BF16, scope=esP1)
                 for i in range(2)]
        t_wdt = b.sb("t_wdt", [128, 256], BF16, scope=esP1)
        t_scr = [b.sb("t_scr0", [128, 3 + T], scope=esP1)]
        t_cv = t_junk[:, 0:1024]
        t_cg2 = t_x[0][:, 1024:2048]
        t_cs = [t_junk[:, 1024:2048]]
        s_x = [b.sem("s_x0")]
        s_wsl = [b.sem("s_wsl0"), b.sem("s_wsl1")]

        m_u = [None] * 9
        m_tw = [None] * 9
        prev_evac = None
        for i in range(9):
            rows = 128 if i < 8 else 3
            sl = 0
            aft_l = [m_tail] if i < 1 else [m_u[i - 1]]
            st("sync", lambda sy, i=i, sl=sl, rows=rows: sy.dma_start(
                out=t_x[sl][0:rows, :], in_=xe[128 * i:128 * i + rows, :]),
               after=aft_l, sem=s_x[sl], k=16)
            m_xl = (s_x[sl], s_x[sl].n)
            aft_x = [m_xl] + ([m_tw[i - 1]] if i >= 1 else [])
            m_u[i] = emit_ln(t_x[sl], t_utok, rows, aft_x)

            prev_w = None
            for w in range(2):
                def _tw(t, i=i, w=w, rows=rows):
                    m = None
                    for kk in range(8):
                        k = 8 * w + kk
                        m = t.transpose(
                            ptb[:, 128 * kk:128 * kk + rows],
                            t_utok[0:rows, 128 * k:128 * (k + 1)],
                            t_identb[0:rows, 0:rows])
                    return m
                aft = [m_u[i]] + ([prev_w] if prev_w else
                                  ([m_tw[i - 1]] if i >= 1 else []))
                m_t = st("tensor", _tw, after=aft, sem=s_pe)

                def _ev(v, i=i, w=w, rows=rows):
                    m = None
                    off = 128 * i
                    for kk in range(8):
                        k = 8 * w + kk
                        m = v.tensor_scalar(
                            out=t_u1[k][:, off:off + rows],
                            in0=ptb[:, 128 * kk:128 * kk + rows],
                            scalar1=t_modc[:, 16 + k:17 + k],
                            scalar2=t_modc[:, k:k + 1],
                            op0=ALU.mult, op1=ALU.add)
                    return m
                prev_w = st("vector", _ev, after=[m_t, m_modc1], sem=s_v)
            m_tw[i] = prev_w
        m_u1all = m_tw[8]

        wfree = [None, None]
        mm_ct = [None] * 41
        ev_ct = [None] * 41
        prev_silu = None
        for cs in range(40):
            sl = cs % 2
            aft_l = [wfree[sl]] if wfree[sl] else [m_tail, m_u1all]
            st("gpsimd", lambda g, cs=cs, sl=sl: g.dma_start(
                out=t_wsl[sl][:],
                in_=w_in[:, 128 * cs:128 * (cs + 1)].rearrange(
                    "(c p) w -> p c w", p=128)),
               after=aft_l, sem=s_wsl[sl], k=16)
            m_wl = (s_wsl[sl], 16 * (cs // 2 + 1))
            for j in range(1):
                ct = cs
                is_hbc = 16 <= ct < 40
                pA = pb[2 * (ct % 2)]
                pB = pb[2 * (ct % 2) + 1]
                pH = pb[4 + ct % 2]
                ph_off = 0

                def _mm(t, ct=ct, sl=sl, j=j, is_hbc=is_hbc, pA=pA, pB=pB,
                        pH=pH, ph_off=ph_off):
                    m = None
                    for k in range(16):
                        lh = t_wsl[sl][:, 128 * k:128 * (k + 1)]
                        m = t.matmul(pA[:, :], lh, t_u1[k][:, 3:515],
                                     start=(k == 0), stop=(k == 15))
                        m = t.matmul(pB[:, :], lh, t_u1[k][:, 515:1027],
                                     start=(k == 0), stop=(k == 15))
                        if is_hbc:
                            m = t.matmul(pH[:, ph_off:ph_off + 3], lh,
                                         t_u1[k][:, 0:3],
                                         start=(k == 0), stop=(k == 15))
                    return m
                aft = [m_u1all, m_wl]
                if ct >= 2:
                    aft.append(ev_ct[ct - 2])
                mm_ct[ct] = st("tensor", _mm, after=aft, sem=s_pe)

                if ct < 16:
                    def _evg(a, ct=ct, pA=pA, pB=pB):
                        a.activation(out=t_sg[ct][:, 0:512], in_=pA[:, :],
                                     func=AF.Silu)
                        return a.activation(out=t_sg[ct][:, 512:1024],
                                            in_=pB[:, :], func=AF.Silu)
                    ev_ct[ct] = st("scalar", _evg, after=[mm_ct[ct]], sem=s_a)
                else:
                    cc = ct - 16
                    ssl = 0

                    def _eva(a, ct=ct, pA=pA, pB=pB, ssl=ssl):
                        a.activation(out=t_scr[ssl][:, 3:515], in_=pA[:, :],
                                     func=AF.Copy)
                        return a.activation(out=t_scr[ssl][:, 515:1027],
                                            in_=pB[:, :], func=AF.Copy)
                    m_ea = st("scalar", _eva, after=[mm_ct[ct]], sem=s_a)

                    def _evh(v, ct=ct, ssl=ssl, pH=pH, ph_off=ph_off):
                        return v.tensor_scalar(
                            out=t_scr[ssl][:, 0:3],
                            in0=pH[:, ph_off:ph_off + 3],
                            scalar1=t_halom[:, 0:1], scalar2=None,
                            op0=ALU.mult)
                    m_eh = st("vector", _evh, after=[mm_ct[ct], m_const],
                              sem=s_v)
                    ev_ct[ct] = m_ea

                    def _cv(v, cc=cc, ssl=ssl):
                        sc = t_scr[ssl]
                        v.tensor_scalar(out=t_cv, in0=sc[:, 0:1024],
                                        scalar1=t_convw[:, 4 * cc:4 * cc + 1],
                                        scalar2=None, op0=ALU.mult)
                        v.tensor_scalar(out=t_cg2, in0=sc[:, 2:1026],
                                        scalar1=t_convw[:, 4 * cc + 2:4 * cc + 3],
                                        scalar2=None, op0=ALU.mult)
                        v.drain()
                        v.scalar_tensor_tensor(
                            out=t_cv, in0=sc[:, 1:1025],
                            scalar=t_convw[:, 4 * cc + 1:4 * cc + 2],
                            in1=t_cv, op0=ALU.mult, op1=ALU.add)
                        return v.scalar_tensor_tensor(
                            out=t_cg2, in0=sc[:, 3:1027],
                            scalar=t_convw[:, 4 * cc + 3:4 * cc + 4],
                            in1=t_cg2, op0=ALU.mult, op1=ALU.add)
                    m_cv = st("vector", _cv, after=[m_ea, m_eh], sem=s_v)

                    m_cm = st("gpsimd",
                              lambda g, ssl=ssl: g.tensor_tensor(
                                  out=t_cs[0], in0=t_cv, in1=t_cg2,
                                  op=ALU.add),
                              after=[m_cv, prev_silu], sem=s_g)
                    st("scalar",
                       lambda a, cc=cc, ssl=ssl: a.activation(
                           out=t_xbc[cc][:], in_=t_cs[0], func=AF.Silu,
                           bias=t_convb[:, cc:cc + 1]),
                       after=[m_cm], sem=s_a)
                    ev_ct[ct] = s_a.mark()
                    prev_silu = ev_ct[ct]
            wfree[sl] = ev_ct[cs]

        st("gpsimd", lambda g: g.dma_start(
            out=t_wdt[:],
            in_=w_in[:, 5120:5136].rearrange("(c p) w -> p c w", p=128)),
           after=[m_tail], sem=s_gd, k=16)
        m_wdt = s_gd.mark()

        def _mmdt(t):
            m = None
            for k in range(16):
                lh = t_wdt[:, 16 * k:16 * (k + 1)]
                m = t.matmul(pb[4][0:16, :], lh, t_u1[k][:, 3:515],
                             start=(k == 0), stop=(k == 15))
                m = t.matmul(pb[5][0:16, :], lh, t_u1[k][:, 515:1027],
                             start=(k == 0), stop=(k == 15))
            return m
        m_mmdt = st("tensor", _mmdt,
                    after=[m_u1all, m_wdt, ev_ct[39], ev_ct[40 - 2]],
                    sem=s_pe)

        def _evdt(a):
            a.activation(out=t_dthm[:, 0:512], in_=pb[4][0:16, :],
                         func=AF.Copy)
            return a.activation(out=t_dthm[:, 512:1024], in_=pb[5][0:16, :],
                                func=AF.Copy)
        m_dt0 = st("scalar", _evdt, after=[m_mmdt], sem=s_a)

        def _sp1(v):
            return v.tensor_scalar(out=t_dthm[:], in0=t_dthm[:],
                                   scalar1=t_dtbcol[0:16, 0:1], scalar2=None,
                                   op0=ALU.add)
        m_sp1 = st("vector", _sp1, after=[m_dt0, m_const], sem=s_v)

        def _sp2(a):
            a.activation(out=t_ldthm[:], in_=t_dthm[:], func=AF.Abs)
            a.drain()
            a.activation(out=t_ldthm[:], in_=t_ldthm[:],
                         func=AF.Exp, scale=-1.0)
            a.drain()
            return a.activation(out=t_ldthm[:], in_=t_ldthm[:],
                                func=AF.Ln, bias=1.0)
        m_sp2 = st("scalar", _sp2, after=[m_sp1], sem=s_a)

        def _sp3(v):
            v.tensor_scalar(out=t_junk[0:16, 0:1024], in0=t_dthm[:],
                            scalar1=0.0, scalar2=None, op0=ALU.max)
            v.drain()
            return v.tensor_tensor(out=t_dthm[:], in0=t_junk[0:16, 0:1024],
                                   in1=t_ldthm[:], op=ALU.add)
        m_sp3 = st("vector", _sp3, after=[m_sp2], sem=s_v)

        m_ldt = st("scalar", lambda a: a.activation(out=t_ldthm[:],
                                                    in_=t_dthm[:], func=AF.Ln),
                   after=[m_sp3], sem=s_a)

        def _cum(v):
            v.tensor_scalar(out=t_junk[0:16, 0:1024], in0=t_dthm[:],
                            scalar1=t_acol[0:16, 0:1], scalar2=None,
                            op0=ALU.mult)
            v.drain()
            m = None
            for c in range(4):
                m = v.tensor_tensor_scan(
                    out=t_cumhm[:, 256 * c:256 * (c + 1)],
                    data0=t_junk[0:16, 256 * c:256 * (c + 1)],
                    data1=t_z16[:, 0:256],
                    initial=0.0, op0=ALU.add, op1=ALU.bypass)
            return m
        m_cum = st("vector", _cum, after=[m_ldt, m_acol, m_nwz], sem=s_v)

        def _whm(v):
            v.tensor_tensor(out=t_bias_hm[:], in0=t_ldthm[:], in1=t_cumhm[:],
                            op=ALU.subtract)
            v.drain()
            m = None
            for c in range(4):
                m = v.tensor_scalar(
                    out=t_whm[:, 256 * c:256 * (c + 1)],
                    in0=t_bias_hm[:, 256 * c:256 * (c + 1)],
                    scalar1=t_cumhm[:, 256 * c + 255:256 * c + 256],
                    scalar2=None, op0=ALU.add)
            return m
        m_whm0 = st("vector", _whm, after=[m_cum], sem=s_v)
        m_whm = st("scalar", lambda a: a.activation(out=t_whm[:], in_=t_whm[:],
                                                    func=AF.Exp),
                   after=[m_whm0], sem=s_a)

        m_ph1 = m_whm
        m_tail = m_ph1

        if debug:
            d_u1 = b.ddbg("d_u1", [16 * 128, 3 + T], BF16)
            d_sg = b.ddbg("d_sg", [16 * 128, T], BF16)
            d_xbc = b.ddbg("d_xbc", [24 * 128, T], BF16)
            d_dtq = b.ddbg("d_dtq", [64, T])

            def _dbg1(sy):
                m = None
                for k in range(16):
                    m = sy.dma_start(out=d_u1[128 * k:128 * (k + 1), :],
                                     in_=t_u1[k][:])
                    s_st.inc(m, 16)
                for k in range(16):
                    m = sy.dma_start(out=d_sg[128 * k:128 * (k + 1), :],
                                     in_=t_sg[k][:])
                    s_st.inc(m, 16)
                for cck in range(24):
                    m = sy.dma_start(out=d_xbc[128 * cck:128 * (cck + 1), :],
                                     in_=t_xbc[cck][:])
                    s_st.inc(m, 16)
                for nm, tt in [(0, t_dthm), (1, t_ldthm), (2, t_cumhm),
                               (3, t_whm)]:
                    m = sy.dma_start(out=d_dtq[16 * nm:16 * (nm + 1), :],
                                     in_=tt[:])
                    s_st.inc(m, 16)
                sy.wait_ge(s_st.h, s_st.n)
                return None
            st("sync", _dbg1, after=[m_ph1, ev_ct[15], m_u1all])
            m_ph1 = st("gpsimd", lambda g: g.engine_nop(),
                       after=[m_ph1, (s_st, s_st.n)], sem=s_g)
            m_tail = m_ph1

        esP1.close()

    if enabled("ph2"):
        esS = ExitStack()
        t_xct = [b.sb(f"t_xct{i}", [128, 128], BF16, scope=esS)
                 for i in range(8)]
        t_bt = [b.sb(f"t_bt{i}", [128, 128], BF16, scope=esS)
                for i in range(8)]
        t_bw = b.sb("t_bw", [128, 512], BF16, scope=esS)
        t_Sb = b.sb("t_Sb", [128, 256], BF16, scope=esS)
        t_e2 = b.sb("t_e2", [128, 512], BF16, scope=esS)
        t_ep = [b.sb(f"t_ep{j}", [128, 512], scope=esS) for j in range(2)]
        t_et = b.sb("t_et", [128, 1024], scope=esS)
        t_mt = [b.sb(f"t_mt{j}", [128, 512], BF16, scope=esS)
                for j in range(2)]
        t_ce = b.sb("t_ce", [128, 512], BF16, scope=esS)
        t_xd = b.sb("t_xd", [128, 256], scope=esS)
        t_y = b.sb("t_y", [128, 256], scope=esS)
        t_g = [b.sb(f"t_g{h}", [128, 256], BF16, scope=esS)
               for h in range(NH)]
        t_sq = b.sb("t_sq", [128, 256], scope=esS)
        t_rhm = b.sb("t_rhm", [16, 256], scope=esS)
        t_biastm = [b.sb(f"t_biastm{j}", [128, 16], scope=esS)
                    for j in range(2)]
        t_wtm = [b.sb(f"t_wtm{j}", [128, 16], scope=esS) for j in range(2)]
        t_cdb = b.sb("t_cdb", [128, 16], scope=esS)
        t_cumlr = b.sb("t_cumlr", [1, 16], scope=esS)
        t_gn = t_sg
        t_dI = [b.sb(f"t_dI{h}", [128, 128], BF16, scope=esS)
                for h in range(NH)]

        def _dib(v):
            m = None
            for h in range(NH):
                m = v.tensor_scalar(out=t_dI[h][:], in0=t_identb[:],
                                    scalar1=t_db[:, h:h + 1], scalar2=None,
                                    op0=ALU.mult)
            return m
        m_dI = st("vector", _dib, after=[m_const, m_db, m_tail], sem=s_v)

        if enabled("pre"):
            m_sz = m_tail
        else:
            def _sz(g):
                for h in range(NH):
                    g.memset(t_S[h][:], 0.0)
                    g.drain()
                return g.engine_nop()
            m_sz = st("gpsimd", _sz, after=[m_tail], sem=s_g)

        m_prev_y = m_tail
        m_prev_mt = m_tail
        m_prev_bc = [m_tail, m_tail]
        m_prev_sc = m_tail
        m_prev_gn = m_tail
        m_prev_gne = m_tail
        m_tr_free = m_tail
        for c in range(4):
            c0 = 256 * c
            def _tr1(t, c0=c0):
                m = None
                for g_ in range(4):
                    for jc in range(2):
                        m = t.transpose(
                            ptb[:, 128 * (2 * g_ + jc):128 * (2 * g_ + jc + 1)],
                            t_xbc[g_][:, c0 + 128 * jc:c0 + 128 * (jc + 1)],
                            t_identb[:])
                return m
            m_t1 = st("tensor", _tr1, after=[m_tail, m_prev_y, m_tr_free,
                                             m_prev_gn], sem=s_pe)

            def _te1(v):
                m = None
                for i in range(8):
                    m = v.tensor_copy(out=t_xct[i][:],
                                      in_=ptb[:, 128 * i:128 * (i + 1)])
                return m
            m_e1 = st("vector", _te1, after=[m_t1], sem=s_v)

            def _tr2(t, c0=c0):
                m = None
                for g_ in range(4):
                    for jc in range(2):
                        m = t.transpose(
                            ptb[:, 128 * (2 * g_ + jc):128 * (2 * g_ + jc + 1)],
                            t_xbc[4 + g_][:, c0 + 128 * jc:c0 + 128 * (jc + 1)],
                            t_identb[:])
                return m
            m_t2 = st("tensor", _tr2, after=[m_e1], sem=s_pe)

            def _te2(v):
                m = None
                for i in range(8):
                    m = v.tensor_copy(out=t_bt[i][:],
                                      in_=ptb[:, 128 * i:128 * (i + 1)])
                return m
            m_e2m = st("vector", _te2, after=[m_t2], sem=s_v)

            def _tr3(t, c0=c0):
                t.transpose(pt32[:, 0:16],
                            t_bias_hm[0:16, c0:c0 + 128], t_identf[0:16, 0:16])
                t.transpose(pt32[:, 16:32],
                            t_bias_hm[0:16, c0 + 128:c0 + 256],
                            t_identf[0:16, 0:16])
                t.transpose(pt32[:, 32:48],
                            t_whm[0:16, c0:c0 + 128], t_identf[0:16, 0:16])
                t.transpose(pt32[:, 48:64],
                            t_whm[0:16, c0 + 128:c0 + 256],
                            t_identf[0:16, 0:16])
                return t.transpose(pt32[0:1, 64:80],
                                   t_cumhm[0:16, c0 + 255:c0 + 256],
                                   t_identf[0:16, 0:16])
            m_t3 = st("tensor", _tr3, after=[m_tail, m_e2m], sem=s_pe)

            def _te3(v):
                v.tensor_copy(out=t_biastm[0][:], in_=pt32[:, 0:16])
                v.tensor_copy(out=t_biastm[1][:], in_=pt32[:, 16:32])
                v.tensor_copy(out=t_wtm[0][:], in_=pt32[:, 32:48])
                v.tensor_copy(out=t_wtm[1][:], in_=pt32[:, 48:64])
                return v.tensor_copy(out=t_cumlr[:], in_=pt32[0:1, 64:80])
            m_e3 = st("vector", _te3, after=[m_t3], sem=s_v)

            m_cdb0 = st("tensor", lambda t: t.matmul(
                pt32[:, 64:80], t_ones_f[:], t_cumlr[:],
                start=True, stop=True), after=[m_e3], sem=s_pe)
            m_cdb = st("scalar", lambda a: a.activation(
                out=t_cdb[:], in_=pt32[:, 64:80], func=AF.Exp),
                after=[m_cdb0], sem=s_a)

            m_ss = m_prev_gn
            m_mtm_p = [m_prev_mt, m_prev_mt]
            m_bc_p = [m_prev_bc[0], m_prev_bc[1]]
            m_y_p = [m_prev_y, m_prev_y]
            m_sc_p = [m_prev_sc, m_prev_sc]
            m_et_prev = m_prev_bc[0]
            m_e2_prev = [m_prev_bc[0], m_prev_bc[1]]
            for h in range(NH):
                g_ = h // 4
                par = h % 2
                pMt = pb[par]
                pBc = pb[2 + par]
                pYS = pb[4 + par]
                yo = 0
                so = 256

                def _mt(t, h=h, g_=g_, c0=c0, pMt=pMt):
                    m = None
                    for jc in range(2):
                        m = t.matmul(
                            pMt[:, 256 * jc:256 * (jc + 1)],
                            t_xbc[4 + g_][:, c0 + 128 * jc:c0 + 128 * (jc + 1)],
                            t_xbc[8 + h][:, c0:c0 + 256],
                            start=True, stop=True)
                    return m
                m_mtp = st("tensor", _mt, after=[m_mtm_p[par], m_e2m],
                           sem=s_pe)

                m_bcp = st("tensor", lambda t, h=h, c0=c0, pBc=pBc: t.matmul(
                    pBc[:, 0:256], t_onesel[0:16, 128 * h:128 * (h + 1)],
                    t_cumhm[0:16, c0:c0 + 256], start=True, stop=True),
                    after=[m_bc_p[par], m_e3], sem=s_pe)

                m_e2b = st("scalar", lambda a, pBc=pBc, par=par: a.activation(
                    out=t_e2[:, 256 * par:256 * (par + 1)], in_=pBc[:, 0:256],
                    func=AF.Exp),
                    after=[m_bcp, m_e2_prev[par], m_mtm_p[par]], sem=s_a)
                m_e2_prev[par] = m_e2b

                def _ep(v, pBc=pBc, par=par):
                    v.tensor_tensor(out=t_ep[par][:, 0:256],
                                    in0=pBc[:, 0:256],
                                    in1=t_masklog[:, 0:256], op=ALU.add)
                    return v.tensor_tensor(out=t_ep[par][:, 256:512],
                                           in0=pBc[:, 0:256],
                                           in1=t_masklog[:, 256:512],
                                           op=ALU.add)
                m_ep = st("vector", _ep, after=[m_bcp], sem=s_v)
                m_bc_p[par] = st("gpsimd", lambda g: g.engine_nop(),
                                 after=[m_ep, m_e2b], sem=s_g)

                def _et(a, h=h, par=par):
                    a.activation(out=t_et[:, 512 * par:512 * par + 256],
                                 in_=t_ep[par][:, 0:256],
                                 func=AF.Exp, bias=t_biastm[0][:, h:h + 1])
                    return a.activation(
                        out=t_et[:, 512 * par + 256:512 * (par + 1)],
                        in_=t_ep[par][:, 256:512],
                        func=AF.Exp, bias=t_biastm[1][:, h:h + 1])
                m_et = st("scalar", _et,
                          after=[m_ep, m_et_prev, m_mtm_p[par]], sem=s_a)
                m_et_prev = m_et

                def _v2(v, h=h, g_=g_, c0=c0, par=par, pMt=pMt):
                    v.tensor_tensor(out=t_mt[0][:, 256 * par:256 * (par + 1)],
                                    in0=pMt[:, 0:256],
                                    in1=t_et[:, 512 * par:512 * par + 256],
                                    op=ALU.mult)
                    v.tensor_tensor(out=t_mt[1][:, 256 * par:256 * (par + 1)],
                                    in0=pMt[:, 256:512],
                                    in1=t_et[:, 512 * par + 256:512 * (par + 1)],
                                    op=ALU.mult)
                    v.tensor_tensor(out=t_ce[:, 256 * par:256 * (par + 1)],
                                    in0=t_xbc[8 + h][:, c0:c0 + 256],
                                    in1=t_e2[:, 256 * par:256 * (par + 1)],
                                    op=ALU.mult)
                    v.tensor_copy(out=t_Sb[:, 128 * par:128 * (par + 1)],
                                  in_=t_S[h][:])
                    v.tensor_scalar(out=t_bw[:, 256 * par:256 * par + 128],
                                    in0=t_bt[2 * g_][:],
                                    scalar1=t_wtm[0][:, h:h + 1], scalar2=None,
                                    op0=ALU.mult)
                    return v.tensor_scalar(
                        out=t_bw[:, 256 * par + 128:256 * (par + 1)],
                        in0=t_bt[2 * g_ + 1][:],
                        scalar1=t_wtm[1][:, h:h + 1], scalar2=None,
                        op0=ALU.mult)
                aft_v2 = [m_et, m_mtp, m_e2b, m_mtm_p[par], m_sc_p[par]]
                if c == 0 and h < 2:
                    aft_v2.append(m_sz)
                m_v2 = st("vector", _v2, after=aft_v2, sem=s_v)
                m_mtm_p[par] = m_v2

                def _y(t, h=h, g_=g_, c0=c0, par=par, pYS=pYS, yo=yo, so=so):
                    t.matmul(pYS[:, yo:yo + 256], t_xct[2 * g_][:],
                             t_mt[0][:, 256 * par:256 * (par + 1)],
                             start=True, stop=False)
                    t.matmul(pYS[:, yo:yo + 256], t_xct[2 * g_ + 1][:],
                             t_mt[1][:, 256 * par:256 * (par + 1)],
                             start=False, stop=False)
                    t.matmul(pYS[:, yo:yo + 256],
                             t_Sb[:, 128 * par:128 * (par + 1)],
                             t_ce[:, 256 * par:256 * (par + 1)],
                             start=False, stop=False)
                    t.matmul(pYS[:, yo:yo + 256], t_dI[h][:],
                             t_xbc[g_][:, c0:c0 + 256],
                             start=False, stop=True)
                    t.matmul(pYS[:, so:so + 128],
                             t_bw[:, 256 * par:256 * par + 128],
                             t_xct[2 * g_][:], start=True, stop=False)
                    return t.matmul(pYS[:, so:so + 128],
                                    t_bw[:, 256 * par + 128:256 * (par + 1)],
                                    t_xct[2 * g_ + 1][:], start=False,
                                    stop=True)
                m_yp = st("tensor", _y,
                          after=[m_v2, m_y_p[par], m_sc_p[par], m_e1] +
                          ([m_dI] if (c == 0 and h == 0) else []), sem=s_pe)

                def _v3(v, h=h, g_=g_, c0=c0, par=par, pYS=pYS, yo=yo,
                        so=so):
                    v.tensor_tensor(out=t_g[h][:], in0=pYS[:, yo:yo + 256],
                                    in1=t_sg[h][:, c0:c0 + 256],
                                    op=ALU.mult)
                    v.scalar_tensor_tensor(
                        out=t_S[h][:], in0=t_S[h][:],
                        scalar=t_cdb[:, h:h + 1], in1=pYS[:, so:so + 128],
                        op0=ALU.mult, op1=ALU.add)
                    v.drain()
                    return v.tensor_tensor(out=t_sq[:], in0=t_g[h][:],
                                           in1=t_g[h][:], op=ALU.mult)
                m_v3 = st("vector", _v3, after=[m_yp, m_cdb, m_ss], sem=s_v)
                m_y_p[par] = m_v3
                m_sc_p[par] = m_v3

                m_ss = st("tensor", lambda t, h=h: t.matmul(
                    pt32[0:16, 256:512], t_oco[:, 16 * h:16 * (h + 1)],
                    t_sq[:], start=(h == 0), stop=(h == 15)),
                    after=[m_v3] + ([m_prev_gn] if h == 0 else []),
                    sem=s_pe)
            m_prev_mt = m_mtm_p[1]
            m_prev_bc = m_bc_p
            m_prev_y = m_y_p[1]
            m_prev_sc = m_sc_p[1]
            m_bw = m_v2
            m_tr_free = m_bw

            def _rhm(a):
                a.activation(out=t_rhm[:], in_=pt32[0:16, 256:512],
                             func=AF.Ln, scale=1.0 / 128.0,
                             bias=t_eps[0:16, 0:1])
                a.drain()
                return a.activation(out=t_rhm[:], in_=t_rhm[:],
                                    func=AF.Exp, scale=-0.5)
            m_rhm = st("scalar", _rhm, after=[m_ss], sem=s_a)

            m_gne_p = [m_prev_gne, m_prev_gne]
            for h in range(NH):
                par = h % 2
                m_gnp = st("tensor", lambda t, h=h, par=par: t.matmul(
                    pb[par][:, 0:256], t_nwsel[0:16, 128 * h:128 * (h + 1)],
                    t_rhm[0:16, 0:256], start=True, stop=True),
                    after=[m_rhm, m_gne_p[par], m_prev_mt], sem=s_pe)
                m_gne = st("vector", lambda v, h=h, c0=c0, par=par:
                           v.tensor_tensor(
                               out=t_gn[h][:, c0:c0 + 256], in0=t_g[h][:],
                               in1=pb[par][:, 0:256], op=ALU.mult),
                           after=[m_gnp], sem=s_v)
                m_gne_p[par] = m_gne
            m_prev_gne = st("gpsimd", lambda g: g.engine_nop(),
                            after=[m_gne_p[0], m_gne_p[1]], sem=s_g)
            m_prev_gn = m_prev_gne
            m_prev_mt = m_prev_gne
            m_prev_bc = [m_prev_gne, m_prev_gne]

        esS.close()
        m_ph2 = m_prev_gne
        m_tail = m_ph2

        if debug:
            d_gn = b.ddbg("d_gn", [16 * 128, T], BF16)
            d_S = b.ddbg("d_S", [16 * 128, 128])

            def _dbg2(sy):
                for k in range(16):
                    s_st.inc(sy.dma_start(
                        out=d_gn[128 * k:128 * (k + 1), :], in_=t_gn[k][:]),
                        16)
                for h in range(16):
                    s_st.inc(sy.dma_start(
                        out=d_S[128 * h:128 * (h + 1), :], in_=t_S[h][:]), 16)
                sy.wait_ge(s_st.h, s_st.n)
                return None
            st("sync", _dbg2, after=[m_ph2, m_prev_sc])

    if enabled("ph3"):
        esA2.close()
        es3 = ExitStack()
        t_wo = [b.sb(f"t_wo{k}", [128, HID], BF16, scope=es3)
                for k in range(16)]
        t_x3 = [b.sb(f"t_x3_{i}", [128, HID], scope=es3) for i in range(2)]
        t_ev3 = b.sb("t_ev3", [128, 1024], scope=es3)
        t_gar = b.sb("t_gar", [1, HID], BF16, scope=es3)
        t_gaj = b.sb("t_gaj", [1, HID], scope=es3)
        t_gab = b.sb("t_gab", [128, HID], BF16, scope=es3)
        s_wo = b.sem("s_wo")
        s_x3 = [b.sem("s_x3_0"), b.sem("s_x3_1")]
        s_xss = [b.sem("s_xss0"), b.sem("s_xss1")]

        st("sync", lambda sy: sy.dma_start(
            out=t_gaj[:], in_=modsc[0:1, 2 * 2048:3 * 2048]),
           after=[m_tail, mod_evac[2][1]], sem=s_hd, k=16)
        m_gl = s_hd.mark()
        def _gs(a):
            a.activation(out=t_gar[:], in_=t_gaj[:], func=AF.Sign)
            return a.activation(out=t_gaj[:], in_=t_gaj[:], func=AF.Abs)
        m_gs = st("scalar", _gs, after=[m_gl], sem=s_a)

        def _gm(v):
            v.tensor_scalar(out=t_gaj[:], in0=t_gaj[:], scalar1=0.25,
                            scalar2=None, op0=ALU.max)
            v.drain()
            return v.tensor_tensor(out=t_gar[:], in0=t_gaj[:], in1=t_gar[:],
                                   op=ALU.mult)
        m_gm = st("vector", _gm, after=[m_gs], sem=s_v)
        m_gab3 = bcast_row(t_gar, t_gab, HID, True, [m_gm, m_tail])

        for k in range(16):
            st("gpsimd", lambda g, k=k: g.dma_start(
                out=t_wo[k][:], in_=w_out[128 * k:128 * (k + 1), :]),
               after=([m_tail] if k == 0 else []), sem=s_wo, k=16)
        m_wol = s_wo.mark()

        m_ev_prev = [m_gab3, m_gab3]
        m_xs_done = []
        for tc in range(8):
            sl = tc % 2
            aft = [m_tail] if tc < 2 else [m_xs_done[tc - 2]]
            st("sync", lambda sy, tc=tc, sl=sl: sy.dma_start(
                out=t_x3[sl][:], in_=xe[3 + 128 * tc:3 + 128 * (tc + 1), :]),
               after=aft, sem=s_x3[sl], k=16)
            m_xl = (s_x3[sl], s_x3[sl].n)
            for half in range(2):
                pA, pB = (pb[0], pb[1]) if half == 0 else (pb[2], pb[3])

                def _wo(t, tc=tc, half=half, pA=pA, pB=pB):
                    m = None
                    for k in range(16):
                        lh = t_gn[k][:, 128 * tc:128 * (tc + 1)]
                        m = t.matmul(pA[:, :], lh,
                                     t_wo[k][:, 1024 * half:1024 * half + 512],
                                     start=(k == 0), stop=(k == 15))
                        m = t.matmul(pB[:, :], lh,
                                     t_wo[k][:, 1024 * half + 512:
                                             1024 * (half + 1)],
                                     start=(k == 0), stop=(k == 15))
                    return m
                m_mm = st("tensor", _wo,
                          after=[m_wol, m_tail, m_ev_prev[half]], sem=s_pe)

                def _ev(v, tc=tc, half=half, pA=pA, pB=pB, sl=sl):
                    m = None
                    for j, pX in enumerate([pA, pB]):
                        off = 1024 * half + 512 * j
                        v.tensor_tensor(out=t_ev3[:, 512 * j:512 * (j + 1)],
                                        in0=pX[:, :],
                                        in1=t_gab[:, off:off + 512],
                                        op=ALU.mult)
                    v.drain()
                    for j in range(2):
                        off = 1024 * half + 512 * j
                        m = v.tensor_tensor(out=t_x3[sl][:, off:off + 512],
                                            in0=t_x3[sl][:, off:off + 512],
                                            in1=t_ev3[:, 512 * j:512 * (j + 1)],
                                            op=ALU.add)
                    return m
                m_ev_prev[half] = st("vector", _ev,
                                     after=[m_mm, m_xl, m_gab3], sem=s_v)
            st("sync", lambda sy, tc=tc, sl=sl: sy.dma_start(
                out=xsc[128 * tc:128 * (tc + 1), :], in_=t_x3[sl][:]),
               after=[m_ev_prev[0], m_ev_prev[1]], sem=s_xss[sl], k=16)
            m_xs_done.append((s_xss[sl], s_xss[sl].n))

        m_ph3 = st("gpsimd", lambda g: g.engine_nop(),
                   after=[(s_xss[0], s_xss[0].n), (s_xss[1], s_xss[1].n),
                          m_ev_prev[0], m_ev_prev[1]], sem=s_g)
        m_tail = m_ph3

        if debug:
            d_x3 = b.ddbg("d_x3", [T, HID])

            def _dbg3(sy):
                m = sy.dma_start(out=d_x3, in_=xsc)
                s_st.inc(m, 16)
                sy.wait_ge(s_st.h, s_st.n)
                return None
            st("sync", _dbg3, after=[m_ph3])
            m_ph3 = st("gpsimd", lambda g: g.engine_nop(),
                       after=[m_ph3, (s_st, s_st.n)], sem=s_g)
            m_tail = m_ph3

        es3.close()
        esA.close()

    if enabled("ph4"):
        esL.close()
        tern_g = b.dint("tern_g", [64, 128, 2048], BF16)
        tern_v = b.dint("tern_v", [64, 128, 2048], BF16)
        tern_o = b.dint("tern_o", [16, 4, 128, 2048], BF16)
        es4 = ExitStack()
        t_gbb4 = b.sb("t_gbb4", [128, HID], BF16, scope=es4)
        t_r4 = b.sb("t_r4", [1, HID], scope=es4)
        t_amx = b.sb("t_amx", [128, 512], scope=es4)
        t_amn = b.sb("t_amn", [128, 512], scope=es4)
        t_amc = b.sb("t_amc", [128, 8], scope=es4)
        t_srow = b.sb("t_srow", [1, 1024], scope=es4)
        t_s1b = b.sb("t_s1b", [128, 512], scope=es4)
        t_s1ib = b.sb("t_s1ib", [128, 512], scope=es4)
        t_s2b = t_s1b
        t_s2ib = t_s1ib
        t_qq = b.sb("t_qq", [128, 512], BF16, scope=es4)
        t_vv = b.sb("t_vv", [128, 512], BF16, scope=es4)
        t_e1 = t_s1b
        t_stg = [b.sb(f"t_stg{i}", [128, 2048], scope=es4) for i in range(4)]
        t_q3 = [b.sb(f"t_q3_{i}", [128, 2048], BF16, scope=es4)
                for i in range(4)]
        t_xdq = [b.sb(f"t_xdq{k}", [128, 512], BF16, scope=es4)
                 for k in range(16)]
        s_stg = [b.sem(f"s_stg{i}") for i in range(4)]
        s_tn = [b.sem(f"s_tn{i}") for i in range(4)]
        s_x4 = [b.sem(f"s_x4_{i}") for i in range(4)]
        s_sto = b.sem("s_sto")

        t_ob4 = b.sb("t_ob4", [1, HID], scope=es4)
        st("sync", lambda sy: sy.dma_start(
            out=t_r4[:], in_=modsc[0:1, 5 * 2048:6 * 2048]),
           after=[m_tail, mod_evac[5][1]], sem=s_hd, k=16)
        st("sync", lambda sy: sy.dma_start(out=t_ob4[:], in_=obrow),
           sem=s_hd, k=16)
        m_r4 = None

        st("sync", lambda sy: sy.dma_start(
            out=t_gbgc[:],
            in_=modsc[0:1, 5 * 2048:6 * 2048].rearrange(
                "a (k p) -> p a k", p=128).rearrange("p a k -> (p a) k"),
            allow_slow_non_contiguous=True),
           after=[m_tail], sem=s_hd, k=16)
        m_gbc0 = s_hd.mark()
        m_r4 = m_gbc0

        def _gb4(a):
            a.activation(out=t_gbb4[0:1, 0:HID], in_=t_r4[:], func=AF.Sign)
            return a.activation(out=t_r4[:], in_=t_r4[:], func=AF.Abs)
        m_gb4a = st("scalar", _gb4, after=[m_r4], sem=s_a)

        def _gb4b(v):
            v.tensor_scalar(out=t_r4[:], in0=t_r4[:], scalar1=0.25,
                            scalar2=None, op0=ALU.max)
            v.drain()
            v.tensor_tensor(out=t_r4[:], in0=t_r4[:],
                            in1=t_gbb4[0:1, 0:HID], op=ALU.mult)
            v.drain()
            return v.tensor_tensor(out=t_r4[:], in0=t_r4[:], in1=t_ob4[:],
                                   op=ALU.mult)
        m_gb4b = st("vector", _gb4b, after=[m_gb4a], sem=s_v)
        m_gbb4 = bcast_row(t_r4, t_gbb4, HID, False, [m_gb4b, m_tail])


        m_gbc1a = st("scalar", lambda a: a.activation(
            out=t_col[:, 0:16], in_=t_gbgc[:], func=AF.Abs),
            after=[m_gbc0], sem=s_a)
        m_gbc2 = st("scalar", lambda a: a.activation(
            out=t_gbgc[:], in_=t_gbgc[:], func=AF.Sign),
            after=[m_gbc1a], sem=s_a)

        def _gbc3(v):
            v.tensor_scalar(out=t_col[:, 0:16], in0=t_col[:, 0:16],
                            scalar1=0.25, scalar2=None, op0=ALU.max)
            v.drain()
            v.tensor_tensor(out=t_gbgc[:], in0=t_gbgc[:], in1=t_col[:, 0:16],
                            op=ALU.mult)
            v.drain()
            return v.tensor_scalar(out=t_gbgc[:], in0=t_gbgc[:],
                                   scalar1=t_gamma[:, 10:11], scalar2=None,
                                   op0=ALU.mult)
        m_gbgc = st("vector", _gbc3, after=[m_gbc2, m_gamt], sem=s_v)

        m_tg_done = m_gbb4
        for tg in range(2):
            tg0 = 512 * tg
            es4a = ExitStack()
            t_h2 = [b.sb(f"t_h2_{tg}_{k}", [128, 512], scope=es4a)
                    for k in range(16)]
            t_x4a = [b.sb(f"t_x4a{tg}_{i}", [128, HID], scope=es4a)
                     for i in range(2)]
            t_ut4 = b.sb(f"t_ut4_{tg}", [128, HID], BF16, scope=es4a)

            m_u4 = [None] * 4
            m_tw4 = [None] * 4
            for i in range(4):
                sl = i % 2
                aft_l = [m_tg_done] if i < 2 else [m_u4[i - 2]]
                st("sync", lambda sy, i=i, sl=sl, tg0=tg0: sy.dma_start(
                    out=t_x4a[sl][:],
                    in_=xsc[tg0 + 128 * i:tg0 + 128 * (i + 1), :]),
                   after=aft_l, sem=s_x4[sl], k=16)
                m_xl = (s_x4[sl], s_x4[sl].n)
                aft_x = [m_xl] + ([m_tw4[i - 1]] if i >= 1 else [])
                m_u4[i] = emit_ln(t_x4a[sl], t_ut4, 128, aft_x)
                prev_w = None
                for w in range(2):
                    def _tw(t, w=w):
                        m = None
                        for kk in range(8):
                            k = 8 * w + kk
                            m = t.transpose(
                                ptb[:, 128 * kk:128 * (kk + 1)],
                                t_ut4[:, 128 * k:128 * (k + 1)],
                                t_identb[:])
                        return m
                    aft = [m_u4[i]] + ([prev_w] if prev_w else
                                       ([m_tw4[i - 1]] if i >= 1 else
                                        [m_tg_done]))
                    m_t = st("tensor", _tw, after=aft, sem=s_pe)

                    def _ev(v, i=i, w=w):
                        m = None
                        for kk in range(8):
                            k = 8 * w + kk
                            m = v.tensor_scalar(
                                out=t_h2[k][:, 128 * i:128 * (i + 1)],
                                in0=ptb[:, 128 * kk:128 * (kk + 1)],
                                scalar1=t_modc[:, 48 + k:49 + k],
                                scalar2=t_modc[:, 32 + k:33 + k],
                                op0=ALU.mult, op1=ALU.add)
                            v.drain()
                        return m
                    prev_w = st("vector", _ev, after=[m_t, m_modc], sem=s_v)
                m_tw4[i] = prev_w
            m_h2 = m_tw4[3]

            def _tree(v):
                v.tensor_copy(out=t_amx[:], in_=t_h2[0][:])
                v.drain()
                v.tensor_copy(out=t_amn[:], in_=t_h2[0][:])
                v.drain()
                for k in range(1, 16):
                    v.tensor_tensor(out=t_amx[:], in0=t_amx[:],
                                    in1=t_h2[k][:], op=ALU.max)
                    v.drain()
                    v.tensor_tensor(out=t_amn[:], in0=t_amn[:],
                                    in1=t_h2[k][:], op=ALU.min)
                    v.drain()
                v.tensor_scalar(out=t_amn[:], in0=t_amn[:], scalar1=-1.0,
                                scalar2=None, op0=ALU.mult)
                v.drain()
                v.tensor_tensor(out=t_amx[:], in0=t_amx[:], in1=t_amn[:],
                                op=ALU.max)
                v.drain()
                return v.engine_nop()
            m_tree = st("vector", _tree, after=[m_h2], sem=s_v)

            def _tp1(t):
                m = None
                for blk in range(4):
                    m = t.transpose(pt32[:, 128 * blk:128 * (blk + 1)],
                                    t_amx[:, 128 * blk:128 * (blk + 1)],
                                    t_identf[:])
                return m
            m_tp1 = st("tensor", _tp1, after=[m_tree, m_tg_done], sem=s_pe)

            def _red(v):
                for blk in range(4):
                    v.tensor_reduce(out=t_amc[:, blk:blk + 1],
                                    in_=pt32[:, 128 * blk:128 * (blk + 1)],
                                    axis=AXX, op=ALU.max)
                v.drain()
                v.tensor_scalar(out=t_amc[:, 0:4], in0=t_amc[:, 0:4],
                                scalar1=1e-5, scalar2=None, op0=ALU.max)
                v.drain()
                return v.reciprocal(out=t_amc[:, 4:8], in_=t_amc[:, 0:4])
            m_red = st("vector", _red, after=[m_tp1], sem=s_v)

            def _tp2(t):
                m = None
                for blk in range(4):
                    m = t.transpose(pt32[0:1, 128 * blk:128 * (blk + 1)],
                                    t_amc[:, 4 + blk:5 + blk], t_identf[:])
                return m
            m_tp2 = st("tensor", _tp2, after=[m_red], sem=s_pe)
            m_sr1 = st("vector", lambda v: v.tensor_copy(
                out=t_srow[0:1, 0:512], in_=pt32[0:1, 0:512]),
                after=[m_tp2], sem=s_v)

            def _tp3(t):
                m = None
                for blk in range(4):
                    m = t.transpose(pt32[0:1, 128 * blk:128 * (blk + 1)],
                                    t_amc[:, blk:blk + 1], t_identf[:])
                return m
            m_tp3 = st("tensor", _tp3, after=[m_sr1], sem=s_pe)
            m_sr2 = st("vector", lambda v: v.tensor_copy(
                out=t_srow[0:1, 512:1024], in_=pt32[0:1, 0:512]),
                after=[m_tp3], sem=s_v)

            m_b1 = st("tensor", lambda t: t.matmul(
                pt32[:, 0:512], t_ones_f[:], t_srow[0:1, 0:512],
                start=True, stop=True), after=[m_sr2], sem=s_pe)
            m_s1b = st("vector", lambda v: v.tensor_scalar(
                out=t_s1b[:], in0=pt32[:, 0:512], scalar1=127.0, scalar2=None,
                op0=ALU.mult), after=[m_b1], sem=s_v)
            m_b2 = st("tensor", lambda t: t.matmul(
                pt32[:, 0:512], t_ones_f[:], t_srow[0:1, 512:1024],
                start=True, stop=True), after=[m_s1b], sem=s_pe)
            m_s1ib = st("vector", lambda v: v.tensor_scalar(
                out=t_s1ib[:], in0=pt32[:, 0:512], scalar1=1.0 / 127.0,
                scalar2=None, op0=ALU.mult), after=[m_b2], sem=s_v)

            qsc = ((t_amx, t_qq), (t_amn, t_vv))
            m_xdq_all = None
            for kp in range(8):
                def _q(v, kp=kp):
                    for j, (sa, sq) in enumerate(qsc):
                        v.tensor_tensor(out=sa[:], in0=t_h2[2 * kp + j][:],
                                        in1=t_s1b[:], op=ALU.mult)
                    v.drain()
                    for j, (sa, sq) in enumerate(qsc):
                        v.tensor_scalar(out=sq[:], in0=sa[:], scalar1=MAGIC,
                                        scalar2=-MAGIC, op0=ALU.add,
                                        op1=ALU.add)
                    v.drain()
                    m = None
                    for j, (sa, sq) in enumerate(qsc):
                        m = v.tensor_tensor(out=t_xdq[2 * kp + j][:],
                                            in0=sq[:], in1=t_s1ib[:],
                                            op=ALU.mult)
                    return m
                m_xdq_all = st("vector", _q, after=[m_s1b, m_s1ib], sem=s_v)

            es4a.close()
            es4b = ExitStack()
            t_gv = [b.sb(f"t_gv{tg}_{h}", [128, 512], BF16, scope=es4b)
                    for h in range(64)]
            t_x4 = [b.sb(f"t_x4_{tg}_{i}", [128, HID], scope=es4b)
                    for i in range(4)]

            GV = {
                "g": dict(stg=(t_stg[0], t_stg[1]), q3=(t_q3[0], t_q3[1]),
                          src=gwB, tdr=tern_g,
                          ssem=(s_stg[0], s_stg[1]), tsem=(s_tn[0], s_tn[1]),
                          ghi=t_gamma[:, 0:1], glo=t_gamma[:, 4:5]),
                "v": dict(stg=(t_stg[2], t_stg[3]), q3=(t_q3[2], t_q3[3]),
                          src=vwB, tdr=tern_v,
                          ssem=(s_stg[2], s_stg[3]), tsem=(s_tn[2], s_tn[3]),
                          ghi=t_gamma[:, 1:2], glo=t_gamma[:, 5:6]),
            }
            ST = {k: dict(stgfree=[m_xdq_all, m_xdq_all],
                          q3free=[m_xdq_all, m_xdq_all],
                          ld={}, tn={}, wr={}) for k in GV}

            def eload(k, h):
                d, s = GV[k], ST[k]
                sl = h % 2
                st("sync", lambda sy, d=d, h=h, sl=sl: sy.dma_start(
                    out=d["stg"][sl][:], in_=d["src"][h]),
                   after=[s["stgfree"][sl]], sem=d["ssem"][sl], k=16)
                s["ld"][h] = d["ssem"][sl].mark()

            def etern(k, h):
                d, s = GV[k], ST[k]
                sl = h % 2

                def _tq(v, d=d, sl=sl):
                    v.tensor_scalar(out=d["q3"][sl][:], in0=d["stg"][sl][:],
                                    scalar1=d["glo"], scalar2=None,
                                    op0=ALU.is_lt)
                    v.drain()
                    return v.scalar_tensor_tensor(
                        out=d["q3"][sl][:], in0=d["stg"][sl][:],
                        scalar=d["ghi"], in1=d["q3"][sl][:],
                        op0=ALU.is_gt, op1=ALU.subtract)
                m1 = st("vector", _tq,
                        after=[s["ld"][h], s["q3free"][sl]], sem=s_v)
                s["stgfree"][sl] = m1
                s["tn"][h] = m1
                st("sync", lambda sy, d=d, h=h, sl=sl: sy.dma_start(
                    out=d["tdr"][h], in_=d["q3"][sl][:]),
                   after=[m1], sem=d["tsem"][sl], k=16)
                s["wr"][h] = d["tsem"][sl].mark()

            def eload_t(k, h):
                d, s = GV[k], ST[k]
                sl = h % 2
                st("sync", lambda sy, d=d, h=h, sl=sl: sy.dma_start(
                    out=d["q3"][sl][:], in_=d["tdr"][h]),
                   after=[s["q3free"][sl]], sem=d["tsem"][sl], k=16)
                s["tn"][h] = d["tsem"][sl].mark()

            def etern_gv(h):
                sl = h % 2
                dg, dv = GV["g"], GV["v"]

                def _tq(v, sl=sl):
                    v.tensor_scalar(out=dg["q3"][sl][:], in0=dg["stg"][sl][:],
                                    scalar1=dg["glo"], scalar2=None,
                                    op0=ALU.is_lt)
                    v.tensor_scalar(out=dv["q3"][sl][:], in0=dv["stg"][sl][:],
                                    scalar1=dv["glo"], scalar2=None,
                                    op0=ALU.is_lt)
                    v.drain()
                    v.scalar_tensor_tensor(
                        out=dg["q3"][sl][:], in0=dg["stg"][sl][:],
                        scalar=dg["ghi"], in1=dg["q3"][sl][:],
                        op0=ALU.is_gt, op1=ALU.subtract)
                    return v.scalar_tensor_tensor(
                        out=dv["q3"][sl][:], in0=dv["stg"][sl][:],
                        scalar=dv["ghi"], in1=dv["q3"][sl][:],
                        op0=ALU.is_gt, op1=ALU.subtract)
                m1 = st("vector", _tq,
                        after=[ST["g"]["ld"][h], ST["v"]["ld"][h],
                               ST["g"]["q3free"][sl], ST["v"]["q3free"][sl]],
                        sem=s_v)
                for k in ("g", "v"):
                    d, s = GV[k], ST[k]
                    s["stgfree"][sl] = m1
                    s["tn"][h] = m1
                    st("sync", lambda sy, d=d, h=h, sl=sl: sy.dma_start(
                        out=d["tdr"][h], in_=d["q3"][sl][:]),
                       after=[m1], sem=d["tsem"][sl], k=16)
                    s["wr"][h] = d["tsem"][sl].mark()

            m_qqfree = m_xdq_all
            m_vvfree = m_xdq_all
            m_amax = m_xdq_all
            m_evg_prev = [m_xdq_all, m_xdq_all]
            m_evv_prev = [m_xdq_all, m_xdq_all]
            if tg == 0:
                eload("g", 0), eload("v", 0), eload("g", 1), eload("v", 1)
                etern_gv(0)
            else:
                eload_t("g", 0), eload_t("v", 0)
                eload_t("g", 1), eload_t("v", 1)
            for h in range(64):
                gsl = h % 2
                if tg == 0:
                    if h + 2 < 64:
                        eload("g", h + 2)
                        eload("v", h + 2)
                    if h + 1 < 64:
                        etern_gv(h + 1)
                pG = pb[0] if h % 2 == 0 else pb[1]

                def _gmm(t, h=h, pG=pG, gsl=gsl):
                    m = None
                    for cdx in range(16):
                        m = t.matmul(pG[:, :],
                                     t_q3[gsl][:, 128 * cdx:128 * (cdx + 1)],
                                     t_xdq[cdx][:], start=(cdx == 0),
                                     stop=(cdx == 15))
                    return m
                m_gmm = st("tensor", _gmm,
                           after=[ST["g"]["tn"][h], m_evg_prev[h % 2]],
                           sem=s_pe)
                ST["g"]["q3free"][gsl] = (
                    m_gmm if tg == 1 else
                    st("gpsimd", lambda g: g.engine_nop(),
                       after=[m_gmm, ST["g"]["wr"][h]], sem=s_g))
                if tg == 1 and h + 2 < 64:
                    eload_t("g", h + 2)

                m_evg = st("scalar", lambda a, h=h, pG=pG: a.activation(
                    out=t_qq[:], in_=pG[:, :], func=AF.Silu,
                    scale=t_gamma[:, 8:9], bias=t_gb2[:, h:h + 1]),
                    after=[m_gmm, m_gamt, m_qqfree], sem=s_a)
                m_evg_prev[h % 2] = m_evg

                pV = pb[2] if h % 2 == 0 else pb[3]

                def _vmm(t, h=h, pV=pV, gsl=gsl):
                    m = None
                    for cdx in range(16):
                        m = t.matmul(
                            pV[:, :],
                            t_q3[2 + gsl][:, 128 * cdx:128 * (cdx + 1)],
                            t_xdq[cdx][:], start=(cdx == 0), stop=(cdx == 15))
                    return m
                m_vmm = st("tensor", _vmm,
                           after=[ST["v"]["tn"][h], m_evv_prev[h % 2]],
                           sem=s_pe)
                ST["v"]["q3free"][gsl] = (
                    m_vmm if tg == 1 else
                    st("gpsimd", lambda g: g.engine_nop(),
                       after=[m_vmm, ST["v"]["wr"][h]], sem=s_g))
                if tg == 1 and h + 2 < 64:
                    eload_t("v", h + 2)

                def _evv(v, h=h, pV=pV):
                    v.tensor_scalar(out=t_vv[:], in0=pV[:, :],
                                    scalar1=t_gamma[:, 9:10],
                                    scalar2=t_vb2[:, h:h + 1],
                                    op0=ALU.mult, op1=ALU.add)
                    v.drain()
                    v.tensor_tensor(out=t_gv[h][:], in0=t_qq[:], in1=t_vv[:],
                                    op=ALU.mult)
                    v.drain()
                    if h == 0:
                        v.tensor_copy(out=t_amx[:], in_=t_gv[0][:])
                        return v.tensor_copy(out=t_amn[:], in_=t_gv[0][:])
                    v.tensor_tensor(out=t_amx[:], in0=t_amx[:],
                                    in1=t_gv[h][:], op=ALU.max)
                    return v.tensor_tensor(out=t_amn[:], in0=t_amn[:],
                                           in1=t_gv[h][:], op=ALU.min)
                m_evv = st("vector", _evv,
                           after=[m_vmm, m_evg, m_gamt, m_vvfree], sem=s_v)
                m_evv_prev[h % 2] = m_evv
                m_qqfree = m_evv
                m_vvfree = m_evv
            m_gvdone = m_evv_prev[(64 - 1) % 2]

            def _cmb2(v):
                v.tensor_scalar(out=t_amn[:], in0=t_amn[:], scalar1=-1.0,
                                scalar2=None, op0=ALU.mult)
                v.drain()
                return v.tensor_tensor(out=t_amx[:], in0=t_amx[:],
                                       in1=t_amn[:], op=ALU.max)
            m_gvdone = st("vector", _cmb2, after=[m_gvdone], sem=s_v)
            m_tp1b = st("tensor", lambda t: [
                t.transpose(pt32[:, 128 * blk:128 * (blk + 1)],
                            t_amx[:, 128 * blk:128 * (blk + 1)],
                            t_identf[:]) for blk in range(4)][-1],
                after=[m_gvdone], sem=s_pe)

            def _red2(v):
                for blk in range(4):
                    v.tensor_reduce(out=t_amc[:, blk:blk + 1],
                                    in_=pt32[:, 128 * blk:128 * (blk + 1)],
                                    axis=AXX, op=ALU.max)
                v.drain()
                v.tensor_scalar(out=t_amc[:, 0:4], in0=t_amc[:, 0:4],
                                scalar1=1e-5, scalar2=None, op0=ALU.max)
                v.drain()
                return v.reciprocal(out=t_amc[:, 4:8], in_=t_amc[:, 0:4])
            m_red2 = st("vector", _red2, after=[m_tp1b], sem=s_v)

            m_tp2b = st("tensor", lambda t: [
                t.transpose(pt32[0:1, 128 * blk:128 * (blk + 1)],
                            t_amc[:, 4 + blk:5 + blk], t_identf[:])
                for blk in range(4)][-1], after=[m_red2], sem=s_pe)
            m_sr1b = st("vector", lambda v: v.tensor_copy(
                out=t_srow[0:1, 0:512], in_=pt32[0:1, 0:512]),
                after=[m_tp2b], sem=s_v)
            m_tp3b = st("tensor", lambda t: [
                t.transpose(pt32[0:1, 128 * blk:128 * (blk + 1)],
                            t_amc[:, blk:blk + 1], t_identf[:])
                for blk in range(4)][-1], after=[m_sr1b], sem=s_pe)
            m_sr2b = st("vector", lambda v: v.tensor_copy(
                out=t_srow[0:1, 512:1024], in_=pt32[0:1, 0:512]),
                after=[m_tp3b], sem=s_v)
            m_b1b = st("tensor", lambda t: t.matmul(
                pt32[:, 0:512], t_ones_f[:], t_srow[0:1, 0:512],
                start=True, stop=True), after=[m_sr2b], sem=s_pe)
            m_s2b = st("vector", lambda v: v.tensor_scalar(
                out=t_s2b[:], in0=pt32[:, 0:512], scalar1=127.0, scalar2=None,
                op0=ALU.mult), after=[m_b1b], sem=s_v)
            m_b2b = st("tensor", lambda t: t.matmul(
                pt32[:, 0:512], t_ones_f[:], t_srow[0:1, 512:1024],
                start=True, stop=True), after=[m_s2b], sem=s_pe)
            m_s2ib = st("vector", lambda v: v.tensor_scalar(
                out=t_s2ib[:], in0=pt32[:, 0:512], scalar1=1.0 / 127.0,
                scalar2=None, op0=ALU.mult), after=[m_b2b], sem=s_v)

            m_q2_all = None
            for hp in range(32):
                def _q2(v, hp=hp):
                    for j, (sa, sq) in enumerate(qsc):
                        v.tensor_tensor(out=sa[:], in0=t_gv[2 * hp + j][:],
                                        in1=t_s2b[:], op=ALU.mult)
                    v.drain()
                    for j, (sa, sq) in enumerate(qsc):
                        v.tensor_scalar(out=sq[:], in0=sa[:], scalar1=MAGIC,
                                        scalar2=-MAGIC, op0=ALU.add,
                                        op1=ALU.add)
                    v.drain()
                    m = None
                    for j, (sa, sq) in enumerate(qsc):
                        m = v.tensor_tensor(out=t_gv[2 * hp + j][:],
                                            in0=sq[:], in1=t_s2ib[:],
                                            op=ALU.mult)
                    return m
                m_q2_all = st("vector", _q2, after=[m_s2b, m_s2ib], sem=s_v)

            m_x4r = [None] * 4
            for i in range(4):
                st("sync", lambda sy, i=i, tg0=tg0: sy.dma_start(
                    out=t_x4[i][:],
                    in_=xsc[tg0 + 128 * i:tg0 + 128 * (i + 1), :]),
                   after=[m_xdq_all], sem=s_x4[i], k=16)
                m_xl = (s_x4[i], s_x4[i].n)
                m_x4r[i] = st("vector", lambda v, i=i: v.tensor_tensor(
                    out=t_x4[i][:], in0=t_x4[i][:], in1=t_gbb4[:],
                    op=ALU.add), after=[m_xl, m_gbb4], sem=s_v)

            OD = dict(stgfree=[ST["g"]["stgfree"][0], ST["g"]["stgfree"][1]],
                      q3free=[ST["g"]["q3free"][0], ST["g"]["q3free"][1]],
                      ld={}, tn={}, wr={})

            def eload_o(n):
                sl = n % 2
                do, sub = divmod(n, 4)
                st("sync", lambda sy, do=do, sub=sub, sl=sl: sy.dma_start(
                    out=t_stg[sl][:], in_=owB[do, sub]),
                   after=[OD["stgfree"][sl]], sem=s_stg[sl], k=16)
                OD["ld"][n] = s_stg[sl].mark()

            def etern_o(n):
                sl = n % 2
                do, sub = divmod(n, 4)

                def _tq(v, sl=sl):
                    v.tensor_scalar(out=t_q3[sl][:], in0=t_stg[sl][:],
                                    scalar1=t_gamma[:, 6:7], scalar2=None,
                                    op0=ALU.is_lt)
                    v.drain()
                    return v.scalar_tensor_tensor(
                        out=t_q3[sl][:], in0=t_stg[sl][:],
                        scalar=t_gamma[:, 2:3], in1=t_q3[sl][:],
                        op0=ALU.is_gt, op1=ALU.subtract)
                m1 = st("vector", _tq,
                        after=[OD["ld"][n], OD["q3free"][sl]], sem=s_v)
                OD["stgfree"][sl] = m1
                OD["tn"][n] = m1
                st("sync", lambda sy, do=do, sub=sub, sl=sl: sy.dma_start(
                    out=tern_o[do, sub], in_=t_q3[sl][:]),
                   after=[m1], sem=s_tn[sl], k=16)
                OD["wr"][n] = s_tn[sl].mark()

            def eload_to(n):
                sl = n % 2
                do, sub = divmod(n, 4)
                st("sync", lambda sy, do=do, sub=sub, sl=sl: sy.dma_start(
                    out=t_q3[sl][:], in_=tern_o[do, sub]),
                   after=[OD["q3free"][sl]], sem=s_tn[sl], k=16)
                OD["tn"][n] = s_tn[sl].mark()

            m_evo_prev = [m_q2_all, m_q2_all]
            m_xadd = list(m_x4r)
            if tg == 0:
                eload_o(0), eload_o(1), etern_o(0)
            else:
                eload_to(0), eload_to(1)
            for n in range(64):
                do, sub = divmod(n, 4)
                sl = n % 2
                if tg == 0:
                    if n + 2 < 64:
                        eload_o(n + 2)
                    if n + 1 < 64:
                        etern_o(n + 1)
                pO = pb[4] if do % 2 == 0 else pb[5]

                def _omm(t, sub=sub, sl=sl, pO=pO):
                    m = None
                    for ci in range(16):
                        m = t.matmul(pO[:, :],
                                     t_q3[sl][:, 128 * ci:128 * (ci + 1)],
                                     t_gv[16 * sub + ci][:],
                                     start=(sub == 0 and ci == 0),
                                     stop=(sub == 3 and ci == 15))
                    return m
                aft_mm = [OD["tn"][n], m_q2_all]
                if sub == 0:
                    aft_mm.append(m_evo_prev[do % 2])
                m_om = st("tensor", _omm, after=aft_mm, sem=s_pe)
                OD["q3free"][sl] = (m_om if tg == 1 else
                                    st("gpsimd", lambda g: g.engine_nop(),
                                       after=[m_om, OD["wr"][n]], sem=s_g))
                if tg == 1 and n + 2 < 64:
                    eload_to(n + 2)
                if sub != 3:
                    continue
                m_oall = m_om

                m_e1 = st("vector", lambda v, do=do, pO=pO: v.tensor_scalar(
                    out=t_e1[:], in0=pO[:, :], scalar1=t_gbgc[:, do:do + 1],
                    scalar2=None, op0=ALU.mult),
                    after=[m_oall, m_gbgc], sem=s_v)
                m_evo_prev[do % 2] = m_e1

                m_tp = st("tensor", lambda t: [
                    t.transpose(pt32[:, 128 * blk:128 * (blk + 1)],
                                t_e1[:, 128 * blk:128 * (blk + 1)],
                                t_identf[:]) for blk in range(4)][-1],
                    after=[m_e1, (m_s2ib if do == 0 else None),
                           m_xadd[3]], sem=s_pe)

                def _xa(v, do=do):
                    m = None
                    for i in range(4):
                        m = v.tensor_tensor(
                            out=t_x4[i][:, 128 * do:128 * (do + 1)],
                            in0=t_x4[i][:, 128 * do:128 * (do + 1)],
                            in1=pt32[:, 128 * i:128 * (i + 1)], op=ALU.add)
                    return m
                m_xa = st("vector", _xa, after=[m_tp] + m_xadd, sem=s_v)
                m_xadd = [m_xa] * 4

            m_sts = []
            for i in range(4):
                st("sync", lambda sy, i=i, tg0=tg0: sy.dma_start(
                    out=xo[tg0 + 128 * i:tg0 + 128 * (i + 1), :],
                    in_=t_x4[i][:]), after=[m_xadd[0]], sem=s_sto, k=16)
            m_st4 = (s_sto, s_sto.n)
            m_tg_done = st("gpsimd", lambda g: g.engine_nop(),
                           after=[m_st4, m_xadd[0]], sem=s_g)
            es4b.close()

            if debug and tg == 0:
                d_h2 = b.ddbg("d_h2", [16 * 128, 512])
        es4.close()

        def _fin(sy):
            sy.wait_ge(s_sto.h, s_sto.n)
            return None
        st("sync", _fin, after=[m_tg_done])
        m_tail = m_tg_done

    M = dict(const=m_const, nwsel=m_nwsel, silc=m_silc, db=m_db, modc=m_modc,
             gamt=m_gamt, mrow=m_mrow)

    if upto_i < len(PHASES) - 1 or not enabled("ph1"):
        m_z = st("gpsimd", lambda g: (g.memset(t_junk[:], 0.0), g.drain(),
                                      g.engine_nop())[-1],
                 after=[m_tail], sem=s_g)

        def _zs(sy):
            m = None
            for i in range(8):
                m = sy.dma_start(out=xo[128 * i:128 * (i + 1), :],
                                 in_=t_junk[:])
                s_st.inc(m, 16)
            sy.wait_ge(s_st.h, s_st.n)
            return None
        st("sync", _zs, after=[m_z])

    es.close()
    return nc, b.dbg


def _host_consts():
    ii = np.arange(256)[None, :]
    jj = np.arange(128)[:, None]
    m0 = np.where(ii >= jj, 0.0, -1.0e30).astype(np.float32)
    m1 = np.where(ii >= jj + 128, 0.0, -1.0e30).astype(np.float32)
    masklog = np.concatenate([m0[:, 0:256], m1[:, 0:256]], axis=1)
    onesel = np.zeros((16, 2048), np.float32)
    for k in range(16):
        onesel[k, 128 * k:128 * (k + 1)] = 1.0
    oco = np.zeros((128, 256), np.float32)
    for h in range(16):
        oco[:, 16 * h + h] = 1.0
    import ml_dtypes
    return {
        "identb": np.eye(128, dtype=ml_dtypes.bfloat16),
        "identf": np.eye(128, dtype=np.float32),
        "masklog": masklog,
        "onesel": onesel,
        "oco": oco,
        "onesrow_b": np.ones((1, 128), ml_dtypes.bfloat16),
        "onesrow_f": np.ones((1, 128), np.float32),
        "onescol_f": np.ones((128, 1), np.float32),
    }


def _shard_inputs(core, I):
    bq, q = divmod(core, 4)
    x = np.asarray(I["L0_attn_hidden_states"])
    start = 1024 * q
    own = x[bq, start:start + T]
    halo = x[bq, start - 3:start] if q > 0 else np.zeros((3, HID), np.float32)
    xe = np.concatenate([halo, own], axis=0)
    pre = np.zeros((NPRE, HID), np.float32)
    if q > 0:
        pre[NPRE - start:] = x[bq, 0:start]
    prem = np.zeros((1, 12), np.float32)
    prem[0, 12 - 4 * q:] = 1.0
    halom = np.full((128, 1), 1.0 if q > 0 else 0.0, np.float32)

    c = np.asarray(I["L0_c"])
    c_pk = np.ascontiguousarray(c[bq].reshape(16, 128).T)

    convw = np.ascontiguousarray(
        np.asarray(I["L0_attn_conv_weight"]).reshape(24, 128, 4)
        .transpose(1, 0, 2).reshape(128, 96))
    convb = np.ascontiguousarray(
        np.asarray(I["L0_attn_conv_bias"]).reshape(24, 128).T)

    gw = np.asarray(I["L0_ffn_gate_w"])
    vw = np.asarray(I["L0_ffn_val_w"])
    ow = np.asarray(I["L0_ffn_out_w"])
    gwB = np.ascontiguousarray(
        gw.T.reshape(16, 128, 64, 128).transpose(2, 1, 0, 3)
        .reshape(64, 128, 2048))
    vwB = np.ascontiguousarray(
        vw.T.reshape(16, 128, 64, 128).transpose(2, 1, 0, 3)
        .reshape(64, 128, 2048))
    owB = np.ascontiguousarray(
        ow.T.reshape(4, 16, 128, 16, 128).transpose(3, 0, 2, 1, 4)
        .reshape(16, 4, 128, 2048))

    d = {
        "xe": xe, "xp": pre, "premrow": prem, "halom": halom,
        "epscol": np.full((128, 1), 1e-5, np.float32),
        "premB": np.broadcast_to(prem, (128, 12)).copy(),
        "w_in": np.asarray(I["L0_attn_W_in"]),
        "wmod": np.asarray(I["L0_wmod"]),
        "c_pk": c_pk, "convw": convw, "convb": convb,
        "acol": np.asarray(I["L0_attn_A_log"]).reshape(16, 1),
        "dtbcol": np.asarray(I["L0_attn_dt_bias"]).reshape(16, 1),
        "drow": np.asarray(I["L0_attn_D"]).reshape(1, 16),
        "nwrow": np.asarray(I["L0_attn_norm_weight"]).reshape(1, HID),
        "w_out": np.asarray(I["L0_attn_W_out"]),
        "gwB": gwB, "vwB": vwB, "owB": owB,
        "gb2": np.ascontiguousarray(
            np.asarray(I["L0_ffn_gate_b"]).reshape(64, 128).T),
        "vb2": np.ascontiguousarray(
            np.asarray(I["L0_ffn_val_b"]).reshape(64, 128).T),
        "obrow": np.asarray(I["L0_ffn_out_b"]).reshape(1, HID),
    }
    d.update(_host_consts())
    for k in d:
        if d[k].dtype == np.float64:
            raise ValueError(f"{k} is float64")
    return d


def kernel(**inputs):
    from concourse.bass_utils import run_bass_kernel_spmd
    nc, _ = build()
    in_maps = [_shard_inputs(c, inputs) for c in range(8)]
    res = run_bass_kernel_spmd(nc, in_maps, core_ids=list(range(8)))
    out = np.zeros((2, 4096, HID), np.float32)
    for c in range(8):
        bq, q = divmod(c, 4)
        out[bq, 1024 * q:1024 * (q + 1)] = res.results[c]["xo"]
    return out


if __name__ == "__main__":
    nc, dbg = build(debug=False, upto="ph0")
    print("built", sum(1 for _ in nc.all_instructions()), "instructions")
